# version 02
# baseline (speedup 1.0000x reference)
import numpy as np
import ml_dtypes
from contextlib import ExitStack

import concourse.bass as bass
import concourse.mybir as mybir
from concourse.bass_utils import run_bass_kernel_spmd

F32 = mybir.dt.float32
BF16 = mybir.dt.bfloat16
AF = mybir.ActivationFunctionType
ALU = mybir.AluOpType
AX = mybir.AxisListType
BF = ml_dtypes.bfloat16

NCORES = 8
D = 1024
DC = D // 128
T = 2048
NT = T // 128
S = 8192
TQC = 512
E = 8
LN_EPS = 1e-5


def _bf(x):
    return np.asarray(x, dtype=BF)

_CONSTS = {}

def host_consts():
    if _CONSTS:
        return _CONSTS
    c = _CONSTS
    c["ident"] = _bf(np.eye(128))
    c["ones_row128"] = _bf(np.ones((1, 128)))
    c["ones_row512"] = _bf(np.ones((1, 512)))
    c["ones_col"] = _bf(np.ones((128, 1)))
    c["ones_row128_f32"] = np.ones((1, 128), dtype=np.float32)
    c["ones_col_f32"] = np.ones((128, 1), dtype=np.float32)
    iota = np.arange(8, dtype=np.float32)[None, :]
    c["iota8"] = np.tile(iota, (128, 1))
    c["iota8p"] = np.tile(iota + 1e4, (128, 1))
    c["ltri_s"] = _bf(np.triu(np.ones((128, 128), np.float32), 1))
    tk = np.arange(128)[:, None]
    tq = np.arange(TQC)[None, :]
    c["masks"] = _bf(np.stack([((tq - tk) >= 128 * r).astype(np.float32) for r in range(4)]))
    c["masks3"] = _bf(np.stack([((tq - tk) >= 128 * r - 1).astype(np.float32) for r in range(4)]))
    m3last = np.zeros((128, TQC), np.float32)
    m3last[0, TQC - 1] = 1.0
    c["mask3_last"] = _bf(m3last)
    d_h = 64
    inv_freq = 1.0 / (10000.0 ** (np.arange(0, d_h, 2, dtype=np.float32) / d_h))
    pos = np.arange(S, dtype=np.float32)
    ang = pos[:, None] * inv_freq[None, :]
    cos = np.concatenate([np.cos(ang), np.cos(ang)], axis=1).T
    sin = np.concatenate([np.sin(ang), np.sin(ang)], axis=1).T
    c["rope_cos"] = _bf(np.ascontiguousarray(cos))
    c["rope_sin"] = _bf(np.ascontiguousarray(sin))
    c["rope_cos2"] = _bf(np.ascontiguousarray(np.concatenate([cos, cos], 0)))
    c["rope_sin2"] = _bf(np.ascontiguousarray(np.concatenate([sin, sin], 0)))
    R = np.zeros((64, 64), np.float32)
    for i in range(32):
        R[i, 32 + i] = -1.0
        R[32 + i, i] = 1.0
    c["ropeRT"] = _bf(np.ascontiguousarray(R.T))
    RT2 = np.zeros((128, 128), np.float32)
    RT2[0:64, 0:64] = R.T
    RT2[64:128, 64:128] = R.T
    c["ropeRT2"] = _bf(RT2)
    sel = np.zeros((2, 128), np.float32)
    sel[0, 0:64] = 1.0
    sel[1, 64:128] = 1.0
    c["selT128"] = _bf(sel)
    c["sel128x2"] = _bf(np.ascontiguousarray(sel.T))
    pk = np.zeros((2, 2, 128), np.float32)
    pk[0, 0, :] = 1.0
    pk[1, 1, :] = 1.0
    c["pickA"] = _bf(pk[0])
    c["pickB"] = _bf(pk[1])
    return c


ENGINES = ("sync", "gpsimd", "tensor", "vector", "scalar")


class _Step:
    __slots__ = ("eng", "fn", "deps", "k", "cum")

    def __init__(self, eng, fn, deps, k):
        self.eng = eng
        self.fn = fn
        self.deps = deps
        self.k = k
        self.cum = None


class Prog:

    def __init__(self, name):
        self.nc = bass.Bass("TRN2", target_bir_lowering=False,
                            detect_race_conditions=False)
        self.name = name
        self.es = ExitStack()
        self.ap = {}
        self.steps = {e: [] for e in ENGINES}
        self._built = False

    def inp(self, name, shape, dt):
        self.ap[name] = self.nc.dram_tensor(name, list(shape), dt, kind="ExternalInput").ap()
        return self.ap[name]

    def outp(self, name, shape, dt):
        self.ap[name] = self.nc.dram_tensor(name, list(shape), dt, kind="ExternalOutput").ap()
        return self.ap[name]

    def dram(self, name, shape, dt, **kw):
        self.ap[name] = self.nc.dram_tensor(name, list(shape), dt, kind="Internal", **kw).ap()
        return self.ap[name]

    def sb(self, name, shape, dt):
        return self.es.enter_context(self.nc.sbuf_tensor(name, list(shape), dt))

    def ps(self, name, shape, dt=F32):
        return self.es.enter_context(self.nc.psum_tensor(name, list(shape), dt))

    def on(self, eng, fn, deps=(), dma=False):
        st = _Step(eng, fn, [d for d in deps if d is not None], 16 if dma else 1)
        self.steps[eng].append(st)
        return st

    def chain(self, eng, fns, deps=()):
        prev = None
        for fn in fns:
            prev = self.on(eng, fn, deps=(list(deps) + [prev]) if prev else deps)
        return prev

    def dma(self, eng, out, in_, deps=()):
        return self.on(eng, (lambda o, i: (lambda e: e.dma_start(out=o, in_=i)))(out, in_),
                       deps=deps, dma=True)

    def build(self):
        assert not self._built
        self._built = True
        sems = {}
        for e in ENGINES:
            kinds = {st.k for st in self.steps[e]}
            for k in kinds:
                tag = "d" if k == 16 else "c"
                sems[(e, tag)] = self.es.enter_context(self.nc.semaphore(f"sem_{e}_{tag}"))
        cum = {}
        for e in ENGINES:
            for st in self.steps[e]:
                tag = "d" if st.k == 16 else "c"
                cum[(e, tag)] = cum.get((e, tag), 0) + st.k
                st.cum = cum[(e, tag)]
        blk = self.es.enter_context(self.nc.Block())

        def make_body(e):
            def body(eng):
                prev_dma = 0
                for st in self.steps[e]:
                    tag = "d" if st.k == 16 else "c"
                    for d in st.deps:
                        dtag = "d" if d.k == 16 else "c"
                        if d.eng == e and dtag == tag and d.k == 16:
                            continue
                        eng.wait_ge(sems[(d.eng, dtag)], d.cum)
                    if st.k == 16:
                        if prev_dma:
                            eng.wait_ge(sems[(e, "d")], prev_dma)
                        prev_dma = st.cum
                    inst = st.fn(eng)
                    if inst is None:
                        inst = eng.nop()
                    inst.then_inc(sems[(e, "d" if st.k == 16 else "c")], st.k)
            return body

        for e in ENGINES:
            if not self.steps[e]:
                continue
            getattr(blk, e)(make_body(e))
        self.es.close()
        return self.nc

    def setup_group_end(self, width=512):
        self._ones_row = self.sb("_gones", (1, 128), BF16)
        self._zrow = self.sb("_gzero", (1, width), BF16)
        l1 = self.dma("sync", self._ones_row[:], self.inp("_c_ones128", (1, 128), BF16)[:])
        z = self.on("vector", lambda e: e.memset(self._zrow[:], 0.0))
        self._ge_dep = [l1, z]
        return self._ge_dep

    def end_group(self, e, ps_ap, m, n):
        return e.matmul(ps_ap, self._ones_row[0:1, 0:m], self._zrow[0:1, 0:n],
                        start=False, stop=True)

    def finalize_wait(self, waiter="gpsimd"):
        deps = [self.steps[e][-1] for e in ENGINES if self.steps[e] and e != waiter]
        self.on(waiter, lambda e: None, deps=deps)


def const_inputs(p, names):
    c = host_consts()
    out = {}
    for n in names:
        arr = c[n]
        dt = BF16 if arr.dtype == BF else F32
        out[n] = p.inp("c_" + n, arr.shape, dt)
    return out


def const_in_map(names):
    c = host_consts()
    m = {"c_" + n: np.asarray(c[n]) for n in names}
    m["_c_ones128"] = np.asarray(c["ones_row128"])
    return m


def fm3(dst_sb, src_ap, w, c=DC):
    return dst_sb[:].rearrange("p (c e) -> p c e", c=c), src_ap.rearrange("(c p) e -> p c e", p=128)


def build_stats(layer, phases=3, ntb=NT):
    p = Prog(f"S{layer}")
    xs = p.inp("xs", (T, D), F32)
    cc = const_inputs(p, ["ident", "iota8", "iota8p", "ones_col", "ones_row128"])
    xnf_o = p.outp("xnf", (DC, 128, T), BF16)
    gate = layer in (0, 1, 3) and phases >= 2
    if gate:
        wr_i = p.inp("wr", (D, E), F32)
    if layer in (0, 1, 3) and gate:
        bias_i = p.inp("gbias", (1, E), F32)
    if phases < 2:
        pass
    elif layer == 0:
        oh_o = p.outp("oh", (T, 16), F32)
        tv_o = p.outp("tv", (T, 2), F32)
        tot_o = p.outp("tot", (1, 16), F32)
    elif layer == 1:
        rw_o = p.outp("rw", (T, E), F32)
    elif layer == 2:
        wq_i = p.inp("wq", (D, 512), F32)
        wk_i = p.inp("wk", (D, 512), F32)
        bq_i = p.inp("bq", (1, 512), F32)
        bk_i = p.inp("bk", (1, 512), F32)
        scp_o = p.outp("scp", (128, 256), F32)
    elif layer == 3:
        gi_o = p.outp("gi", (T, 4), F32)
    if phases < 2 and layer == 2:
        pass

    xt = [p.sb(f"xt{i}", (128, D), F32) for i in range(2)]
    xsq = [p.sb(f"xsq{i}", (128, D), F32) for i in range(2)]
    xnb = [p.sb(f"xnb{i}", (128, D), BF16) for i in range(2)]
    xnf = p.sb("xnf_s", (128, DC * T), BF16)
    id_s = p.sb("id_s", (128, 128), BF16)
    onesrb = p.sb("onesrb", (1, 128), BF16)
    colA = p.sb("colA", (128, 1), F32)
    colB = p.sb("colB", (128, 1), F32)
    colC = p.sb("colC", (128, 1), F32)
    colD = p.sb("colD", (128, 1), F32)
    colF = p.sb("colF", (128, 1), F32)
    colE = p.sb("colE", (128, 1), F32)
    ps_tr = [p.ps(f"ps_tr{i}", (128, 128), BF16) for i in range(2)]

    onescb = p.sb("onescb", (128, 1), BF16)
    ld_id = p.dma("sync", id_s[:], cc["ident"][:])
    ld_oc = p.dma("sync", onescb[:], cc["ones_col"][:])
    ld_or = p.dma("sync", onesrb[:], cc["ones_row128"][:])
    if gate:
        wr_s = p.sb("wr_s", (128, DC * E), BF16)
        iota_s = p.sb("iota_s", (128, 8), F32)
        iotap_s = p.sb("iotap_s", (128, 8), F32)
        ld_wr = p.dma("gpsimd", *fm3(wr_s, wr_i, E))
        ld_i1 = p.dma("sync", iota_s[:], cc["iota8"][:])
        ld_i2 = p.dma("sync", iotap_s[:], cc["iota8p"][:])
        gbias_s = p.sb("gbias_s", (1, E), BF16)
        ld_gb = p.dma("gpsimd", gbias_s[:], bias_i[:])
    if layer == 2:
        wq_s = p.sb("wq_s", (128, DC * 512), BF16)
        wk_s = p.sb("wk_s", (128, DC * 512), BF16)
        bq_s = p.sb("bq_s", (1, 512), BF16)
        bk_s = p.sb("bk_s", (1, 512), BF16)
        ld_wq = p.dma("gpsimd", *fm3(wq_s, wq_i, 512))
        ld_wk = p.dma("gpsimd", *fm3(wk_s, wk_i, 512))
        ld_bq = p.dma("gpsimd", bq_s[:], bq_i[:])
        ld_bk = p.dma("gpsimd", bk_s[:], bk_i[:])

    ev = {}
    last_norm = {}
    last_sq = {}
    for t in range(NT):
        b = t % 2
        dep_ld = []
        if t >= 2:
            dep_ld = [last_norm[t - 2], last_sq[t - 2]]
        s_load = p.dma("sync", xt[b][:], xs[t * 128:(t + 1) * 128, :], deps=dep_ld)
        s_sq = p.on("scalar", (lambda b: lambda e: e.activation(xsq[b][:], xt[b][:], AF.Square))(b),
                    deps=[s_load] + ([last_sq[t - 2]] if t >= 2 else []))
        last_sq[t] = s_sq

        v_stats = p.chain("vector", [
            (lambda b: lambda e: e.reduce_sum(colA[:], xt[b][:], axis=AX.X))(b),
            (lambda b: lambda e: e.reduce_sum(colB[:], xsq[b][:], axis=AX.X))(b),
            lambda e: e.tensor_scalar_mul(colC[:], colA[:], 1.0 / D),
            lambda e: e.tensor_scalar_mul(colD[:], colB[:], 1.0 / D),
            lambda e: e.tensor_mul(colB[:], colC[:], colC[:]),
            lambda e: e.tensor_sub(colD[:], colD[:], colB[:]),
            lambda e: e.tensor_scalar_add(colD[:], colD[:], LN_EPS),
        ], deps=[s_load, s_sq])
        s_sqrt = p.on("scalar", lambda e: e.activation(colF[:], colD[:], AF.Sqrt),
                      deps=[v_stats])
        v_norm = p.chain("vector", [
            lambda e: e.reciprocal(colE[:], colF[:]),
            (lambda b: lambda e: e.tensor_scalar(xnb[b][:], xt[b][:], colC[:], colE[:],
                                                 ALU.subtract, ALU.mult))(b),
        ], deps=[s_sqrt])
        last_norm[t] = v_norm
        for dc in range(DC):
            j = (t * DC + dc) % 2
            deps = [v_norm, ld_id]
            prev = ev.get((t * DC + dc) - 2)
            if prev is not None:
                deps.append(prev)
            tr = p.on("tensor",
                      (lambda b, dc, j: lambda e: e.transpose(
                          ps_tr[j][:], xnb[b][:, dc * 128:(dc + 1) * 128], id_s[:]))(b, dc, j),
                      deps=deps)
            ev[t * DC + dc] = p.on("scalar",
                                   (lambda t, dc, j: lambda e: e.copy(
                                       xnf[:, dc * T + t * 128: dc * T + (t + 1) * 128],
                                       ps_tr[j][:]))(t, dc, j),
                                   deps=[tr])

    st_xnf = []
    for dc in range(DC):
        st_xnf.append(p.dma("sync", xnf_o[dc, :, :], xnf[:, dc * T:(dc + 1) * T],
                            deps=[ev[(NT - 1) * DC + DC - 1]]))

    if gate and phases >= 2:
        ps_lg = p.ps("ps_lg", (128, 8), F32)
        probs = p.sb("probs", (128, 8), F32)
        base2 = p.sb("base2", (128, 8), F32)
        ohm = p.sb("ohm", (128, 8), F32)
        cand = p.sb("cand", (128, 8), F32)
        oh1 = p.sb("oh1", (128, 8), F32)
        oh2 = p.sb("oh2", (128, 8), F32)
        v1c = p.sb("v1c", (128, 1), F32)
        v2c = p.sb("v2c", (128, 1), F32)
        i1c = p.sb("i1c", (128, 1), F32)
        i2c = p.sb("i2c", (128, 1), F32)
        nmx = p.sb("nmx", (128, 1), F32)
        sm = p.sb("sm", (128, 1), F32)
        pc1 = p.sb("pc1", (128, 1), F32)
        pc2 = p.sb("pc2", (128, 1), F32)
        exw = 16 if layer == 0 else (8 if layer == 1 else 4)
        exo = p.sb("exo", (128, exw), F32)
        exo_bf = p.sb("exo_bf", (128, 16), BF16) if layer == 0 else None
        tvt = p.sb("tvt", (128, 2), F32) if layer == 0 else None
        if layer == 0:
            ps_tot = p.ps("ps_tot", (1, 16), F32)
            tot_sb = p.sb("tot_sb", (1, 16), F32)
            zz = p.on("vector", lambda e: e.memset(tot_sb[:], 0.0))
        prev_exp = None
        prev_fin = None
        prev_tot = None
        prev_sts = []
        for t in range(ntb):
            def gmm_fn(t):
                def f(e):
                    for dc in range(DC):
                        e.matmul(ps_lg[:], xnf[:, dc * T + t * 128: dc * T + (t + 1) * 128],
                                 wr_s[:, dc * E:(dc + 1) * E],
                                 start=(dc == 0), stop=False)
                    return e.matmul(ps_lg[:], onesrb[:], gbias_s[:], start=False, stop=True)
                return f
            gdeps = [ev[t * DC + DC - 1], ld_wr, ld_or, ld_gb]
            if prev_exp is not None:
                gdeps.append(prev_exp)
            g_mm = p.on("tensor", gmm_fn(t), deps=gdeps)
            if layer in (0, 3):
                v_mx = p.chain("vector", [
                    lambda e: e.reduce_max(v1c[:], ps_lg[:], axis=AX.X),
                    lambda e: e.tensor_scalar_mul(nmx[:], v1c[:], -1.0),
                ], deps=[g_mm] + ([prev_fin] if prev_fin else []))
                s_exp = p.on("scalar", lambda e: e.activation(probs[:], ps_lg[:], AF.Exp,
                                                              bias=nmx[:], accum_out=sm[:]),
                             deps=[v_mx])
                prev_exp = s_exp
                basedep = s_exp
            else:
                v_cpy = p.on("vector", lambda e: e.tensor_copy(probs[:], ps_lg[:]),
                             deps=[g_mm] + ([prev_fin] if prev_fin else []))
                prev_exp = v_cpy
                basedep = v_cpy

            fns = [
                lambda e: e.reduce_max(v1c[:], probs[:], axis=AX.X),
                lambda e: e.tensor_single_scalar(ohm[:], probs[:], v1c[:], ALU.is_equal),
                lambda e: e.scalar_tensor_tensor(cand[:], ohm[:], -1e4, iotap_s[:], ALU.mult, ALU.add),
                lambda e: e.tensor_reduce(i1c[:], cand[:], AX.X, ALU.min),
                lambda e: e.tensor_single_scalar(oh1[:], iota_s[:], i1c[:], ALU.is_equal),
                lambda e: e.scalar_tensor_tensor(base2[:], oh1[:], -1e9, probs[:], ALU.mult, ALU.add),
                lambda e: e.reduce_max(v2c[:], base2[:], axis=AX.X),
                lambda e: e.tensor_single_scalar(ohm[:], base2[:], v2c[:], ALU.is_equal),
                lambda e: e.scalar_tensor_tensor(cand[:], ohm[:], -1e4, iotap_s[:], ALU.mult, ALU.add),
                lambda e: e.tensor_reduce(i2c[:], cand[:], AX.X, ALU.min),
                lambda e: e.tensor_single_scalar(oh2[:], iota_s[:], i2c[:], ALU.is_equal),
            ]
            if layer == 1:
                fns.append(lambda e: e.tensor_sub(pc1[:], v1c[:], v2c[:]))
                fns.append(lambda e: e.tensor_sub(pc2[:], v2c[:], v1c[:]))
            v_top = p.chain("vector", fns, deps=[basedep])

            if layer == 0:
                deps = [v_top] + prev_sts
                if prev_tot is not None:
                    deps.append(prev_tot)
                fin_fns = [
                    lambda e: e.tensor_add(colA[:], v1c[:], v2c[:]),
                    lambda e: e.reciprocal(colB[:], colA[:]),
                    lambda e: e.tensor_mul(tvt[:, 0:1], v1c[:], colB[:]),
                    lambda e: e.tensor_mul(tvt[:, 1:2], v2c[:], colB[:]),
                    lambda e: e.tensor_copy(exo[:, 0:8], oh1[:]),
                    lambda e: e.tensor_copy(exo[:, 8:16], oh2[:]),
                    lambda e: e.tensor_copy(exo_bf[:, 0:8], oh1[:]),
                    lambda e: e.tensor_copy(exo_bf[:, 8:16], oh2[:]),
                ]
                if phases == 2:
                    fin_fns = fin_fns[4:6]
                v_fin = p.chain("vector", fin_fns, deps=deps)
                if phases >= 3:
                    t_mm = p.on("tensor",
                                (lambda t: lambda e: e.matmul(ps_tot[:], onescb[:], exo_bf[:],
                                                              start=True, stop=True))(t),
                                deps=[v_fin, ld_oc, zz] + ([prev_tot] if prev_tot else []))
                    t_tot = p.on("vector", lambda e: e.tensor_add(tot_sb[:], ps_tot[:], tot_sb[:]),
                                 deps=[t_mm])
                    prev_tot = t_tot
                else:
                    t_tot = v_fin
                st1 = p.dma("sync", oh_o[t * 128:(t + 1) * 128, :], exo[:], deps=[v_fin])
                prev_sts = [st1]
                if phases >= 3:
                    st2 = p.dma("sync", tv_o[t * 128:(t + 1) * 128, :], tvt[:], deps=[v_fin])
                    prev_sts = [st1, st2]
                prev_fin = t_tot
            elif layer == 1:
                s_sig = p.on("scalar", lambda e: (e.activation(pc1[:], pc1[:], AF.Sigmoid),
                                                  e.activation(pc2[:], pc2[:], AF.Sigmoid))[-1],
                             deps=[v_top])
                v_fin = p.chain("vector", [
                    lambda e: e.tensor_scalar_mul(cand[:], oh1[:], pc1[:]),
                    lambda e: e.scalar_tensor_tensor(exo[:], oh2[:], pc2[:], cand[:],
                                                     ALU.mult, ALU.add),
                ], deps=[s_sig] + prev_sts)
                st1 = p.dma("sync", rw_o[t * 128:(t + 1) * 128, :], exo[:], deps=[v_fin])
                prev_sts = [st1]
                prev_fin = v_fin
            else:
                v_fin = p.chain("vector", [
                    lambda e: e.reciprocal(colB[:], sm[:]),
                    lambda e: e.tensor_mul(exo[:, 0:1], v1c[:], colB[:]),
                    lambda e: e.tensor_mul(exo[:, 1:2], v2c[:], colB[:]),
                    lambda e: e.tensor_copy(exo[:, 2:3], i1c[:]),
                    lambda e: e.tensor_copy(exo[:, 3:4], i2c[:]),
                ], deps=[v_top] + prev_sts)
                st1 = p.dma("sync", gi_o[t * 128:(t + 1) * 128, :], exo[:], deps=[v_fin])
                prev_sts = [st1]
                prev_fin = v_fin
        if layer == 0 and phases >= 3:
            p.dma("sync", tot_o[:], tot_sb[:], deps=[prev_tot])

    if layer == 2 and phases >= 2:
        q_all = p.sb("q_all", (128, NT * 512), BF16)
        k_all = p.sb("k_all", (128, NT * 512), BF16)
        ps_q = p.ps("ps_q", (128, 512), F32)
        ps_k = p.ps("ps_k", (128, 512), F32)
        ps_sc = [p.ps(f"ps_sc{i}", (128, 256), F32) for i in range(2)]
        sc_sb = p.sb("sc_sb", (128, 256), F32)
        prev_qe = None
        prev_ke = None
        qk_done = []
        for t in range(NT):
            def qk_fn(t, w_s, b_s, pstile):
                def f(e):
                    for dc in range(DC):
                        e.matmul(pstile[:], xnf[:, dc * T + t * 128: dc * T + (t + 1) * 128],
                                 w_s[:, dc * 512:(dc + 1) * 512], start=(dc == 0), stop=False)
                    return e.matmul(pstile[:], onesrb[:], b_s[:], start=False, stop=True)
                return f
            q_mm = p.on("tensor", qk_fn(t, wq_s, bq_s, ps_q),
                        deps=[ev[t * DC + DC - 1], ld_wq, ld_bq, ld_or] + ([prev_qe] if prev_qe else []))
            s_qe = p.on("scalar", (lambda t: lambda e: e.copy(
                q_all[:, t * 512:(t + 1) * 512], ps_q[:]))(t), deps=[q_mm])
            k_mm = p.on("tensor", qk_fn(t, wk_s, bk_s, ps_k),
                        deps=[ld_wk, ld_bk] + ([prev_ke] if prev_ke else []))
            s_ke = p.on("scalar", (lambda t: lambda e: e.copy(
                k_all[:, t * 512:(t + 1) * 512], ps_k[:]))(t), deps=[k_mm])
            prev_qe, prev_ke = s_qe, s_ke
            qk_done.append(s_qe)
            qk_done.append(s_ke)
        prev_ev = {}
        for h in range(8):
            bank = h % 2
            base = 64 * (h % 2)
            def sch_fn(h, bank, base):
                def f(e):
                    inst = None
                    for t in range(NT):
                        inst = e.matmul(ps_sc[bank][base:base + 64, 0:64],
                                        q_all[:, t * 512 + h * 64: t * 512 + (h + 1) * 64],
                                        k_all[:, t * 512 + h * 64: t * 512 + (h + 1) * 64],
                                        start=(t == 0), stop=(t == NT - 1),
                                        tile_position=(0, base))
                    return inst
                return f
            deps = [qk_done[-1], qk_done[-2]]
            if h - 2 in prev_ev:
                deps.append(prev_ev[h - 2])
            sc_mm = p.on("tensor", sch_fn(h, bank, base), deps=deps)
            prev_ev[h] = p.on("vector",
                              (lambda h, bank, base: lambda e: e.tensor_copy(
                                  sc_sb[base:base + 64, (h // 2) * 64:(h // 2) * 64 + 64],
                                  ps_sc[bank][base:base + 64, 0:64]))(h, bank, base),
                              deps=[sc_mm])
        p.dma("sync", scp_o[:], sc_sb[:], deps=[prev_ev[6], prev_ev[7]])

    p.finalize_wait()
    return p


def _load_xnf(p):
    xnf_i = p.inp("xnf", (DC, 128, T), BF16)
    xnf = p.sb("xnf_s", (128, DC * T), BF16)
    ld = p.dma("sync", xnf[:].rearrange("p (c t) -> p c t", c=DC),
               xnf_i.rearrange("c p t -> p c t"))
    return xnf, ld


def _emit_out_finish(p, out_sb, g_o, dep_map, act=True):
    for t in range(NT):
        sl = out_sb[:, t * D:(t + 1) * D]
        s_act = p.on("scalar", (lambda sl: lambda e: e.activation(sl, sl, AF.Silu))(sl),
                     deps=[dep_map[t]])
        p.dma("sync", p.ap["g"][t * 128:(t + 1) * 128, :], sl, deps=[s_act])


def _emit_dense_moe(p, xnf, ld_xnf, w_sb, w_deps, nf, act_fn, has_b2, n_e=E, split_ld=False, stage='full'):
    FC = 512
    NFC = nf // FC
    FP = FC // 128
    w1_i = p.ap["w1"]
    b1_i = p.ap["b1"]
    w2_i = p.ap["w2"]
    b2_i = p.ap["b2"] if has_b2 else None
    out_sb = p.sb("out_sb", (128, NT * D), F32)
    h_sb = p.sb("h_sb", (128, FP * T), BF16)
    w1c = [p.sb(f"w1c{i}", (128, DC * FC), BF16) for i in range(2)]
    w2c = [p.sb(f"w2c{i}", (128, FP * D), BF16) for i in range(2)]
    b1c = [p.sb(f"b1c{i}", (1, FC), BF16) for i in range(2)]
    b2c = p.sb("b2c", (1, D), BF16) if has_b2 else None
    onesr5 = p.sb("onesr5", (1, 512), BF16)
    tmp_bf = p.sb("tmp_bf", (128, 512), BF16)
    ps_h = [p.ps(f"ps_h{i}", (128, 512), F32) for i in range(2)]
    ps_o = [p.ps(f"ps_o{i}", (128, 512), F32) for i in range(2)]
    cc = p.ap["c_ones_row512"]
    ld_or5 = p.dma("sync", onesr5[:], cc[:])
    z = p.on("vector", lambda e: e.memset(out_sb[:], 0.0))
    ld_b2 = None

    lastuse_w = [None, None]
    lastuse_b2 = None
    last_hps_evac = [None, None]
    last_ops_evac = [None, None]
    out_dep = {t: z for t in range(NT)}
    prev_loads = {}
    chunk_idx = 0
    for e in range(n_e):
        if has_b2:
            deps = [lastuse_b2] if lastuse_b2 else []
            ld_b2 = p.dma("gpsimd", b2c[:], b2_i[e:e + 1, :], deps=deps)
        for fc in range(NFC):
            bi = chunk_idx % 2
            f0 = fc * FC
            dl = [lastuse_w[bi]] if lastuse_w[bi] else []
            if split_ld:
                for dc in range(DC):
                    ld1 = p.dma("gpsimd", w1c[bi][:, dc * FC:(dc + 1) * FC],
                                w1_i[e, dc * 128:(dc + 1) * 128, f0:f0 + FC], deps=dl)
                for q in range(FP):
                    ld2 = p.dma("gpsimd", w2c[bi][:, q * D:(q + 1) * D],
                                w2_i[e, f0 + q * 128: f0 + (q + 1) * 128, :], deps=dl)
            else:
                ld1 = p.dma("gpsimd", w1c[bi][:].rearrange("p (c f) -> p c f", c=DC),
                            w1_i[e, :, f0:f0 + FC].rearrange("(c p) f -> p c f", p=128),
                            deps=dl)
                ld2 = p.dma("gpsimd", w2c[bi][:].rearrange("p (q d) -> p q d", q=FP),
                            w2_i[e, f0:f0 + FC, :].rearrange("(q p) d -> p q d", p=128),
                            deps=dl)
            ld3 = p.dma("gpsimd", b1c[bi][:], b1_i[e:e + 1, f0:f0 + FC], deps=dl)
            h_evac = []
            for fp in range(FP):
                for tc in range(4):
                    pb = (fp * 4 + tc) % 2
                    def hmm(bi, fp, tc, pb):
                        def f(e_):
                            for dc in range(DC):
                                e_.matmul(ps_h[pb][:],
                                          w1c[bi][:, dc * FC + fp * 128: dc * FC + (fp + 1) * 128],
                                          xnf[:, dc * T + tc * 512: dc * T + (tc + 1) * 512],
                                          start=(dc == 0), stop=False)
                            return e_.matmul(ps_h[pb][:], b1c[bi][0:1, fp * 128:(fp + 1) * 128],
                                             onesr5[:], start=False, stop=True)
                        return f
                    deps = [ld1, ld3, ld_xnf, ld_or5]
                    if last_hps_evac[pb] is not None:
                        deps.append(last_hps_evac[pb])
                    if fp == 0 and tc == 0 and prev_loads.get("w2_last_mm") is not None:
                        deps.append(prev_loads["w2_last_mm"])
                    mm = p.on("tensor", hmm(bi, fp, tc, pb), deps=deps)
                    ev = p.on("scalar",
                              (lambda fp, tc, pb: lambda e_: e_.activation(
                                  h_sb[:, fp * T + tc * 512: fp * T + (tc + 1) * 512],
                                  ps_h[pb][:], act_fn))(fp, tc, pb),
                              deps=[mm])
                    last_hps_evac[pb] = ev
                    h_evac.append(ev)
            if stage == 'h':
                prev_loads["w2_last_mm"] = h_evac[-1]
                lastuse_w[bi] = h_evac[-1]
                chunk_idx += 1
                continue
            addb2 = has_b2 and (fc == NFC - 1)
            import os as _os
            if _os.environ.get("F0B2ALL"):
                addb2 = has_b2
            for t in range(NT):
                for half in range(2):
                    pb = half
                    def omm(bi, t, half, pb, addb2):
                        def f(e_):
                            for fp in range(FP):
                                e_.matmul(ps_o[pb][:],
                                          h_sb[:, fp * T + t * 128: fp * T + (t + 1) * 128],
                                          w2c[bi][:, fp * D + half * 512: fp * D + half * 512 + 512],
                                          start=(fp == 0), stop=False)
                            if addb2:
                                return e_.matmul(ps_o[pb][:], p._onesr128[:],
                                                 b2c[0:1, half * 512: half * 512 + 512],
                                                 start=False, stop=True)
                            return p.end_group(e_, ps_o[pb][:], 128, 512)
                        return f
                    deps = [h_evac[-1], ld2] + ([ld_b2] if addb2 else []) + p._ge_dep
                    if last_ops_evac[pb] is not None:
                        deps.append(last_ops_evac[pb])
                    mm = p.on("tensor", omm(bi, t, half, pb, addb2), deps=deps)
                    lastuse_w[bi] = mm
                    if has_b2:
                        lastuse_b2 = mm
                    prev_loads["w2_last_mm"] = mm
                    wcol = w_sb[:, t * E + e: t * E + e + 1]
                    if stage == 'w2noevac':
                        sc = p.on("vector",
                                  (lambda pb: lambda e_: e_.tensor_copy(tmp_bf[:], ps_o[pb][:]))(pb),
                                  deps=[mm] + ([last_ops_evac[pb]] if last_ops_evac[pb] else []))
                    else:
                        sc = p.on("scalar",
                                  (lambda pb, wcol: lambda e_: e_.activation(
                                      tmp_bf[:], ps_o[pb][:], AF.Copy, scale=wcol))(pb, wcol),
                                  deps=[mm] + w_deps + ([last_ops_evac[pb]] if last_ops_evac[pb] else []))
                    ad = p.on("vector",
                              (lambda t, half: lambda e_: e_.tensor_add(
                                  out_sb[:, t * D + half * 512: t * D + half * 512 + 512],
                                  tmp_bf[:],
                                  out_sb[:, t * D + half * 512: t * D + half * 512 + 512]))(t, half),
                              deps=[sc, out_dep[t]])
                    last_ops_evac[pb] = ad
                    out_dep[t] = ad
            chunk_idx += 1
    return out_sb, out_dep


def build_ffn0(layer=0, n_e=E, skip_w=False, split_ld=False):
    p = Prog("F0")
    cc = const_inputs(p, ["ones_row512", "ones_row128", "ltri_s", "ones_col"])
    xnf, ld_xnf = _load_xnf(p)
    p.inp("w1", (E, D, 4096), F32)
    p.inp("b1", (E, 4096), F32)
    p.inp("w2", (E, 4096, D), F32)
    p.inp("b2", (E, D), F32)
    oh_i = p.inp("oh", (T, 16), F32)
    tv_i = p.inp("tv", (T, 2), F32)
    totst_i = p.inp("totst", (16, NCORES), F32)
    maskb_i = p.inp("maskb", (16, NCORES), F32)
    g_o = p.outp("g", (T, D), F32)
    onesr128 = p.sb("onesr128b", (1, 128), BF16)
    p._onesr128 = onesr128
    ld_o128 = p.dma("sync", onesr128[:], cc["ones_row128"][:])
    p.setup_group_end(512)

    CAP = 2560
    if skip_w:
        w_sb = p.sb("w_sb", (128, NT * E), F32)
        sk = True
    sk = skip_w
    if skip_w:
        pass
    w_sb = w_sb if skip_w else p.sb("w_sb", (128, NT * E), F32)
    ltri = p.sb("ltri", (128, 128), BF16)
    rank_dep = None
    if not skip_w:
        totst = p.sb("totst_s", (16, NCORES), F32)
        maskb = p.sb("maskb_s", (16, NCORES), F32)
        offc = p.sb("offc", (16, 1), F32)
        offrow = p.sb("offrow", (1, 16), F32)
        runrow = p.sb("runrow", (1, 16), F32)
        runlo = p.sb("runlo", (1, 16), BF16)
        runhi = p.sb("runhi", (1, 16), BF16)
        runtmp = p.sb("runtmp", (1, 16), F32)
        oh_t = p.sb("oh_t", (128, 16), F32)
        oh_bf = p.sb("oh_bf", (128, 16), BF16)
        tv_t = p.sb("tv_t", (128, 2), F32)
        keep = p.sb("keep", (128, 16), F32)
        m1 = p.sb("m1", (128, 8), F32)
        m2 = p.sb("m2", (128, 8), F32)
        off_dram = p.dram("off_scratch", (16, 1), F32)
        ps_rank = p.ps("ps_rank", (128, 16), F32)
        ps_t16 = p.ps("ps_t16", (1, 16), F32)
        ld_lt = p.dma("sync", ltri[:], cc["ltri_s"][:])
        ld_ts = p.dma("sync", totst[:], totst_i[:])
        ld_mb = p.dma("sync", maskb[:], maskb_i[:])
        v_off = p.chain("vector", [
            lambda e: e.tensor_mul(totst[:], totst[:], maskb[:]),
            lambda e: e.reduce_sum(offc[:], totst[:], axis=AX.X),
        ], deps=[ld_ts, ld_mb])
        d1 = p.dma("sync", off_dram[:, :], offc[:, 0:1], deps=[v_off])
        d2 = p.dma("sync", offrow[:], off_dram.rearrange("a b -> b a"), deps=[d1])
        v_run = p.chain("vector", [
            lambda e: e.tensor_copy(runrow[:], offrow[:]),
            lambda e: e.tensor_copy(runhi[:], runrow[:]),
            lambda e: e.tensor_sub(runtmp[:], runrow[:], runhi[:]),
            lambda e: e.tensor_copy(runlo[:], runtmp[:]),
        ], deps=[d2])
        rank_dep = None
        for t in range(NT):
            ld_oh = p.dma("sync", oh_t[:], oh_i[t * 128:(t + 1) * 128, :],
                          deps=[rank_dep] if rank_dep else [])
            ld_tv = p.dma("sync", tv_t[:], tv_i[t * 128:(t + 1) * 128, :],
                          deps=[rank_dep] if rank_dep else [])
            v_cast = p.on("vector", lambda e: e.tensor_copy(oh_bf[:], oh_t[:]), deps=[ld_oh])
            def rmm(e):
                e.matmul(ps_rank[:], ltri[:], oh_bf[:], start=True, stop=False)
                e.matmul(ps_rank[:], p._onesr128[:], runhi[:], start=False, stop=False)
                return e.matmul(ps_rank[:], p._onesr128[:], runlo[:], start=False, stop=True)
            mm_r = p.on("tensor", rmm, deps=[v_cast, v_run, ld_lt, ld_o128] +
                        ([rank_dep] if rank_dep else []))
            onescb2 = getattr(p, "_onescb2", None)
            if onescb2 is None:
                onescb2 = p.sb("onescb2", (128, 1), BF16)
                p._onescb2 = onescb2
                ld_ocb = p.dma("sync", onescb2[:], cc["ones_col"][:])
                p._ld_ocb = ld_ocb
            mm_t = p.on("tensor", (lambda: lambda e: e.matmul(ps_t16[:], p._onescb2[:], oh_bf[:],
                                                              start=True, stop=True))(),
                        deps=[v_cast, p._ld_ocb] + ([rank_dep] if rank_dep else []))
            v_runup = p.chain("vector", [
                lambda e: e.tensor_add(runrow[:], ps_t16[:], runrow[:]),
                lambda e: e.tensor_copy(runhi[:], runrow[:]),
                lambda e: e.tensor_sub(runtmp[:], runrow[:], runhi[:]),
                lambda e: e.tensor_copy(runlo[:], runtmp[:]),
            ], deps=[mm_t, mm_r])
            wv = p.chain("vector", [
                lambda e: e.tensor_single_scalar(keep[:], ps_rank[:], CAP - 0.5, ALU.is_le),
                lambda e: e.tensor_mul(m1[:], keep[:, 0:8], oh_t[:, 0:8]),
                lambda e: e.tensor_mul(m2[:], keep[:, 8:16], oh_t[:, 8:16]),
                lambda e: e.tensor_scalar_mul(m1[:], m1[:], tv_t[:, 0:1]),
                (lambda t: lambda e: e.scalar_tensor_tensor(
                    w_sb[:, t * E:(t + 1) * E], m2[:], tv_t[:, 1:2], m1[:],
                    ALU.mult, ALU.add))(t),
            ], deps=[mm_r, ld_tv, v_runup])
            rank_dep = wv

    w_deps = [rank_dep]
    if skip_w:
        w_deps = [p.on("vector", lambda e: e.memset(w_sb[:], 0.25))]
    import os as _os
    out_sb, out_dep = _emit_dense_moe(p, xnf, ld_xnf, w_sb, w_deps, 4096, AF.Gelu, True,
                                      n_e=n_e, split_ld=split_ld,
                                      stage=_os.environ.get("F0STAGE", "full"))
    _emit_out_finish(p, out_sb, g_o, out_dep)
    p.finalize_wait()
    return p

def build_ffn3():
    p = Prog("F3")
    cc = const_inputs(p, ["ones_row512", "ones_row128", "ones_col"])
    xnf, ld_xnf = _load_xnf(p)
    p.inp("w1", (E, D, 4096), F32)
    p.inp("b1", (E, 4096), F32)
    p.inp("w2", (E, 4096, D), F32)
    p.inp("b2", (E, D), F32)
    g1a_i = p.inp("g1a", (128, 128), F32)
    g2a_i = p.inp("g2a", (128, 128), F32)
    i1a_i = p.inp("i1a", (128, 128), F32)
    i2a_i = p.inp("i2a", (128, 128), F32)
    gio_i = p.inp("gio", (T, 4), F32)
    g_o = p.outp("g", (T, D), F32)
    import os as _os
    _dbg = bool(_os.environ.get("F3DBG"))
    if _dbg:
        th_o = p.outp("th_dbg", (128, E), F32)
        w_o = p.outp("w_dbg", (128, NT * E), F32)
    onesr128 = p.sb("onesr128b", (1, 128), BF16)
    p._onesr128 = onesr128
    ld_o128 = p.dma("sync", onesr128[:], cc["ones_row128"][:])
    p.setup_group_end(512)
    onescb = p.sb("onescb", (128, 1), BF16)
    ld_ocb = p.dma("sync", onescb[:], cc["ones_col"][:])

    CAP3 = 2458
    NIT = 34
    g1a = p.sb("g1a_s", (128, 128), F32)
    g2a = p.sb("g2a_s", (128, 128), F32)
    i1a = p.sb("i1a_s", (128, 128), F32)
    i2a = p.sb("i2a_s", (128, 128), F32)
    lds = [p.dma("sync", s_[:], a_[:]) for s_, a_ in
           ((g1a, g1a_i), (g2a, g2a_i), (i1a, i1a_i), (i2a, i2a_i))]
    eq1 = p.sb("eq1", (128, 128), F32)
    eq2 = p.sb("eq2", (128, 128), F32)
    mt1 = p.sb("mt1", (128, 128), F32)
    mt2 = p.sb("mt2", (128, 128), F32)
    rsum = p.sb("rsum", (128, 1), F32)
    rsum_b = p.sb("rsum_b", (128, 1), BF16)
    cnt_sb = p.sb("cnt_sb", (1, 1), F32)
    cond1 = p.sb("cond1", (1, 1), F32)
    cond1b = p.sb("cond1b", (1, 1), BF16)
    lo = p.sb("lo", (128, 1), F32)
    hi = p.sb("hi", (128, 1), F32)
    midc = p.sb("midc", (128, 1), F32)
    cond = p.sb("cond", (128, 1), F32)
    notc = p.sb("notc", (128, 1), F32)
    d1c = p.sb("d1c", (128, 1), F32)
    d2c = p.sb("d2c", (128, 1), F32)
    theta = p.sb("theta", (128, E), F32)
    ps_cnt = p.ps("ps_cnt", (1, 1), F32)
    ps_bc = p.ps("ps_bc", (128, 1), F32)
    th_dep = None
    for e_i in range(E):
        eqd = p.chain("vector", [
            (lambda e_i: lambda e: e.tensor_single_scalar(eq1[:], i1a[:], float(e_i), ALU.is_equal))(e_i),
            (lambda e_i: lambda e: e.tensor_single_scalar(eq2[:], i2a[:], float(e_i), ALU.is_equal))(e_i),
            lambda e: e.memset(lo[:], -1.0),
            lambda e: e.memset(hi[:], 2.0),
        ], deps=lds + ([th_dep] if th_dep else []))
        dep = eqd
        for it in range(NIT):
            vch = p.chain("vector", [
                lambda e: e.tensor_add(midc[:], lo[:], hi[:]),
                lambda e: e.tensor_scalar_mul(midc[:], midc[:], 0.5),
                lambda e: e.tensor_single_scalar(mt1[:], g1a[:], midc[:], ALU.is_gt),
                lambda e: e.tensor_mul(mt1[:], mt1[:], eq1[:]),
                lambda e: e.tensor_single_scalar(mt2[:], g2a[:], midc[:], ALU.is_gt),
                lambda e: e.tensor_mul(mt2[:], mt2[:], eq2[:]),
                lambda e: e.tensor_add(mt1[:], mt1[:], mt2[:]),
                lambda e: e.reduce_sum(rsum[:], mt1[:], axis=AX.X),
                lambda e: e.tensor_copy(rsum_b[:], rsum[:]),
            ], deps=[dep])
            mm_c = p.on("tensor", lambda e: e.matmul(ps_cnt[:], onescb[:], rsum_b[:],
                                                     start=True, stop=True),
                        deps=[vch, ld_ocb])
            sc_e = p.on("scalar", lambda e: e.copy(cnt_sb[:], ps_cnt[:]), deps=[mm_c])
            vc = p.chain("vector", [
                lambda e: e.tensor_single_scalar(cond1[:], cnt_sb[:], CAP3 + 0.5, ALU.is_lt),
                lambda e: e.tensor_copy(cond1b[:], cond1[:]),
            ], deps=[sc_e])
            mm_b = p.on("tensor", lambda e: e.matmul(ps_bc[:], onesr128[:], cond1b[:],
                                                     start=True, stop=True),
                        deps=[vc, ld_o128])
            dep = p.chain("vector", [
                lambda e: e.tensor_copy(cond[:], ps_bc[:]),
                lambda e: e.tensor_scalar(notc[:], cond[:], -1.0, 1.0, ALU.mult, ALU.add),
                lambda e: e.tensor_sub(d1c[:], midc[:], hi[:]),
                lambda e: e.tensor_mul(d1c[:], d1c[:], cond[:]),
                lambda e: e.tensor_add(hi[:], hi[:], d1c[:]),
                lambda e: e.tensor_sub(d2c[:], midc[:], lo[:]),
                lambda e: e.tensor_mul(d2c[:], d2c[:], notc[:]),
                lambda e: e.tensor_add(lo[:], lo[:], d2c[:]),
            ], deps=[mm_b])
        th_dep = p.on("vector",
                      (lambda e_i: lambda e: e.tensor_copy(theta[:, e_i:e_i + 1], hi[:]))(e_i),
                      deps=[dep])

    w_sb = p.sb("w_sb", (128, NT * E), F32)
    gio_t = p.sb("gio_t", (128, 4), F32)
    c1 = p.sb("c1", (128, 1), F32)
    c2 = p.sb("c2", (128, 1), F32)
    w_dep = th_dep
    for t in range(NT):
        ldg = p.dma("sync", gio_t[:], gio_i[t * 128:(t + 1) * 128, :], deps=[w_dep])
        fns = []
        for e_i in range(E):
            def mk(e_i, t):
                def f1(e):
                    return e.tensor_single_scalar(c1[:], gio_t[:, 0:1], theta[:, e_i:e_i + 1], ALU.is_gt)
                def f2(e):
                    return e.tensor_single_scalar(c2[:], gio_t[:, 2:3], float(e_i), ALU.is_equal)
                def f3(e):
                    return e.tensor_mul(c1[:], c1[:], c2[:])
                def f4(e):
                    return e.tensor_mul(c1[:], c1[:], gio_t[:, 0:1])
                def f5(e):
                    return e.tensor_single_scalar(c2[:], gio_t[:, 1:2], theta[:, e_i:e_i + 1], ALU.is_gt)
                def f6(e):
                    return e.tensor_mul(c1[:], c1[:], 1.0) if False else None
                return [f1, f2, f3, f4]
            fns += mk(e_i, t)
            def mk2(e_i, t):
                def f1(e):
                    return e.tensor_single_scalar(c2[:], gio_t[:, 1:2], theta[:, e_i:e_i + 1], ALU.is_gt)
                def f2(e):
                    return e.tensor_single_scalar(cond[:], gio_t[:, 3:4], float(e_i), ALU.is_equal)
                def f3(e):
                    return e.tensor_mul(c2[:], c2[:], cond[:])
                def f4(e):
                    return e.tensor_mul(c2[:], c2[:], gio_t[:, 1:2])
                def f5(e):
                    return e.scalar_tensor_tensor(w_sb[:, t * E + e_i: t * E + e_i + 1],
                                                  c2[:], 1.0, c1[:], ALU.mult, ALU.add)
                return [f1, f2, f3, f4, f5]
            fns += mk2(e_i, t)
        w_dep = p.chain("vector", fns, deps=[ldg])
    if _dbg:
        p.dma("sync", th_o[:], theta[:], deps=[w_dep])
        p.dma("sync", w_o[:], w_sb[:], deps=[w_dep])
    out_sb, out_dep = _emit_dense_moe(p, xnf, ld_xnf, w_sb, [w_dep], 4096, AF.Silu, True)
    _emit_out_finish(p, out_sb, g_o, out_dep)
    p.finalize_wait()
    return p

def build_ffn1():
    p = Prog("F1")
    cc = const_inputs(p, ["ones_row512", "ones_row128", "ones_col_f32"])
    xnf, ld_xnf = _load_xnf(p)
    wg_i = p.inp("wg", (E, D, 512), F32)
    wu_i = p.inp("wu", (E, D, 512), F32)
    wd_i = p.inp("wd", (E, 512, D), F32)
    sg_i = p.inp("sg", (D, D), F32)
    su_i = p.inp("su", (D, D), F32)
    sd_i = p.inp("sd", (D, D), F32)
    rw_i = p.inp("rw", (T, E), F32)
    g_o = p.outp("g", (T, D), F32)
    onesr128 = p.sb("onesr128b", (1, 128), BF16)
    p._onesr128 = onesr128
    ld_o128 = p.dma("sync", onesr128[:], cc["ones_row128"][:])
    p.setup_group_end(512)
    FC = 512
    FP = 4
    out_sb = p.sb("out_sb", (128, NT * D), F32)
    h_sb = p.sb("h_sb", (128, FP * T), BF16)
    sil_sb = p.sb("sil_sb", (128, 512), BF16)
    wgc = [p.sb(f"wgc{i}", (128, DC * FC), BF16) for i in range(2)]
    wuc = [p.sb(f"wuc{i}", (128, DC * FC), BF16) for i in range(2)]
    wdc = [p.sb(f"wdc{i}", (128, FP * D), BF16) for i in range(2)]
    rw_sb = p.sb("rw_sb", (128, NT * E), F32)
    ones_col = p.sb("ones_colf", (128, 1), F32)
    tmp_bf = p.sb("tmp_bf", (128, 512), BF16)
    ps_a = [p.ps(f"ps_a{i}", (128, 512), F32) for i in range(2)]
    ps_o = [p.ps(f"ps_o{i}", (128, 512), F32) for i in range(2)]
    ld_rw = p.dma("sync", rw_sb[:].rearrange("p (t e) -> p t e", t=NT),
                  rw_i.rearrange("(t p) e -> p t e", p=128))
    ld_oc = p.dma("sync", ones_col[:], cc["ones_col_f32"][:])
    z = p.on("vector", lambda e: e.memset(out_sb[:], 0.0))

    lastuse = [None, None]
    last_aps = [None, None]
    last_ops = [None, None]
    last_sil = None
    last_h = None
    out_dep = {t: z for t in range(NT)}
    passes = [(e_i, fc, False) for e_i in range(E) for fc in range(1)] + \
             [(0, fc, True) for fc in range(2)]
    ci = 0
    for (e_i, fc2, shared) in passes:
        bi = ci % 2
        dl = [lastuse[bi]] if lastuse[bi] else []
        if not shared:
            ldg = p.dma("gpsimd", wgc[bi][:].rearrange("p (c f) -> p c f", c=DC),
                        wg_i[e_i].rearrange("(c p) f -> p c f", p=128), deps=dl)
            ldu = p.dma("gpsimd", wuc[bi][:].rearrange("p (c f) -> p c f", c=DC),
                        wu_i[e_i].rearrange("(c p) f -> p c f", p=128), deps=dl)
            ldd = p.dma("gpsimd", wdc[bi][:].rearrange("p (q d) -> p q d", q=FP),
                        wd_i[e_i].rearrange("(q p) d -> p q d", p=128), deps=dl)
        else:
            f0 = fc2 * FC
            ldg = p.dma("gpsimd", wgc[bi][:].rearrange("p (c f) -> p c f", c=DC),
                        sg_i[:, f0:f0 + FC].rearrange("(c p) f -> p c f", p=128), deps=dl)
            ldu = p.dma("gpsimd", wuc[bi][:].rearrange("p (c f) -> p c f", c=DC),
                        su_i[:, f0:f0 + FC].rearrange("(c p) f -> p c f", p=128), deps=dl)
            ldd = p.dma("gpsimd", wdc[bi][:].rearrange("p (q d) -> p q d", q=FP),
                        sd_i[f0:f0 + FC, :].rearrange("(q p) d -> p q d", p=128), deps=dl)
        h_done = []
        for fp in range(FP):
            for tc in range(4):
                def gmm(bi, fp, tc, w_s, pb):
                    def f(e_):
                        for dc in range(DC):
                            e_.matmul(ps_a[pb][:],
                                      w_s[bi][:, dc * FC + fp * 128: dc * FC + (fp + 1) * 128],
                                      xnf[:, dc * T + tc * 512: dc * T + (tc + 1) * 512],
                                      start=(dc == 0), stop=False)
                        return p.end_group(e_, ps_a[pb][:], 128, 512)
                    return f
                mm_g = p.on("tensor", gmm(bi, fp, tc, wgc, 0),
                            deps=[ldg, ld_xnf] + p._ge_dep + ([last_aps[0]] if last_aps[0] else []))
                sil = p.on("scalar",
                           lambda e_: e_.activation(sil_sb[:], ps_a[0][:], AF.Silu),
                           deps=[mm_g] + ([last_sil] if last_sil else []))
                last_aps[0] = sil
                mm_u = p.on("tensor", gmm(bi, fp, tc, wuc, 1),
                            deps=[ldu] + ([last_aps[1]] if last_aps[1] else []))
                hv = p.on("vector",
                          (lambda fp, tc: lambda e_: e_.scalar_tensor_tensor(
                              h_sb[:, fp * T + tc * 512: fp * T + (tc + 1) * 512],
                              ps_a[1][:], 1.0, sil_sb[:], ALU.mult, ALU.mult))(fp, tc),
                          deps=[mm_u, sil, prev_loads_hwar(ci, h_done)])
                last_aps[1] = hv
                last_sil = hv
                h_done.append(hv)
        last_h = h_done[-1]
        for t in range(NT):
            for half in range(2):
                pb = half
                def omm(bi, t, half, pb):
                    def f(e_):
                        for fp in range(FP):
                            e_.matmul(ps_o[pb][:],
                                      h_sb[:, fp * T + t * 128: fp * T + (t + 1) * 128],
                                      wdc[bi][:, fp * D + half * 512: fp * D + half * 512 + 512],
                                      start=(fp == 0), stop=False)
                        return p.end_group(e_, ps_o[pb][:], 128, 512)
                    return f
                mm = p.on("tensor", omm(bi, t, half, pb),
                          deps=[h_done[-1], ldd] + p._ge_dep +
                               ([last_ops[pb]] if last_ops[pb] else []))
                lastuse[bi] = mm
                wcol = ones_col[:] if shared else rw_sb[:, t * E + e_i: t * E + e_i + 1]
                sc = p.on("scalar",
                          (lambda pb, wcol: lambda e_: e_.activation(
                              tmp_bf[:], ps_o[pb][:], AF.Copy, scale=wcol))(pb, wcol),
                          deps=[mm, ld_rw, ld_oc] + ([last_ops[pb]] if last_ops[pb] else []))
                ad = p.on("vector",
                          (lambda t, half: lambda e_: e_.tensor_add(
                              out_sb[:, t * D + half * 512: t * D + half * 512 + 512],
                              tmp_bf[:],
                              out_sb[:, t * D + half * 512: t * D + half * 512 + 512]))(t, half),
                          deps=[sc, out_dep[t]])
                last_ops[pb] = ad
                out_dep[t] = ad
        ci += 1
    _emit_out_finish(p, out_sb, g_o, out_dep)
    p.finalize_wait()
    return p


def prev_loads_hwar(ci, h_done):
    return None

def build_ffn2():
    p = Prog("F2")
    cc = const_inputs(p, ["ident", "ones_row128"])
    xnf, ld_xnf = _load_xnf(p)
    w1t_i = p.inp("w1t", (D, 4096), F32)
    w2t_i = p.inp("w2t", (D, 4096), F32)
    h1t_i = p.inp("h1t", (4096, 4096), F32)
    h2t_i = p.inp("h2t", (4096, 4096), F32)
    w3t_i = p.inp("w3t", (4096, D), F32)
    s1_i = p.inp("s1", (128, 32), F32)
    s2_i = p.inp("s2", (128, 32), F32)
    h1s_i = p.inp("h1s", (128, 32), F32)
    h2s_i = p.inp("h2s", (128, 32), F32)
    w3s_i = p.inp("w3s", (128, 8), F32)
    g_o = p.outp("g", (T, D), F32)
    import os as _os
    _dbg = bool(_os.environ.get("F2DBG"))
    if _dbg:
        t1_o = p.outp("t1_dbg", (128, 32 * 1024), BF16)
        t2_o = p.outp("t2_dbg", (128, 32 * 1024), BF16)
    onesr128 = p.sb("onesr128b", (1, 128), BF16)
    p._onesr128 = onesr128
    ld_o128 = p.dma("sync", onesr128[:], cc["ones_row128"][:])
    p.setup_group_end(512)
    id_s = p.sb("id_s", (128, 128), BF16)
    ld_id = p.dma("sync", id_s[:], cc["ident"][:])
    SW = 1024
    NS = T // SW
    t1 = p.sb("t1", (128, 32 * SW), BF16)
    t2 = p.sb("t2", (128, 32 * SW), BF16)
    sil_sb = p.sb("sil_sb", (128, 512), BF16)
    gfm = p.sb("gfm", (128, DC * SW), BF16)
    gtile = p.sb("gtile", (128, D), F32)
    wa = [p.sb(f"wa{i}", (128, 8 * 128), BF16) for i in range(2)]
    wb = [p.sb(f"wb{i}", (128, 8 * 128), BF16) for i in range(2)]
    wh = [p.sb(f"wh{i}", (128, 32 * 128), BF16) for i in range(2)]
    scl = {}
    ld_scl = {}
    for nm, ap_ in (("s1", s1_i), ("s2", s2_i), ("h1s", h1s_i), ("h2s", h2s_i), ("w3s", w3s_i)):
        scl[nm] = p.sb(nm + "_s", list(ap_.tensor.shape) if False else (128, ap_.shape[1]), F32)
        ld_scl[nm] = p.dma("sync", scl[nm][:], ap_[:])
    ps_a = [p.ps(f"ps_a{i}", (128, 512), F32) for i in range(2)]
    ps_c = [p.ps(f"ps_c{i}", (128, 512), F32) for i in range(2)]
    ps_tr = [p.ps(f"ps_tr{i}", (128, 128), BF16) for i in range(2)]

    lw = {"a": [None, None], "b": [None, None], "h": [None, None]}
    last_ps = [None, None]
    last_psc = [None, None]
    last_sil = None
    last_t1w = None
    last_t1r = None
    last_t2w = None
    last_t2r = None
    last_trev = [None, None]
    ci = 0
    for s_ in range(NS):
        tok0 = s_ * SW
        aw = []
        for ob in range(32):
            bi = ci % 2
            ld1 = p.dma("gpsimd", wa[bi][:].rearrange("p (c f) -> p c f", c=DC),
                        w1t_i[:, ob * 128:(ob + 1) * 128].rearrange("(c p) f -> p c f", p=128),
                        deps=[lw["a"][bi]] if lw["a"][bi] else [])
            ld2 = p.dma("gpsimd", wb[bi][:].rearrange("p (c f) -> p c f", c=DC),
                        w2t_i[:, ob * 128:(ob + 1) * 128].rearrange("(c p) f -> p c f", p=128),
                        deps=[lw["b"][bi]] if lw["b"][bi] else [])
            for tc in range(SW // 512):
                def amm(bi, ob, tc, w_s, pb, ps, tok0):
                    def f(e_):
                        for dc in range(DC):
                            e_.matmul(ps[pb][:],
                                      w_s[bi][:, dc * 128: dc * 128 + 128],
                                      xnf[:, dc * T + tok0 + tc * 512: dc * T + tok0 + tc * 512 + 512],
                                      start=(dc == 0), stop=False)
                        return p.end_group(e_, ps[pb][:], 128, 512)
                    return f
                mm1 = p.on("tensor", amm(bi, ob, tc, wa, 0, ps_a, tok0),
                           deps=[ld1, ld_xnf] + p._ge_dep + ([last_ps[0]] if last_ps[0] else []))
                sil = p.on("scalar",
                           (lambda ob: lambda e_: e_.activation(
                               sil_sb[:], ps_a[0][:], AF.Silu,
                               scale=scl["s1"][:, ob:ob + 1]))(ob),
                           deps=[mm1, ld_scl["s1"]] + ([last_sil] if last_sil else []))
                last_ps[0] = sil
                mm2 = p.on("tensor", amm(bi, ob, tc, wb, 1, ps_a, tok0),
                           deps=[ld2] + ([last_ps[1]] if last_ps[1] else []))
                hv = p.on("vector",
                          (lambda ob, tc: lambda e_: e_.scalar_tensor_tensor(
                              t1[:, ob * SW + tc * 512: ob * SW + tc * 512 + 512],
                              ps_a[1][:], scl["s2"][:, ob:ob + 1], sil_sb[:],
                              ALU.mult, ALU.mult))(ob, tc),
                          deps=[mm2, sil, ld_scl["s2"]] + ([last_t1r] if last_t1r else []))
                last_ps[1] = hv
                last_sil = hv
                aw.append(hv)
            lw["a"][bi] = mm1
            lw["b"][bi] = mm2
            ci += 1
        last_t1w = aw[-1]
        if _dbg and s_ == 0:
            last_t1r = p.dma("sync", t1_o[:], t1[:], deps=[last_t1w])
        for stage, (win, wsn, src, dstw, nob, actf) in enumerate((
                ("h1t", "h1s", "t1", "t2", 32, AF.Gelu),
                ("h2t", "h2s", "t2", "t1", 32, AF.Gelu),
                ("w3t", "w3s", "t1", "gfm", 8, AF.Silu))):
            w_in = p.ap[win]
            src_sb = {"t1": t1, "t2": t2}[src]
            dst_sb = {"t2": t2, "t1": t1, "gfm": gfm}[dstw]
            dw = []
            for ob in range(nob):
                bi = ci % 2
                ldh = p.dma("gpsimd", wh[bi][:].rearrange("p (c f) -> p c f", c=32),
                            w_in[:, ob * 128:(ob + 1) * 128].rearrange("(c p) f -> p c f", p=128),
                            deps=[lw["h"][bi]] if lw["h"][bi] else [])
                for tc in range(SW // 512):
                    def bmm(bi, ob, tc, src_sb):
                        def f(e_):
                            for ic in range(32):
                                e_.matmul(ps_c[tc][:],
                                          wh[bi][:, ic * 128: ic * 128 + 128],
                                          src_sb[:, ic * SW + tc * 512: ic * SW + tc * 512 + 512],
                                          start=(ic == 0), stop=False)
                            return p.end_group(e_, ps_c[tc][:], 128, 512)
                        return f
                    srcdep = last_t1w if src == "t1" else last_t2w
                    deps = [ldh, srcdep] + p._ge_dep
                    if last_psc[tc] is not None:
                        deps.append(last_psc[tc])
                    wardep = None
                    if ob == 0 and tc == 0:
                        wardep = last_t2r if dstw == "t2" else (last_t1r if dstw == "t1" else None)
                    mm = p.on("tensor", bmm(bi, ob, tc, src_sb), deps=deps + ([wardep] if wardep else []))
                    lw["h"][bi] = mm
                    if src == "t1":
                        last_t1r = mm
                    else:
                        last_t2r = mm
                    colw = SW if dstw != "gfm" else SW
                    ev = p.on("scalar",
                              (lambda ob, tc, dst_sb, actf, wsn: lambda e_: e_.activation(
                                  dst_sb[:, ob * SW + tc * 512: ob * SW + tc * 512 + 512],
                                  ps_c[tc][:], actf,
                                  scale=scl[wsn][:, ob:ob + 1]))(ob, tc, dst_sb, actf, wsn),
                              deps=[mm, ld_scl[wsn]])
                    last_psc[tc] = ev
                    dw.append(ev)
                ci += 1
            if dstw == "t2":
                last_t2w = dw[-1]
                if _dbg and s_ == 0:
                    last_t2r = p.dma("sync", t2_o[:], t2[:], deps=[last_t2w])
            elif dstw == "t1":
                last_t1w = dw[-1]
        gfm_done = dw[-1]
        for tt in range(SW // 128):
            evs = []
            for dmb in range(DC):
                j = dmb % 2
                tr = p.on("tensor",
                          (lambda dmb, tt, j: lambda e_: e_.transpose(
                              ps_tr[j][:], gfm[:, dmb * SW + tt * 128: dmb * SW + (tt + 1) * 128],
                              id_s[:]))(dmb, tt, j),
                          deps=[gfm_done, ld_id] + ([last_trev[j]] if last_trev[j] else []))
                ev = p.on("scalar",
                          (lambda dmb, j: lambda e_: e_.copy(
                              gtile[:, dmb * 128:(dmb + 1) * 128], ps_tr[j][:]))(dmb, j),
                          deps=[tr])
                last_trev[j] = ev
                evs.append(ev)
            st = p.dma("sync", g_o[tok0 + tt * 128: tok0 + (tt + 1) * 128, :], gtile[:],
                       deps=evs)
            last_trev[0] = st
            last_trev[1] = st
    p.finalize_wait()
    return p


def build_attn(layer):
    p = Prog(f"A{layer}")
    L3 = layer == 3
    NKVB = 65 if L3 else 64
    xnb_i = p.inp("xnb", (DC, 128, S), BF16)
    cl = ["ones_row128", "ones_row512", "masks3" if L3 else "masks", "selT128", "sel128x2", "pickA", "pickB"]
    if layer == 0:
        cl += ["rope_cos2", "rope_sin2", "ropeRT2"]
    if L3:
        cl += ["mask3_last"]
    cc = const_inputs(p, cl)
    wq_i = p.inp("wq", (D, 256), F32)
    wk_i = p.inp("wk", (D, 128), F32)
    if layer == 1:
        wk2_i = p.inp("wk2", (D, 128), F32)
        wv_i = p.inp("wv", (D, 256), F32)
    else:
        wv_i = p.inp("wv", (D, 64), F32)
    if layer == 0:
        bq_i = p.inp("bq", (1, 256), F32)
        bk_i = p.inp("bk", (1, 128), F32)
        bv_i = p.inp("bv", (1, 64), F32)
    if L3:
        nullk_i = p.inp("nullk", (1, 64), F32)
        nullkd_i = p.inp("nullkd", (128, 1), F32)
    ao_o = p.outp("ao", (256, S), BF16)
    import os as _os
    _adbg = bool(_os.environ.get("ADBG"))
    if _adbg:
        oh_o2 = p.outp("oh_dbg", (65, 512), F32)
        ps_o2 = p.outp("ps_dbg", (128, 512), F32)
        pp_o2 = p.outp("pp_dbg", (128, 512), BF16)
        q0_o = p.outp("q0_dbg", (128, S), BF16)
        k0_o = p.outp("k0_dbg", (128, NKVB * 128), BF16)
        v0_o = p.outp("v0_dbg", (128, NKVB * 65), BF16)
        nb_o = p.outp("nb_dbg", (128, 2), F32)

    onesr128 = p.sb("onesr128b", (1, 128), BF16)
    p._onesr128 = onesr128
    ld_o128 = p.dma("sync", onesr128[:], cc["ones_row128"][:])
    p.setup_group_end(512)
    onesr512 = p.sb("onesr512b", (1, 512), BF16)
    ld_o512 = p.dma("sync", onesr512[:], cc["ones_row512"][:])
    SCALE = 0.125
    NC_ = S // 512
    NPAIR_K = 2 if layer == 1 else 1
    NVH = 4 if layer == 1 else 1

    xt = [p.sb("xt0", (128, DC * 512), BF16)] * 2
    xtb = [p.sb("xtb0", (128, DC * 128), BF16)] * 2
    q_s = [p.sb(f"q_s{i}", (128, S), BF16) for i in range(2)]
    k_s = [p.sb(f"k_s{i}", (128, NKVB * 128), BF16) for i in range(NPAIR_K)]
    v_s = [p.sb(f"v_s{i}", (128, NKVB * 65), BF16) for i in range(NVH)]
    wq_s = p.sb("wq_s", (128, DC * 256), BF16)
    wk_s = [p.sb(f"wk_sb{i}", (128, DC * 128), BF16) for i in range(NPAIR_K)]
    wv_cols = 256 if layer == 1 else 64
    wv_s = p.sb("wv_s", (128, DC * wv_cols), BF16)
    out_h = [p.sb(f"out_h{i}", (65, S), F32) for i in range(2)]
    l0_s = p.sb("l0_s", (1, 512), F32)
    linv = p.sb("linv", (1, 512), BF16)
    linv_f = p.sb("linv_f", (1, 512), F32)
    p_sb = [p.sb(f"p_sb{i}", (128, 512), BF16) for i in range(4)]
    mask_s = p.sb("mask_s", (128, 4 * 512), BF16)
    if L3:
        m3l_s = p.sb("m3l_s", (128, 512), BF16)
    selT = p.sb("selT", (2, 128), BF16)
    selC = p.sb("selC", (128, 2), BF16)
    pick = [p.sb(f"pick{i}", (2, 128), BF16) for i in range(2)]
    sq_f = p.sb("sq_f", (128, 512), BF16)
    prodc = p.sb("prodc", (2, 1), F32)
    kn2 = [p.sb(f"kn2_{i}", (2, 1), F32) for i in range(NPAIR_K)]
    qn2 = [p.sb(f"qn2_{i}", (2, 1), F32) for i in range(2)]
    tmp21 = p.sb("tmp21", (2, 1), F32)
    negb = p.sb("negb", (2, 1), BF16)
    negb_col = [p.sb(f"negbc{i}", (128, 1), F32) for i in range(4)]
    vtmp = p.sb("vtmp", (128, wv_cols), F32)
    ao_sb = p.sb("ao_sb", (64, 512), BF16)
    if layer == 0:
        cos_s = p.sb("cos_s", (128, S), BF16)
        sin_s = p.sb("sin_s", (128, S), BF16)
        rot_s = p.sb("rot_s", (128, 128), BF16)
        qraw = p.sb("qraw", (128, 512), BF16)
        rtmp = p.sb("rtmp", (128, 512), F32)
        b_sb = {}
    if L3:
        kv_dram = p.dram("kv_scratch", (S, 64), F32)
        nulld = p.sb("nulld", (128, 1), BF16)

    ps_q = [p.ps(f"ps_q{i}", (128, 512), F32) for i in range(2)]
    ps_s = [p.ps(f"ps_s{i}", (128, 512), F32) for i in range(2)]
    ps_pv = [p.ps(f"ps_pv{i}", (128, 512), F32) for i in range(2)]
    ps_n = p.ps("ps_n", (2, 512), F32)
    ps_bc = p.ps("ps_bc", (128, 512), F32)
    ps_rot = ps_bc if layer == 0 else None

    ld_msk = p.dma("sync", mask_s[:].rearrange("p (a b) -> p a b", a=4),
                   cc["masks3" if L3 else "masks"].rearrange("a p b -> p a b"))
    ld_sel = p.dma("sync", selT[:], cc["selT128"][:])
    ld_selc = p.dma("sync", selC[:], cc["sel128x2"][:])
    ld_pk = [p.dma("sync", pick[0][:], cc["pickA"][:]),
             p.dma("sync", pick[1][:], cc["pickB"][:])]
    lds = [ld_o128, ld_o512, ld_msk, ld_sel, ld_selc] + ld_pk
    ld_wq = p.dma("gpsimd", wq_s[:].rearrange("p (c f) -> p c f", c=DC),
                  wq_i.rearrange("(c p) f -> p c f", p=128))
    ld_wk = [p.dma("gpsimd", wk_s[0][:].rearrange("p (c f) -> p c f", c=DC),
                   wk_i.rearrange("(c p) f -> p c f", p=128))]
    ld_wv = p.dma("gpsimd", wv_s[:].rearrange("p (c f) -> p c f", c=DC),
                  wv_i.rearrange("(c p) f -> p c f", p=128))
    lds += [ld_wq, ld_wk[0], ld_wv]
    if layer == 1:
        ld_wk.append(p.dma("gpsimd", wk_s[1][:].rearrange("p (c f) -> p c f", c=DC),
                           wk2_i.rearrange("(c p) f -> p c f", p=128)))
        lds.append(ld_wk[1])
    if layer == 0:
        for nm, ap_, w in (("bq", bq_i, 256), ("bk", bk_i, 128), ("bv", bv_i, 64)):
            b_sb[nm] = p.sb(nm + "_sb", (1, w), BF16)
            lds.append(p.dma("gpsimd", b_sb[nm][:], ap_[:]))
        lds.append(p.dma("sync", cos_s[:], cc["rope_cos2"][:]))
        lds.append(p.dma("sync", sin_s[:], cc["rope_sin2"][:]))
        lds.append(p.dma("sync", rot_s[:], cc["ropeRT2"][:]))
    if L3:
        ld_null = p.dma("gpsimd", nulld[:], nullkd_i[:])
        lds.append(ld_null)

    vz = [p.on("vector", (lambda i: lambda e: e.memset(v_s[i][:], 1.0))(i))
          for i in range(NVH)]
    kz = [p.on("vector", (lambda i: lambda e: e.memset(k_s[i][:], 0.0))(i))
          for i in range(NPAIR_K)]

    def rope_emit(dst_sb, dst_off, src_bf, rotdep, evdep):
        steps = []
        def f1(e):
            return e.tensor_mul(rtmp[:, :], ps_rot[:, :], sin_s[:, dst_off:dst_off + 512])
        def f2(e):
            return e.scalar_tensor_tensor(
                dst_sb[:, dst_off:dst_off + 512], src_bf[:, :], 1.0,
                cos_s[:, dst_off:dst_off + 512], ALU.mult, ALU.mult)
        def f3(e):
            return e.tensor_add(dst_sb[:, dst_off:dst_off + 512],
                                dst_sb[:, dst_off:dst_off + 512], rtmp[:, :])
        return p.chain("vector", [f1, f2, f3], deps=[rotdep, evdep])

    last_evq = None
    last_evk = None
    prev_use = None
    for tc in range(NC_):
        bi = 0
        ldx = p.dma("sync", xt[bi][:].rearrange("p (c f) -> p c f", c=DC),
                    xnb_i[:, :, tc * 512:(tc + 1) * 512].rearrange("c p f -> p c f"),
                    deps=[prev_use] if prev_use else [])
        uses = []
        for pr in range(2):
            def qmm(bi, pr):
                def f(e_):
                    for dc in range(DC):
                        e_.matmul(ps_q[0][:],
                                  wq_s[:, dc * 256 + pr * 128: dc * 256 + pr * 128 + 128],
                                  xt[bi][:, dc * 512:(dc + 1) * 512],
                                  start=(dc == 0), stop=False)
                    if layer == 0:
                        return e_.matmul(ps_q[0][:],
                                         b_sb["bq"][0:1, pr * 128: pr * 128 + 128],
                                         onesr512[:], start=False, stop=True)
                    return p.end_group(e_, ps_q[0][:], 128, 512)
                return f
            mmq = p.on("tensor", qmm(bi, pr),
                       deps=[ldx, ld_wq] + p._ge_dep + ([last_evq] if last_evq else lds))
            if layer == 0:
                ev0 = p.on("scalar", lambda e: e.copy(qraw[:], ps_q[0][:]),
                           deps=[mmq] + ([last_evq] if last_evq else []))
                mmr = p.on("tensor", lambda e: e.matmul(ps_rot[:], rot_s[:], qraw[:],
                                                        start=True, stop=True), deps=[ev0])
                evq = rope_emit(q_s[pr], tc * 512, qraw, mmr, ev0)
            else:
                evq = p.on("scalar",
                           (lambda pr, tc: lambda e: e.copy(
                               q_s[pr][:, tc * 512:(tc + 1) * 512], ps_q[0][:]))(pr, tc),
                           deps=[mmq])
            last_evq = evq
            uses.append(evq)
        for ki in range(NPAIR_K):
            kcol0 = 1 if L3 else 0
            def kmm(bi, ki):
                def f(e_):
                    for dc in range(DC):
                        e_.matmul(ps_q[1][:], wk_s[ki][:, dc * 128: dc * 128 + 128],
                                  xt[bi][:, dc * 512:(dc + 1) * 512],
                                  start=(dc == 0), stop=False)
                    if layer == 0:
                        return e_.matmul(ps_q[1][:], b_sb["bk"][:], onesr512[:],
                                         start=False, stop=True)
                    return p.end_group(e_, ps_q[1][:], 128, 512)
                return f
            mmk = p.on("tensor", kmm(bi, ki),
                       deps=[ldx, ld_wk[ki]] + p._ge_dep + kz +
                            ([last_evk] if last_evk else []))
            if layer == 0:
                ev0 = p.on("scalar", lambda e: e.copy(qraw[:], ps_q[1][:]),
                           deps=[mmk, last_evq])
                mmr = p.on("tensor", lambda e: e.matmul(ps_rot[:], rot_s[:], qraw[:],
                                                        start=True, stop=True), deps=[ev0])
                evk = rope_emit(k_s[0], tc * 512, qraw, mmr, ev0)
            else:
                evk = p.on("scalar",
                           (lambda ki, tc, kcol0: lambda e: e.copy(
                               k_s[ki][:, kcol0 + tc * 512: kcol0 + tc * 512 + 512],
                               ps_q[1][:]))(ki, tc, kcol0),
                           deps=[mmk])
            last_evk = evk
            uses.append(evk)
        prev_use = uses[-1]
    if L3:
        last_evk = p.chain("vector", [
            lambda e: e.tensor_copy(k_s[0][:, 0:1], nulld[:]),
        ], deps=[ld_null, last_evk])

    last_vp = None
    vparts = []
    for blk in range(64):
        bi = 0
        ldxb = p.dma("sync", xtb[bi][:].rearrange("p (c f) -> p c f", c=DC),
                     xnb_i[:, :, blk * 128:(blk + 1) * 128].rearrange("c p f -> p c f"),
                     deps=[last_vp] if last_vp else [])
        def vmm(bi):
            def f(e_):
                for dc in range(DC):
                    e_.matmul(ps_q[1][:, 0:wv_cols],
                              xtb[bi][:, dc * 128:(dc + 1) * 128],
                              wv_s[:, dc * wv_cols:(dc + 1) * wv_cols],
                              start=(dc == 0), stop=False)
                if layer == 0:
                    return e_.matmul(ps_q[1][:, 0:wv_cols], onesr128[:], b_sb["bv"][:],
                                     start=False, stop=True)
                return p.end_group(e_, ps_q[1][:, 0:wv_cols], 128, wv_cols)
            return f
        mmv = p.on("tensor", vmm(bi),
                   deps=[ldxb, ld_wv] + p._ge_dep + ([last_vp] if last_vp else [last_evk]))
        if L3:
            evv = p.on("vector", lambda e: e.tensor_copy(vtmp[:], ps_q[1][:, 0:64]),
                       deps=[mmv] + ([last_vp] if last_vp else []))
            std = p.dma("sync", kv_dram[blk * 128:(blk + 1) * 128, :], vtmp[:, 0:64],
                        deps=[evv])
            last_vp = std
        else:
            evs = []
            for h in range(NVH):
                evv = p.on("scalar",
                           (lambda h, blk: lambda e: e.copy(
                               v_s[h][:, blk * 65: blk * 65 + 64],
                               ps_q[1][:, h * 64:(h + 1) * 64]))(h, blk),
                           deps=[mmv] + vz + ([last_vp] if last_vp else []))
                evs.append(evv)
            last_vp = evs[-1]
        vparts.append(last_vp)
    if L3:
        d0 = p.dma("gpsimd", v_s[0][0:1, 0:64], nullk_i[:], deps=[last_vp] + vz)
        d1 = p.dma("gpsimd", v_s[0][1:128, 0:64], kv_dram[0:127, :], deps=[d0])
        last = d1
        for blk in range(1, 64):
            last = p.dma("gpsimd", v_s[0][:, blk * 65: blk * 65 + 64],
                         kv_dram[blk * 128 - 1: blk * 128 + 127, :], deps=[last])
        last = p.dma("gpsimd", v_s[0][0:1, 64 * 65: 64 * 65 + 64],
                     kv_dram[S - 1: S, :], deps=[last])
        last_vp = last
        vparts.append(last_vp)

    def norms_emit(src_sb, ncols, dst, deps):
        z = p.on("vector", (lambda dst: lambda e: e.memset(dst[:], 0.0))(dst), deps=deps)
        last = z
        lastn = None
        for ch in range(ncols // 512):
            sq = p.on("scalar",
                      (lambda ch: lambda e: e.activation(
                          sq_f[:], src_sb[:, ch * 512:(ch + 1) * 512], AF.Square))(ch),
                      deps=[last] if last else [])
            mm = p.on("tensor", lambda e: e.matmul(ps_n[:], selC[:], sq_f[:],
                                                   start=True, stop=True),
                      deps=[sq, ld_selc] + ([lastn] if lastn else []))
            last = p.chain("vector", [
                lambda e: e.tensor_reduce(tmp21[:], ps_n[:], AX.X, ALU.max),
                (lambda dst: lambda e: e.tensor_max(dst[:], dst[:], tmp21[:]))(dst),
            ], deps=[mm])
            lastn = last
        return last
    nq = []
    nk = []
    prevn = None
    for pr in range(2):
        nq.append(norms_emit(q_s[pr], S, qn2[pr],
                             [last_evq, last_evk] + ([prevn] if prevn else [])))
        prevn = nq[-1]
    for ki in range(NPAIR_K):
        nk.append(norms_emit(k_s[ki], NKVB * 128, kn2[ki], [prevn]))
        prevn = nk[-1]

    bias_done = []
    for pr in range(2):
        ki = pr if layer == 1 else 0
        vch = p.chain("vector", [
            (lambda pr, ki: lambda e: e.tensor_mul(prodc[:], qn2[pr][:], kn2[ki][:]))(pr, ki),
        ], deps=[nq[pr], nk[ki]] + (bias_done[-1:] if bias_done else []))
        sch = p.on("scalar", lambda e: e.activation(prodc[:], prodc[:], AF.Sqrt), deps=[vch])
        vch2 = p.chain("vector", [
            lambda e: e.tensor_scalar_mul(prodc[:], prodc[:], -1.02 * SCALE),
            lambda e: e.tensor_copy(negb[:], prodc[:]),
        ], deps=[sch])
        for half in range(2):
            mmb = p.on("tensor",
                       (lambda half: lambda e: e.matmul(ps_bc[:, 0:1], pick[half][:],
                                                        negb[:], start=True, stop=True))(half),
                       deps=[vch2] + ld_pk + (bias_done[-1:] if bias_done else []))
            evb = p.on("vector",
                       (lambda pr, half: lambda e: e.tensor_copy(
                           negb_col[2 * pr + half][:], ps_bc[:, 0:1]))(pr, half),
                       deps=[mmb])
            bias_done.append(evb)

    if _adbg:
        p.dma("sync", q0_o[:], q_s[0][:], deps=[last_evq, last_evk])
        p.dma("sync", k0_o[:], k_s[0][:], deps=[last_evq, last_evk])
        p.dma("sync", v0_o[:], v_s[0][:], deps=[last_vp] + vparts[:1])
        nbpk = p.sb("nbpk", (128, 2), F32)
        nbc = p.chain("vector", [
            lambda e: e.memset(nbpk[:], 0.0),
            lambda e: e.tensor_copy(nbpk[0:2, 0:1], negb[:]),
            lambda e: e.tensor_copy(nbpk[:, 1:2], negb_col[0][:]),
        ], deps=bias_done)
        p.dma("sync", nb_o[:], nbpk[:], deps=[nbc])
    sbank = {(h, 0): ps_s[h] for h in range(2)}
    sbank.update({(h, 1): ps_q[h] for h in range(2)})
    last_exp = {}
    last_pv_p = {}
    last_pmaskv = None
    last_fin = None
    it = 0
    for pr in range(2):
        hA, hB = 2 * pr, 2 * pr + 1
        ki = pr if layer == 1 else 0
        pv_evac = {}
        for half in range(2):
            pv_evac[half] = p.on("vector",
                                 (lambda half: lambda e: e.memset(out_h[half][:], 0.0))(half),
                                 deps=[last_fin] if last_fin else [])
        for g in range((NKVB + 7) // 8):
            blo = g * 8
            bhi = min(NKVB, blo + 8)
            for c in range(2 * g, NC_):
                blks = [b for b in range(blo, bhi) if b <= 4 * c + 3]
                if not blks:
                    continue
                first = True
                for b in blks:
                    par = it % 2
                    it += 1
                    def smm(pr, ki, b, c, half, par):
                        def f(e_):
                            return e_.matmul(
                                sbank[(half, par)][:],
                                k_s[ki][half * 64:(half + 1) * 64, b * 128:(b + 1) * 128],
                                q_s[pr][half * 64:(half + 1) * 64, c * 512:(c + 1) * 512],
                                start=True, stop=True,
                                tile_position=(half * 64, 0))
                        return f
                    sdeps = [last_evq, last_evk, last_vp] + bias_done
                    mms = []
                    for half in range(2):
                        dd = sdeps + ([last_exp[(half, par)]] if (half, par) in last_exp else [])
                        mms.append(p.on("tensor", smm(pr, ki, b, c, half, par), deps=dd))
                    exps = []
                    for half in range(2):
                        ex = p.on("scalar",
                                  (lambda half, pr, par: lambda e: e.activation(
                                      p_sb[2 * half + par][:], sbank[(half, par)][:], AF.Exp,
                                      bias=negb_col[2 * pr + half][:], scale=SCALE))(half, pr, par),
                                  deps=[mms[half]] +
                                       ([last_pv_p[(half, par)]] if (half, par) in last_pv_p else []))
                        exps.append(ex)
                        last_exp[(half, par)] = ex
                    pdeps = []
                    r = b - 4 * c
                    for half in range(2):
                        if L3 and b == 64:
                            mfn = (lambda half, par: lambda e: e.tensor_mul(
                                p_sb[2 * half + par][:], p_sb[2 * half + par][:], m3l_s[:]))(half, par)
                        elif 0 <= r <= 3:
                            mfn = (lambda half, r, par: lambda e: e.tensor_mul(
                                p_sb[2 * half + par][:], p_sb[2 * half + par][:],
                                mask_s[:, r * 512:(r + 1) * 512]))(half, r, par)
                        else:
                            mfn = None
                        if mfn is not None:
                            pm = p.on("vector", mfn, deps=[exps[half], ld_msk] +
                                      ([last_pmaskv] if last_pmaskv else []))
                            last_pmaskv = pm
                            pdeps.append(pm)
                        else:
                            pdeps.append(exps[half])
                    if _adbg and pr == 0 and g == int(_os.environ.get("DG", "0")) \
                            and c == int(_os.environ.get("DCk", "0")) and b == int(_os.environ.get("DB", "0")):
                        snp = p.sb("snp", (128, 512), F32)
                        sv = p.on("vector", lambda e: e.tensor_copy(snp[:], ps_s[0][:]),
                                  deps=[mms[0], pdeps[0]])
                        p.dma("sync", ps_o2[:], snp[:], deps=[sv])
                        dpp = p.dma("sync", pp_o2[:], p_sb[0][:], deps=[pdeps[0]])
                        pdeps[0] = dpp
                    for half in range(2):
                        def pvmm(pr, half, b, first, lastb, par):
                            def f(e_):
                                e_.matmul(ps_pv[half][0:65, :],
                                          v_s[0 if NVH == 1 else (2 * pr + half)][:, b * 65:(b + 1) * 65],
                                          p_sb[2 * half + par][:],
                                          start=first, stop=False)
                                if lastb:
                                    return p.end_group(e_, ps_pv[half][0:65, :], 65, 512)
                                return None
                            return f
                        lastb = (b == blks[-1])
                        dd = [pdeps[half], last_vp] + p._ge_dep
                        if first and (half, "acc") in last_pv_p:
                            dd.append(last_pv_p[(half, "acc")])
                        pv = p.on("tensor", pvmm(pr, half, b, first, lastb, par), deps=dd)
                        last_pv_p[(half, par)] = pv
                        if lastb:
                            ad = p.on("vector",
                                      (lambda half, c: lambda e: e.tensor_add(
                                          out_h[half][:, c * 512:(c + 1) * 512],
                                          ps_pv[half][0:65, :],
                                          out_h[half][:, c * 512:(c + 1) * 512]))(half, c),
                                      deps=[pv, pv_evac[half]])
                            pv_evac[half] = ad
                            last_pv_p[(half, "acc")] = ad
                            if _adbg and pr == 0 and half == 0 and g == int(_os.environ.get("DG", "0")) \
                                    and c == int(_os.environ.get("DCk", "0")):
                                dx = p.dma("sync", oh_o2[:], out_h[0][0:65, c * 512:(c + 1) * 512],
                                           deps=[ad])
                                pv_evac[half] = dx
                    first = False
        for half in range(2):
            h_ = 2 * pr + half
            for c in range(NC_):
                dmal = p.dma("sync", l0_s[:], out_h[half][64:65, c * 512:(c + 1) * 512],
                             deps=[pv_evac[half]] + ([last_fin] if last_fin else []))
                rv = p.chain("vector", [
                    lambda e: e.reciprocal(linv_f[:], l0_s[:]),
                    lambda e: e.tensor_copy(linv[:], linv_f[:]),
                ], deps=[dmal] + ([last_fin] if last_fin else []))
                mmb = p.on("tensor",
                           lambda e: e.matmul(ps_bc[0:64, :], onesr128[0:1, 0:64],
                                              linv[0:1, :], start=True, stop=True),
                           deps=[rv, ld_o128] + ([last_fin] if last_fin else []))
                mv = p.on("vector",
                          (lambda half, c: lambda e: e.tensor_mul(
                              ao_sb[:], out_h[half][0:64, c * 512:(c + 1) * 512],
                              ps_bc[0:64, :]))(half, c),
                          deps=[mmb] + ([last_fin] if last_fin else []))
                st = p.dma("sync", ao_o[h_ * 64:(h_ + 1) * 64, c * 512:(c + 1) * 512],
                           ao_sb[:], deps=[mv])
                last_fin = st
    p.finalize_wait()
    return p


def build_attn2():
    p = Prog("A2")
    cc = const_inputs(p, ["ones_row128", "ones_row512"])
    xnf, ld_xnf = _load_xnf(p)
    xs_i = p.inp("xs", (T, D), F32)
    scp_i = p.inp("scp", (4, 128, 256), F32)
    wv_i = p.inp("wv", (D, 512), F32)
    bv_i = p.inp("bv", (1, 512), F32)
    wfc_i = p.inp("wfc", (512, D), F32)
    bfc_i = p.inp("bfc", (1, D), F32)
    gm_i = p.inp("gm", (1, D), F32)
    bt_i = p.inp("bt", (1, D), F32)
    s_o = p.outp("s", (T, D), F32)
    onesr128 = p.sb("onesr128b", (1, 128), BF16)
    p._onesr128 = onesr128
    ld_o128 = p.dma("sync", onesr128[:], cc["ones_row128"][:])
    onesr512 = p.sb("onesr512b", (1, 512), BF16)
    ld_o512 = p.dma("sync", onesr512[:], cc["ones_row512"][:])
    p.setup_group_end(512)

    scp_s = p.sb("scp_s", (128, 4 * 256), F32)
    sc = p.sb("sc", (128, 256), F32)
    attn_bf = p.sb("attn_bf", (128, 256), BF16)
    mx = p.sb("mx", (128, 1), F32)
    nmx = p.sb("nmx", (128, 1), F32)
    smc = p.sb("smc", (128, 1), F32)
    rsc = p.sb("rsc", (128, 1), F32)
    att_f = p.sb("att_f", (128, 64), F32)
    wv_s = p.sb("wv_s", (128, DC * 512), BF16)
    bv_s = p.sb("bv_s", (1, 512), BF16)
    wfc_s = p.sb("wfc_s", (128, 4 * D), BF16)
    bfc_s = p.sb("bfc_s", (1, D), BF16)
    gm_s = p.sb("gm_s", (1, D), BF16)
    bt_s = p.sb("bt_s", (1, D), BF16)
    gmb = p.sb("gmb", (128, D), F32)
    btb = p.sb("btb", (128, D), F32)
    vf = p.sb("vf", (128, 4 * T), BF16)
    of = p.sb("of", (128, 4 * T), BF16)
    xt_ = p.sb("xt_", (128, D), F32)
    xsq_ = p.sb("xsq_", (128, D), F32)
    xn_tm = p.sb("xn_tm", (128, D), F32)
    z_sb = p.sb("z_sb", (128, D), F32)
    s_sb = p.sb("s_sb", (128, D), F32)
    cA = p.sb("cA", (128, 1), F32)
    cB = p.sb("cB", (128, 1), F32)
    cC = p.sb("cC", (128, 1), F32)
    cD = p.sb("cD", (128, 1), F32)
    cE = p.sb("cE", (128, 1), F32)
    cF = p.sb("cF", (128, 1), F32)
    ps_a = [p.ps(f"ps_a{i}", (128, 512), F32) for i in range(2)]
    ps_b = p.ps("ps_b", (128, 512), F32)

    ld_sc = p.dma("sync", scp_s[:].rearrange("p (a b) -> p a b", a=4),
                  scp_i.rearrange("a p b -> p a b"))
    ld_wv = p.dma("gpsimd", *fm3(wv_s, wv_i, 512))
    ld_bv = p.dma("gpsimd", bv_s[:], bv_i[:])
    ld_wfc = p.dma("gpsimd", wfc_s[:].rearrange("p (c d) -> p c d", c=4),
                   wfc_i.rearrange("(c p) d -> p c d", p=128))
    ld_bfc = p.dma("gpsimd", bfc_s[:], bfc_i[:])
    ld_gm = p.dma("gpsimd", gm_s[:], gm_i[:])
    ld_bt = p.dma("gpsimd", bt_s[:], bt_i[:])

    gb = []
    for half in range(2):
        for (src_r, dstb) in ((gm_s, gmb), (bt_s, btb)):
            mm = p.on("tensor",
                      (lambda src_r, half: lambda e: e.matmul(
                          ps_b[:], onesr128[:], src_r[0:1, half * 512: half * 512 + 512],
                          start=True, stop=True))(src_r, half),
                      deps=[ld_gm, ld_bt, ld_o128] + gb[-1:])
            ev = p.on("vector",
                      (lambda dstb, half: lambda e: e.tensor_copy(
                          dstb[:, half * 512: half * 512 + 512], ps_b[:]))(dstb, half),
                      deps=[mm])
            gb.append(ev)
    gb_done = gb[-1]

    vsum = p.chain("vector", [
        lambda e: e.tensor_add(sc[:], scp_s[:, 0:256], scp_s[:, 256:512]),
        lambda e: e.tensor_add(sc[:], sc[:], scp_s[:, 512:768]),
        lambda e: e.tensor_add(sc[:], sc[:], scp_s[:, 768:1024]),
    ], deps=[ld_sc])
    at_done = []
    for qd in range(4):
        sl = slice(qd * 64, qd * 64 + 64)
        vm = p.chain("vector", [
            (lambda sl: lambda e: e.reduce_max(mx[:], sc[:, sl], axis=AX.X))(sl),
            lambda e: e.tensor_scalar_mul(nmx[:], mx[:], -0.125),
        ], deps=[vsum] + at_done[-1:])
        se = p.on("scalar",
                  (lambda sl: lambda e: e.activation(att_f[:], sc[:, sl], AF.Exp,
                                                     bias=nmx[:], scale=0.125,
                                                     accum_out=smc[:]))(sl),
                  deps=[vm])
        vn = p.chain("vector", [
            lambda e: e.reciprocal(rsc[:], smc[:]),
            (lambda sl: lambda e: e.tensor_scalar_mul(att_f[:], att_f[:], rsc[:]))(sl),
            (lambda sl: lambda e: e.tensor_copy(attn_bf[:, sl], att_f[:]))(sl),
        ], deps=[se])
        at_done.append(vn)
    at_done = at_done[-1]

    vev = None
    for vc in range(4):
        for tc in range(4):
            def vmm(vc, tc):
                def f(e_):
                    for dc in range(DC):
                        e_.matmul(ps_a[0][:],
                                  wv_s[:, dc * 512 + vc * 128: dc * 512 + (vc + 1) * 128],
                                  xnf[:, dc * T + tc * 512: dc * T + (tc + 1) * 512],
                                  start=(dc == 0), stop=False)
                    return e_.matmul(ps_a[0][:], bv_s[0:1, vc * 128:(vc + 1) * 128],
                                     onesr512[:], start=False, stop=True)
                return f
            mm = p.on("tensor", vmm(vc, tc), deps=[ld_wv, ld_bv, ld_xnf, ld_o512] +
                      ([vev] if vev else []))
            vev = p.on("scalar",
                       (lambda vc, tc: lambda e: e.copy(
                           vf[:, vc * T + tc * 512: vc * T + (tc + 1) * 512],
                           ps_a[0][:]))(vc, tc),
                       deps=[mm])

    oev = None
    for ec in range(4):
        for tc in range(4):
            def omm(ec, tc, half):
                def f(e_):
                    return e_.matmul(ps_a[1][half * 64:half * 64 + 64, :],
                                     attn_bf[half * 64:half * 64 + 64,
                                             ec * 64:(ec + 1) * 64],
                                     vf[half * 64:half * 64 + 64,
                                        ec * T + tc * 512: ec * T + (tc + 1) * 512],
                                     start=True, stop=True,
                                     tile_position=(half * 64, half * 64))
                return f
            mms = []
            for half in range(2):
                mms.append(p.on("tensor", omm(ec, tc, half),
                                deps=[at_done, vev] + ([oev] if oev else [])))
            oev = p.on("scalar",
                       (lambda ec, tc: lambda e: e.copy(
                           of[:, ec * T + tc * 512: ec * T + (tc + 1) * 512],
                           ps_a[1][:]))(ec, tc),
                       deps=mms)

    prev = None
    for t in range(NT):
        ldx = p.dma("sync", xt_[:], xs_i[t * 128:(t + 1) * 128, :],
                    deps=[prev] if prev else [])
        s_sq = p.on("scalar", lambda e: e.activation(xsq_[:], xt_[:], AF.Square),
                    deps=[ldx] + ([prev] if prev else []))
        v_st = p.chain("vector", [
            lambda e: e.reduce_sum(cA[:], xt_[:], axis=AX.X),
            lambda e: e.reduce_sum(cB[:], xsq_[:], axis=AX.X),
            lambda e: e.tensor_scalar_mul(cC[:], cA[:], 1.0 / D),
            lambda e: e.tensor_scalar_mul(cD[:], cB[:], 1.0 / D),
            lambda e: e.tensor_mul(cB[:], cC[:], cC[:]),
            lambda e: e.tensor_sub(cD[:], cD[:], cB[:]),
            lambda e: e.tensor_scalar_add(cD[:], cD[:], LN_EPS),
        ], deps=[ldx, s_sq])
        s_sq2 = p.on("scalar", lambda e: e.activation(cF[:], cD[:], AF.Sqrt), deps=[v_st])
        v_n = p.chain("vector", [
            lambda e: e.reciprocal(cE[:], cF[:]),
            lambda e: e.tensor_scalar(xn_tm[:], xt_[:], cC[:], cE[:], ALU.subtract, ALU.mult),
        ], deps=[s_sq2])
        zdeps = []
        for half in range(2):
            def pmm(t, half):
                def f(e_):
                    for ec in range(4):
                        e_.matmul(ps_a[half][:],
                                  of[:, ec * T + t * 128: ec * T + (t + 1) * 128],
                                  wfc_s[:, ec * D + half * 512: ec * D + half * 512 + 512],
                                  start=(ec == 0), stop=False)
                    return e_.matmul(ps_a[half][:], onesr128[:],
                                     bfc_s[0:1, half * 512: half * 512 + 512],
                                     start=False, stop=True)
                return f
            mm = p.on("tensor", pmm(t, half),
                      deps=[oev, ld_wfc, ld_bfc] + ([prev] if prev else []))
            ad = p.on("vector",
                      (lambda t, half: lambda e: e.tensor_add(
                          z_sb[:, half * 512: half * 512 + 512], ps_a[half][:],
                          xn_tm[:, half * 512: half * 512 + 512]))(t, half),
                      deps=[mm, v_n])
            zdeps.append(ad)
        s_sq3 = p.on("scalar", lambda e: e.activation(xsq_[:], z_sb[:], AF.Square),
                     deps=zdeps)
        v_st2 = p.chain("vector", [
            lambda e: e.reduce_sum(cA[:], z_sb[:], axis=AX.X),
            lambda e: e.reduce_sum(cB[:], xsq_[:], axis=AX.X),
            lambda e: e.tensor_scalar_mul(cC[:], cA[:], 1.0 / D),
            lambda e: e.tensor_scalar_mul(cD[:], cB[:], 1.0 / D),
            lambda e: e.tensor_mul(cB[:], cC[:], cC[:]),
            lambda e: e.tensor_sub(cD[:], cD[:], cB[:]),
            lambda e: e.tensor_scalar_add(cD[:], cD[:], 1e-6),
        ], deps=zdeps + [s_sq3])
        s_sq4 = p.on("scalar", lambda e: e.activation(cF[:], cD[:], AF.Sqrt), deps=[v_st2])
        v_fin = p.chain("vector", [
            lambda e: e.reciprocal(cE[:], cF[:]),
            lambda e: e.tensor_scalar(s_sb[:], z_sb[:], cC[:], cE[:], ALU.subtract, ALU.mult),
            lambda e: e.tensor_mul(s_sb[:], s_sb[:], gmb[:]),
            lambda e: e.tensor_add(s_sb[:], s_sb[:], btb[:]),
        ], deps=[s_sq4, gb_done])
        prev = p.dma("sync", s_o[t * 128:(t + 1) * 128, :], s_sb[:], deps=[v_fin])
    p.finalize_wait()
    return p



def build_resid(layer):
    p = Prog(f"R{layer}")
    xs_i = p.inp("xs", (T, D), F32)
    g_i = p.inp("g", (T, D), F32)
    xo_o = p.outp("xo", (T, D), F32)
    if layer == 2:
        s_i = p.inp("s", (T, D), F32)
    else:
        cc = const_inputs(p, ["ones_row128"])
        aof_i = p.inp("aof", (DC, 128, T), BF16)
        wo_i = p.inp("wo", (D, D), F32)
        onesr128 = p.sb("onesr128b", (1, 128), BF16)
        p._onesr128 = onesr128
        ld_o128 = p.dma("sync", onesr128[:], cc["ones_row128"][:])
        p.setup_group_end(512)
        if layer == 0:
            bo_i = p.inp("bo", (1, D), F32)
            bo_s = p.sb("bo_s", (1, D), BF16)
            ld_bo = p.dma("gpsimd", bo_s[:], bo_i[:])
        aof = p.sb("aof_s", (128, DC * T), BF16)
        wo_s = p.sb("wo_s", (128, DC * D), BF16)
        ld_ao = p.dma("sync", aof[:].rearrange("p (c t) -> p c t", c=DC),
                      aof_i.rearrange("c p t -> p c t"))
        ld_wo = p.dma("gpsimd", *fm3(wo_s, wo_i, D))
        ps_s = [p.ps(f"ps_s{i}", (128, 512), F32) for i in range(2)]
    xt_ = p.sb("xt_", (128, D), F32)
    gt_ = p.sb("gt_", (128, D), F32)
    st_ = p.sb("st_", (128, D), F32)
    xo_sb = p.sb("xo_sb", (128, D), F32)
    prev = None
    for t in range(NT):
        dl = [prev] if prev else []
        ldx = p.dma("sync", xt_[:], xs_i[t * 128:(t + 1) * 128, :], deps=dl)
        ldg = p.dma("sync", gt_[:], g_i[t * 128:(t + 1) * 128, :], deps=dl)
        if layer == 2:
            lds_ = p.dma("sync", st_[:], s_i[t * 128:(t + 1) * 128, :], deps=dl)
            v_fin = p.chain("vector", [
                lambda e: e.tensor_mul(st_[:], st_[:], gt_[:]),
                lambda e: e.tensor_add(xo_sb[:], xt_[:], st_[:]),
            ], deps=[ldx, ldg, lds_])
        else:
            vparts = []
            for half in range(2):
                def pmm(t, half):
                    def f(e_):
                        for fc in range(DC):
                            e_.matmul(ps_s[half][:],
                                      aof[:, fc * T + t * 128: fc * T + (t + 1) * 128],
                                      wo_s[:, fc * D + half * 512: fc * D + half * 512 + 512],
                                      start=(fc == 0), stop=False)
                        if layer == 0:
                            return e_.matmul(ps_s[half][:], onesr128[:],
                                             bo_s[0:1, half * 512: half * 512 + 512],
                                             start=False, stop=True)
                        return p.end_group(e_, ps_s[half][:], 128, 512)
                    return f
                deps = [ld_ao, ld_wo] + p._ge_dep + dl
                if layer == 0:
                    deps.append(ld_bo)
                mm = p.on("tensor", pmm(t, half), deps=deps)
                vch = p.chain("vector", [
                    (lambda t, half: lambda e: e.tensor_mul(
                        st_[:, half * 512: half * 512 + 512], ps_s[half][:],
                        gt_[:, half * 512: half * 512 + 512]))(t, half),
                    (lambda t, half: lambda e: e.tensor_add(
                        xo_sb[:, half * 512: half * 512 + 512],
                        xt_[:, half * 512: half * 512 + 512],
                        st_[:, half * 512: half * 512 + 512]))(t, half),
                ], deps=[mm, ldx, ldg])
                vparts.append(vch)
            v_fin = vparts[-1]
        prev = p.dma("sync", xo_o[t * 128:(t + 1) * 128, :], xo_sb[:], deps=[v_fin] +
                     (vparts if layer != 2 else []))
    p.finalize_wait()
    return p


_NC_CACHE = {}

def _get_nc(name, builder):
    if name not in _NC_CACHE:
        _NC_CACHE[name] = builder().build()
    return _NC_CACHE[name]


def _run(nc, in_maps):
    res = run_bass_kernel_spmd(nc, in_maps, core_ids=list(range(NCORES)))
    return res.results


def kernel(**inp):
    f32 = np.float32
    TT = NCORES * T
    x = np.ascontiguousarray(inp["L0_attn_x"].astype(f32, copy=False).reshape(TT, D))
    cN = lambda *names: const_in_map(list(names))
    CONST_S = cN("ident", "iota8", "iota8p", "ones_col", "ones_row128")
    zeros8 = np.zeros((1, E), f32)
    sL = {0: dict(wr=np.ascontiguousarray(inp["L0_ffn_Wr"]), gbias=zeros8),
          1: dict(wr=np.ascontiguousarray(inp["L1_ffn_Wr"]),
                  gbias=np.ascontiguousarray(inp["L1_ffn_expert_bias"].reshape(1, E))),
          2: dict(wq=np.ascontiguousarray(inp["L2_attn_w_qkv"][:, 0:512]),
                  wk=np.ascontiguousarray(inp["L2_attn_w_qkv"][:, 512:1024]),
                  bq=np.ascontiguousarray(inp["L2_attn_b_qkv"][0:512].reshape(1, 512)),
                  bk=np.ascontiguousarray(inp["L2_attn_b_qkv"][512:1024].reshape(1, 512))),
          3: dict(wr=np.ascontiguousarray(inp["L3_ffn_Wr"]),
                  gbias=np.ascontiguousarray(inp["L3_ffn_br"].reshape(1, E)))}

    for layer in range(4):
        nc = _get_nc(f"S{layer}", lambda: build_stats(layer))
        ims = [{"xs": x[c * T:(c + 1) * T], **CONST_S, **sL[layer]} for c in range(NCORES)]
        rs = _run(nc, ims)
        xnf = [np.asarray(r["xnf"]) for r in rs]
        xnb = [np.concatenate([xnf[b * 4 + j] for j in range(4)], axis=2) for b in range(2)]

        if layer == 0:
            ncf = _get_nc("F0", lambda: build_ffn0())
            tots = np.stack([np.asarray(rs[c]["tot"])[0] for c in range(NCORES)])
            totst = np.ascontiguousarray(tots.T)
            imf = []
            for c in range(NCORES):
                mb = np.zeros((16, NCORES), f32)
                mb[:, :c] = 1.0
                imf.append({"xnf": xnf[c], "w1": inp["L0_ffn_W1"], "b1": inp["L0_ffn_b1"],
                            "w2": inp["L0_ffn_W2"], "b2": inp["L0_ffn_b2"],
                            "oh": np.asarray(rs[c]["oh"]), "tv": np.asarray(rs[c]["tv"]),
                            "totst": totst, "maskb": mb,
                            **cN("ones_row512", "ones_row128", "ltri_s", "ones_col")})
            g = [np.asarray(r["g"]) for r in _run(ncf, imf)]
        elif layer == 1:
            ncf = _get_nc("F1", lambda: build_ffn1())
            imf = [{"xnf": xnf[c], "wg": inp["L1_ffn_Wg"], "wu": inp["L1_ffn_Wu"],
                    "wd": inp["L1_ffn_Wd"], "sg": inp["L1_ffn_Sg"], "su": inp["L1_ffn_Su"],
                    "sd": inp["L1_ffn_Sd"], "rw": np.asarray(rs[c]["rw"]),
                    **cN("ones_row512", "ones_row128", "ones_col_f32")}
                   for c in range(NCORES)]
            g = [np.asarray(r["g"]) for r in _run(ncf, imf)]
        elif layer == 2:
            ncf = _get_nc("F2", lambda: build_ffn2())
            sc_ = lambda v: np.ascontiguousarray(v.reshape(-1, 128).T)
            w2f = dict(w1t=np.ascontiguousarray(inp["L2_ffn_w1_w"].T),
                       w2t=np.ascontiguousarray(inp["L2_ffn_w2_w"].T),
                       h1t=np.ascontiguousarray(inp["L2_ffn_h1_w"].T),
                       h2t=np.ascontiguousarray(inp["L2_ffn_h2_w"].T),
                       w3t=np.ascontiguousarray(inp["L2_ffn_w3_w"].T),
                       s1=sc_(inp["L2_ffn_w1_s"]), s2=sc_(inp["L2_ffn_w2_s"]),
                       h1s=sc_(inp["L2_ffn_h1_s"]), h2s=sc_(inp["L2_ffn_h2_s"]),
                       w3s=sc_(inp["L2_ffn_w3_s"]))
            imf = [{"xnf": xnf[c], **w2f, **cN("ident", "ones_row128")}
                   for c in range(NCORES)]
            g = [np.asarray(r["g"]) for r in _run(ncf, imf)]
        else:
            ncf = _get_nc("F3", lambda: build_ffn3())
            gi = np.concatenate([np.asarray(rs[c]["gi"]) for c in range(NCORES)])
            arr = lambda v: np.ascontiguousarray(v.reshape(128, 128)).astype(f32)
            gparts = dict(g1a=arr(gi[:, 0]), g2a=arr(gi[:, 1]),
                          i1a=arr(gi[:, 2]), i2a=arr(gi[:, 3]))
            imf = [{"xnf": xnf[c], "w1": inp["L3_ffn_W1"], "b1": inp["L3_ffn_b1"],
                    "w2": inp["L3_ffn_W2"], "b2": inp["L3_ffn_b2"],
                    **gparts, "gio": np.ascontiguousarray(gi[c * T:(c + 1) * T]),
                    **cN("ones_row512", "ones_row128", "ones_col")}
                   for c in range(NCORES)]
            g = [np.asarray(r["g"]) for r in _run(ncf, imf)]

        if layer != 2:
            nca = _get_nc(f"A{layer}", lambda: build_attn(layer))
            ima = []
            for c in range(NCORES):
                b, g_ = c // 4, c % 4
                m = {"xnb": xnb[b]}
                if layer == 0:
                    m.update(wq=np.ascontiguousarray(inp["L0_attn_Wq"][:, g_ * 256:(g_ + 1) * 256]),
                             wk=np.ascontiguousarray(np.concatenate(
                                 [inp["L0_attn_Wk"][:, g_ * 64:(g_ + 1) * 64]] * 2, 1)),
                             wv=np.ascontiguousarray(inp["L0_attn_Wv"][:, g_ * 64:(g_ + 1) * 64]),
                             bq=np.ascontiguousarray(inp["L0_attn_bq"][g_ * 256:(g_ + 1) * 256].reshape(1, 256)),
                             bk=np.ascontiguousarray(np.concatenate(
                                 [inp["L0_attn_bk"][g_ * 64:(g_ + 1) * 64]] * 2).reshape(1, 128)),
                             bv=np.ascontiguousarray(inp["L0_attn_bv"][g_ * 64:(g_ + 1) * 64].reshape(1, 64)))
                    m.update(cN("ones_row128", "ones_row512", "masks", "selT128",
                                "sel128x2", "pickA", "pickB", "rope_cos2", "rope_sin2", "ropeRT2"))
                elif layer == 1:
                    tr = lambda W: np.ascontiguousarray(np.transpose(W, (2, 0, 1)).reshape(D, -1))
                    m.update(wq=tr(inp["L1_attn_W_Q"][4 * g_:4 * g_ + 4]),
                             wk=tr(inp["L1_attn_W_K"][4 * g_:4 * g_ + 2]),
                             wk2=tr(inp["L1_attn_W_K"][4 * g_ + 2:4 * g_ + 4]),
                             wv=tr(inp["L1_attn_W_V"][4 * g_:4 * g_ + 4]))
                    m.update(cN("ones_row128", "ones_row512", "masks", "selT128",
                                "sel128x2", "pickA", "pickB"))
                else:
                    nk = inp["L3_attn_null_kv"].reshape(1, 64)
                    m.update(wq=np.ascontiguousarray(inp["L3_attn_Wq"][:, g_ * 256:(g_ + 1) * 256]),
                             wk=np.ascontiguousarray(np.concatenate([inp["L3_attn_Wkv"]] * 2, 1)),
                             wv=np.ascontiguousarray(inp["L3_attn_Wkv"]),
                             nullk=np.ascontiguousarray(nk),
                             nullkd=np.ascontiguousarray(np.concatenate([nk.T] * 2, 0)))
                    m.update(cN("ones_row128", "ones_row512", "masks3", "selT128",
                                "sel128x2", "pickA", "pickB", "mask3_last"))
                ima.append(m)
            ra = _run(nca, ima)
            ao = [np.asarray(r["ao"]) for r in ra]
            AO = [np.concatenate([ao[b * 4 + j] for j in range(4)], axis=0) for b in range(2)]
            ncr = _get_nc(f"R{layer}", lambda: build_resid(layer))
            wo = {0: inp["L0_attn_Wo"], 1: np.ascontiguousarray(inp["L1_attn_W_O"].T),
                  3: inp["L3_attn_Wout"]}[layer]
            imr = []
            for c in range(NCORES):
                b = c // 4
                aof = np.ascontiguousarray(
                    AO[b][:, (c % 4) * T:((c % 4) + 1) * T]).reshape(DC, 128, T)
                m = {"xs": x[c * T:(c + 1) * T], "g": g[c], "aof": aof, "wo": wo,
                     **cN("ones_row128")}
                if layer == 0:
                    m["bo"] = np.ascontiguousarray(inp["L0_attn_bo"].reshape(1, D))
                imr.append(m)
            xo = [np.asarray(r["xo"]) for r in _run(ncr, imr)]
        else:
            nca = _get_nc("A2f", lambda: build_attn2())
            scps = [np.asarray(rs[c]["scp"]) for c in range(NCORES)]
            scpg = [np.stack([scps[b * 4 + j] for j in range(4)]) for b in range(2)]
            ima = [{"xnf": xnf[c], "xs": x[c * T:(c + 1) * T], "scp": scpg[c // 4],
                    "wv": np.ascontiguousarray(inp["L2_attn_w_qkv"][:, 1024:1536]),
                    "bv": np.ascontiguousarray(inp["L2_attn_b_qkv"][1024:1536].reshape(1, 512)),
                    "wfc": inp["L2_attn_w_fc"],
                    "bfc": np.ascontiguousarray(inp["L2_attn_b_fc"].reshape(1, D)),
                    "gm": np.ascontiguousarray(inp["L2_attn_gamma"].reshape(1, D)),
                    "bt": np.ascontiguousarray(inp["L2_attn_beta"].reshape(1, D)),
                    **cN("ones_row128", "ones_row512")}
                   for c in range(NCORES)]
            s_ = [np.asarray(r["s"]) for r in _run(nca, ima)]
            ncr = _get_nc("R2", lambda: build_resid(2))
            imr = [{"xs": x[c * T:(c + 1) * T], "g": g[c], "s": s_[c]}
                   for c in range(NCORES)]
            xo = [np.asarray(r["xo"]) for r in _run(ncr, imr)]
        x = np.ascontiguousarray(np.concatenate(xo, axis=0))

    return x.reshape(2, S, D)


# version 03 (final)
# speedup vs baseline: 1.1341x; 1.1341x over previous
import numpy as np
import ml_dtypes
from contextlib import ExitStack

import concourse.bass as bass
import concourse.mybir as mybir
from concourse.bass_utils import run_bass_kernel_spmd

F32 = mybir.dt.float32
BF16 = mybir.dt.bfloat16
AF = mybir.ActivationFunctionType
ALU = mybir.AluOpType
AX = mybir.AxisListType
BF = ml_dtypes.bfloat16

NCORES = 8
D = 1024
DC = D // 128
T = 2048
NT = T // 128
S = 8192
TQC = 512
E = 8
LN_EPS = 1e-5


def _bf(x):
    return np.asarray(x, dtype=BF)

_CONSTS = {}

def host_consts():
    if _CONSTS:
        return _CONSTS
    c = _CONSTS
    c["ident"] = _bf(np.eye(128))
    c["ones_row128"] = _bf(np.ones((1, 128)))
    c["ones_row512"] = _bf(np.ones((1, 512)))
    c["ones_col"] = _bf(np.ones((128, 1)))
    c["ones_row128_f32"] = np.ones((1, 128), dtype=np.float32)
    c["ones_col_f32"] = np.ones((128, 1), dtype=np.float32)
    iota = np.arange(8, dtype=np.float32)[None, :]
    c["iota8"] = np.tile(iota, (128, 1))
    c["iota8p"] = np.tile(iota + 1e4, (128, 1))
    c["ltri_s"] = _bf(np.triu(np.ones((128, 128), np.float32), 1))
    tk = np.arange(128)[:, None]
    tq = np.arange(TQC)[None, :]
    c["masks"] = _bf(np.stack([((tq - tk) >= 128 * r).astype(np.float32) for r in range(4)]))
    c["masks3"] = _bf(np.stack([((tq - tk) >= 128 * r - 1).astype(np.float32) for r in range(4)]))
    m3last = np.zeros((128, TQC), np.float32)
    m3last[0, TQC - 1] = 1.0
    c["mask3_last"] = _bf(m3last)
    d_h = 64
    inv_freq = 1.0 / (10000.0 ** (np.arange(0, d_h, 2, dtype=np.float32) / d_h))
    pos = np.arange(S, dtype=np.float32)
    ang = pos[:, None] * inv_freq[None, :]
    cos = np.concatenate([np.cos(ang), np.cos(ang)], axis=1).T
    sin = np.concatenate([np.sin(ang), np.sin(ang)], axis=1).T
    c["rope_cos"] = _bf(np.ascontiguousarray(cos))
    c["rope_sin"] = _bf(np.ascontiguousarray(sin))
    c["rope_cos2"] = _bf(np.ascontiguousarray(np.concatenate([cos, cos], 0)))
    c["rope_sin2"] = _bf(np.ascontiguousarray(np.concatenate([sin, sin], 0)))
    R = np.zeros((64, 64), np.float32)
    for i in range(32):
        R[i, 32 + i] = -1.0
        R[32 + i, i] = 1.0
    c["ropeRT"] = _bf(np.ascontiguousarray(R.T))
    RT2 = np.zeros((128, 128), np.float32)
    RT2[0:64, 0:64] = R.T
    RT2[64:128, 64:128] = R.T
    c["ropeRT2"] = _bf(RT2)
    sel = np.zeros((2, 128), np.float32)
    sel[0, 0:64] = 1.0
    sel[1, 64:128] = 1.0
    c["selT128"] = _bf(sel)
    c["sel128x2"] = _bf(np.ascontiguousarray(sel.T))
    pk = np.zeros((2, 2, 128), np.float32)
    pk[0, 0, :] = 1.0
    pk[1, 1, :] = 1.0
    c["pickA"] = _bf(pk[0])
    c["pickB"] = _bf(pk[1])
    return c


ENGINES = ("sync", "gpsimd", "tensor", "vector", "scalar")


class _Step:
    __slots__ = ("eng", "fn", "deps", "k", "cum")

    def __init__(self, eng, fn, deps, k):
        self.eng = eng
        self.fn = fn
        self.deps = deps
        self.k = k
        self.cum = None


class Prog:

    def __init__(self, name):
        self.nc = bass.Bass("TRN2", target_bir_lowering=False,
                            detect_race_conditions=False)
        self.name = name
        self.es = ExitStack()
        self.ap = {}
        self.steps = {e: [] for e in ENGINES}
        self._built = False

    def inp(self, name, shape, dt):
        self.ap[name] = self.nc.dram_tensor(name, list(shape), dt, kind="ExternalInput").ap()
        return self.ap[name]

    def outp(self, name, shape, dt):
        self.ap[name] = self.nc.dram_tensor(name, list(shape), dt, kind="ExternalOutput").ap()
        return self.ap[name]

    def dram(self, name, shape, dt, **kw):
        self.ap[name] = self.nc.dram_tensor(name, list(shape), dt, kind="Internal", **kw).ap()
        return self.ap[name]

    def sb(self, name, shape, dt):
        return self.es.enter_context(self.nc.sbuf_tensor(name, list(shape), dt))

    def ps(self, name, shape, dt=F32):
        return self.es.enter_context(self.nc.psum_tensor(name, list(shape), dt))

    def on(self, eng, fn, deps=(), dma=False):
        st = _Step(eng, fn, [d for d in deps if d is not None], 16 if dma else 1)
        self.steps[eng].append(st)
        return st

    def chain(self, eng, fns, deps=()):
        prev = None
        for fn in fns:
            prev = self.on(eng, fn, deps=(list(deps) + [prev]) if prev else deps)
        return prev

    def dma(self, eng, out, in_, deps=()):
        return self.on(eng, (lambda o, i: (lambda e: e.dma_start(out=o, in_=i)))(out, in_),
                       deps=deps, dma=True)

    def build(self):
        assert not self._built
        self._built = True
        sems = {}
        for e in ENGINES:
            kinds = {st.k for st in self.steps[e]}
            for k in kinds:
                tag = "d" if k == 16 else "c"
                sems[(e, tag)] = self.es.enter_context(self.nc.semaphore(f"sem_{e}_{tag}"))
        cum = {}
        for e in ENGINES:
            for st in self.steps[e]:
                tag = "d" if st.k == 16 else "c"
                cum[(e, tag)] = cum.get((e, tag), 0) + st.k
                st.cum = cum[(e, tag)]
        blk = self.es.enter_context(self.nc.Block())

        def make_body(e):
            def body(eng):
                prev_dma = 0
                seen = {}
                for st in self.steps[e]:
                    tag = "d" if st.k == 16 else "c"
                    need = {}
                    for d in st.deps:
                        dtag = "d" if d.k == 16 else "c"
                        if d.eng == e and dtag == tag and d.k == 16:
                            continue
                        key = (d.eng, dtag)
                        if d.cum > need.get(key, 0):
                            need[key] = d.cum
                    if st.k == 16 and prev_dma:
                        if prev_dma > need.get((e, "d"), 0):
                            need[(e, "d")] = prev_dma
                    for key, c_ in need.items():
                        if c_ <= seen.get(key, 0):
                            continue
                        eng.wait_ge(sems[key], c_)
                        seen[key] = c_
                    if st.k == 16:
                        prev_dma = st.cum
                    inst = st.fn(eng)
                    if inst is None:
                        inst = eng.nop()
                    inst.then_inc(sems[(e, "d" if st.k == 16 else "c")], st.k)
            return body

        for e in ENGINES:
            if not self.steps[e]:
                continue
            getattr(blk, e)(make_body(e))
        self.es.close()
        return self.nc

    def setup_group_end(self, width=512):
        self._ones_row = self.sb("_gones", (1, 128), BF16)
        self._zrow = self.sb("_gzero", (1, width), BF16)
        l1 = self.dma("sync", self._ones_row[:], self.inp("_c_ones128", (1, 128), BF16)[:])
        z = self.on("vector", lambda e: e.memset(self._zrow[:], 0.0))
        self._ge_dep = [l1, z]
        return self._ge_dep

    def end_group(self, e, ps_ap, m, n):
        return e.matmul(ps_ap, self._ones_row[0:1, 0:m], self._zrow[0:1, 0:n],
                        start=False, stop=True)

    def finalize_wait(self, waiter="gpsimd"):
        deps = [self.steps[e][-1] for e in ENGINES if self.steps[e] and e != waiter]
        self.on(waiter, lambda e: None, deps=deps)


def const_inputs(p, names):
    c = host_consts()
    out = {}
    for n in names:
        arr = c[n]
        dt = BF16 if arr.dtype == BF else F32
        out[n] = p.inp("c_" + n, arr.shape, dt)
    return out


def const_in_map(names):
    c = host_consts()
    m = {"c_" + n: np.asarray(c[n]) for n in names}
    m["_c_ones128"] = np.asarray(c["ones_row128"])
    return m


def fm3(dst_sb, src_ap, w, c=DC):
    return dst_sb[:].rearrange("p (c e) -> p c e", c=c), src_ap.rearrange("(c p) e -> p c e", p=128)


def build_stats(layer, phases=3, ntb=NT):
    p = Prog(f"S{layer}")
    xs = p.inp("xs", (T, D), F32)
    cc = const_inputs(p, ["ident", "iota8", "iota8p", "ones_col", "ones_row128"])
    xnf_o = p.outp("xnf", (DC, 128, T), BF16)
    gate = layer in (0, 1, 3) and phases >= 2
    if gate:
        wr_i = p.inp("wr", (D, E), F32)
    if layer in (0, 1, 3) and gate:
        bias_i = p.inp("gbias", (1, E), F32)
    if phases < 2:
        pass
    elif layer == 0:
        oh_o = p.outp("oh", (T, 16), F32)
        tv_o = p.outp("tv", (T, 2), F32)
        tot_o = p.outp("tot", (1, 16), F32)
    elif layer == 1:
        rw_o = p.outp("rw", (T, E), F32)
    elif layer == 2:
        wq_i = p.inp("wq", (D, 512), F32)
        wk_i = p.inp("wk", (D, 512), F32)
        bq_i = p.inp("bq", (1, 512), F32)
        bk_i = p.inp("bk", (1, 512), F32)
        scp_o = p.outp("scp", (128, 256), F32)
    elif layer == 3:
        gi_o = p.outp("gi", (T, 4), F32)
    if phases < 2 and layer == 2:
        pass

    xt = [p.sb(f"xt{i}", (128, D), F32) for i in range(2)]
    xsq = [p.sb(f"xsq{i}", (128, D), F32) for i in range(2)]
    xnb = [p.sb(f"xnb{i}", (128, D), BF16) for i in range(2)]
    xnf = p.sb("xnf_s", (128, DC * T), BF16)
    id_s = p.sb("id_s", (128, 128), BF16)
    onesrb = p.sb("onesrb", (1, 128), BF16)
    colA = p.sb("colA", (128, 1), F32)
    colB = p.sb("colB", (128, 1), F32)
    colC = p.sb("colC", (128, 1), F32)
    colD = p.sb("colD", (128, 1), F32)
    colF = p.sb("colF", (128, 1), F32)
    colE = p.sb("colE", (128, 1), F32)
    ps_tr = [p.ps(f"ps_tr{i}", (128, 128), BF16) for i in range(2)]

    onescb = p.sb("onescb", (128, 1), BF16)
    ld_id = p.dma("sync", id_s[:], cc["ident"][:])
    ld_oc = p.dma("sync", onescb[:], cc["ones_col"][:])
    ld_or = p.dma("sync", onesrb[:], cc["ones_row128"][:])
    if gate:
        wr_s = p.sb("wr_s", (128, DC * E), BF16)
        iota_s = p.sb("iota_s", (128, 8), F32)
        iotap_s = p.sb("iotap_s", (128, 8), F32)
        ld_wr = p.dma("gpsimd", *fm3(wr_s, wr_i, E))
        ld_i1 = p.dma("sync", iota_s[:], cc["iota8"][:])
        ld_i2 = p.dma("sync", iotap_s[:], cc["iota8p"][:])
        gbias_s = p.sb("gbias_s", (1, E), BF16)
        ld_gb = p.dma("gpsimd", gbias_s[:], bias_i[:])
    if layer == 2:
        wq_s = p.sb("wq_s", (128, DC * 512), BF16)
        wk_s = p.sb("wk_s", (128, DC * 512), BF16)
        bq_s = p.sb("bq_s", (1, 512), BF16)
        bk_s = p.sb("bk_s", (1, 512), BF16)
        ld_wq = p.dma("gpsimd", *fm3(wq_s, wq_i, 512))
        ld_wk = p.dma("gpsimd", *fm3(wk_s, wk_i, 512))
        ld_bq = p.dma("gpsimd", bq_s[:], bq_i[:])
        ld_bk = p.dma("gpsimd", bk_s[:], bk_i[:])

    ev = {}
    last_norm = {}
    last_sq = {}
    for t in range(NT):
        b = t % 2
        dep_ld = []
        if t >= 2:
            dep_ld = [last_norm[t - 2], last_sq[t - 2]]
        s_load = p.dma("sync", xt[b][:], xs[t * 128:(t + 1) * 128, :], deps=dep_ld)
        s_sq = p.on("scalar", (lambda b: lambda e: e.activation(xsq[b][:], xt[b][:], AF.Square))(b),
                    deps=[s_load] + ([last_sq[t - 2]] if t >= 2 else []))
        last_sq[t] = s_sq

        v_stats = p.chain("vector", [
            (lambda b: lambda e: e.reduce_sum(colA[:], xt[b][:], axis=AX.X))(b),
            (lambda b: lambda e: e.reduce_sum(colB[:], xsq[b][:], axis=AX.X))(b),
            lambda e: e.tensor_scalar_mul(colC[:], colA[:], 1.0 / D),
            lambda e: e.tensor_scalar_mul(colD[:], colB[:], 1.0 / D),
            lambda e: e.tensor_mul(colB[:], colC[:], colC[:]),
            lambda e: e.tensor_sub(colD[:], colD[:], colB[:]),
            lambda e: e.tensor_scalar_add(colD[:], colD[:], LN_EPS),
        ], deps=[s_load, s_sq])
        s_sqrt = p.on("scalar", lambda e: e.activation(colF[:], colD[:], AF.Sqrt),
                      deps=[v_stats])
        v_norm = p.chain("vector", [
            lambda e: e.reciprocal(colE[:], colF[:]),
            (lambda b: lambda e: e.tensor_scalar(xnb[b][:], xt[b][:], colC[:], colE[:],
                                                 ALU.subtract, ALU.mult))(b),
        ], deps=[s_sqrt])
        last_norm[t] = v_norm
        for dc in range(DC):
            j = (t * DC + dc) % 2
            deps = [v_norm, ld_id]
            prev = ev.get((t * DC + dc) - 2)
            if prev is not None:
                deps.append(prev)
            tr = p.on("tensor",
                      (lambda b, dc, j: lambda e: e.transpose(
                          ps_tr[j][:], xnb[b][:, dc * 128:(dc + 1) * 128], id_s[:]))(b, dc, j),
                      deps=deps)
            ev[t * DC + dc] = p.on("scalar",
                                   (lambda t, dc, j: lambda e: e.copy(
                                       xnf[:, dc * T + t * 128: dc * T + (t + 1) * 128],
                                       ps_tr[j][:]))(t, dc, j),
                                   deps=[tr])

    st_xnf = []
    for dc in range(DC):
        st_xnf.append(p.dma("sync", xnf_o[dc, :, :], xnf[:, dc * T:(dc + 1) * T],
                            deps=[ev[(NT - 1) * DC + DC - 1]]))

    if gate and phases >= 2:
        ps_lg = p.ps("ps_lg", (128, 8), F32)
        probs = p.sb("probs", (128, 8), F32)
        base2 = p.sb("base2", (128, 8), F32)
        ohm = p.sb("ohm", (128, 8), F32)
        cand = p.sb("cand", (128, 8), F32)
        oh1 = p.sb("oh1", (128, 8), F32)
        oh2 = p.sb("oh2", (128, 8), F32)
        v1c = p.sb("v1c", (128, 1), F32)
        v2c = p.sb("v2c", (128, 1), F32)
        i1c = p.sb("i1c", (128, 1), F32)
        i2c = p.sb("i2c", (128, 1), F32)
        nmx = p.sb("nmx", (128, 1), F32)
        sm = p.sb("sm", (128, 1), F32)
        pc1 = p.sb("pc1", (128, 1), F32)
        pc2 = p.sb("pc2", (128, 1), F32)
        exw = 16 if layer == 0 else (8 if layer == 1 else 4)
        exo = p.sb("exo", (128, exw), F32)
        exo_bf = p.sb("exo_bf", (128, 16), BF16) if layer == 0 else None
        tvt = p.sb("tvt", (128, 2), F32) if layer == 0 else None
        if layer == 0:
            ps_tot = p.ps("ps_tot", (1, 16), F32)
            tot_sb = p.sb("tot_sb", (1, 16), F32)
            zz = p.on("vector", lambda e: e.memset(tot_sb[:], 0.0))
        prev_exp = None
        prev_fin = None
        prev_tot = None
        prev_sts = []
        for t in range(ntb):
            def gmm_fn(t):
                def f(e):
                    for dc in range(DC):
                        e.matmul(ps_lg[:], xnf[:, dc * T + t * 128: dc * T + (t + 1) * 128],
                                 wr_s[:, dc * E:(dc + 1) * E],
                                 start=(dc == 0), stop=False)
                    return e.matmul(ps_lg[:], onesrb[:], gbias_s[:], start=False, stop=True)
                return f
            gdeps = [ev[t * DC + DC - 1], ld_wr, ld_or, ld_gb]
            if prev_exp is not None:
                gdeps.append(prev_exp)
            g_mm = p.on("tensor", gmm_fn(t), deps=gdeps)
            if layer in (0, 3):
                v_mx = p.chain("vector", [
                    lambda e: e.reduce_max(v1c[:], ps_lg[:], axis=AX.X),
                    lambda e: e.tensor_scalar_mul(nmx[:], v1c[:], -1.0),
                ], deps=[g_mm] + ([prev_fin] if prev_fin else []))
                s_exp = p.on("scalar", lambda e: e.activation(probs[:], ps_lg[:], AF.Exp,
                                                              bias=nmx[:], accum_out=sm[:]),
                             deps=[v_mx])
                prev_exp = s_exp
                basedep = s_exp
            else:
                v_cpy = p.on("vector", lambda e: e.tensor_copy(probs[:], ps_lg[:]),
                             deps=[g_mm] + ([prev_fin] if prev_fin else []))
                prev_exp = v_cpy
                basedep = v_cpy

            fns = [
                lambda e: e.reduce_max(v1c[:], probs[:], axis=AX.X),
                lambda e: e.tensor_single_scalar(ohm[:], probs[:], v1c[:], ALU.is_equal),
                lambda e: e.scalar_tensor_tensor(cand[:], ohm[:], -1e4, iotap_s[:], ALU.mult, ALU.add),
                lambda e: e.tensor_reduce(i1c[:], cand[:], AX.X, ALU.min),
                lambda e: e.tensor_single_scalar(oh1[:], iota_s[:], i1c[:], ALU.is_equal),
                lambda e: e.scalar_tensor_tensor(base2[:], oh1[:], -1e9, probs[:], ALU.mult, ALU.add),
                lambda e: e.reduce_max(v2c[:], base2[:], axis=AX.X),
                lambda e: e.tensor_single_scalar(ohm[:], base2[:], v2c[:], ALU.is_equal),
                lambda e: e.scalar_tensor_tensor(cand[:], ohm[:], -1e4, iotap_s[:], ALU.mult, ALU.add),
                lambda e: e.tensor_reduce(i2c[:], cand[:], AX.X, ALU.min),
                lambda e: e.tensor_single_scalar(oh2[:], iota_s[:], i2c[:], ALU.is_equal),
            ]
            if layer == 1:
                fns.append(lambda e: e.tensor_sub(pc1[:], v1c[:], v2c[:]))
                fns.append(lambda e: e.tensor_sub(pc2[:], v2c[:], v1c[:]))
            v_top = p.chain("vector", fns, deps=[basedep])

            if layer == 0:
                deps = [v_top] + prev_sts
                if prev_tot is not None:
                    deps.append(prev_tot)
                fin_fns = [
                    lambda e: e.tensor_add(colA[:], v1c[:], v2c[:]),
                    lambda e: e.reciprocal(colB[:], colA[:]),
                    lambda e: e.tensor_mul(tvt[:, 0:1], v1c[:], colB[:]),
                    lambda e: e.tensor_mul(tvt[:, 1:2], v2c[:], colB[:]),
                    lambda e: e.tensor_copy(exo[:, 0:8], oh1[:]),
                    lambda e: e.tensor_copy(exo[:, 8:16], oh2[:]),
                    lambda e: e.tensor_copy(exo_bf[:, 0:8], oh1[:]),
                    lambda e: e.tensor_copy(exo_bf[:, 8:16], oh2[:]),
                ]
                if phases == 2:
                    fin_fns = fin_fns[4:6]
                v_fin = p.chain("vector", fin_fns, deps=deps)
                if phases >= 3:
                    t_mm = p.on("tensor",
                                (lambda t: lambda e: e.matmul(ps_tot[:], onescb[:], exo_bf[:],
                                                              start=True, stop=True))(t),
                                deps=[v_fin, ld_oc, zz] + ([prev_tot] if prev_tot else []))
                    t_tot = p.on("vector", lambda e: e.tensor_add(tot_sb[:], ps_tot[:], tot_sb[:]),
                                 deps=[t_mm])
                    prev_tot = t_tot
                else:
                    t_tot = v_fin
                st1 = p.dma("sync", oh_o[t * 128:(t + 1) * 128, :], exo[:], deps=[v_fin])
                prev_sts = [st1]
                if phases >= 3:
                    st2 = p.dma("sync", tv_o[t * 128:(t + 1) * 128, :], tvt[:], deps=[v_fin])
                    prev_sts = [st1, st2]
                prev_fin = t_tot
            elif layer == 1:
                s_sig = p.on("scalar", lambda e: (e.activation(pc1[:], pc1[:], AF.Sigmoid),
                                                  e.activation(pc2[:], pc2[:], AF.Sigmoid))[-1],
                             deps=[v_top])
                v_fin = p.chain("vector", [
                    lambda e: e.tensor_scalar_mul(cand[:], oh1[:], pc1[:]),
                    lambda e: e.scalar_tensor_tensor(exo[:], oh2[:], pc2[:], cand[:],
                                                     ALU.mult, ALU.add),
                ], deps=[s_sig] + prev_sts)
                st1 = p.dma("sync", rw_o[t * 128:(t + 1) * 128, :], exo[:], deps=[v_fin])
                prev_sts = [st1]
                prev_fin = v_fin
            else:
                v_fin = p.chain("vector", [
                    lambda e: e.reciprocal(colB[:], sm[:]),
                    lambda e: e.tensor_mul(exo[:, 0:1], v1c[:], colB[:]),
                    lambda e: e.tensor_mul(exo[:, 1:2], v2c[:], colB[:]),
                    lambda e: e.tensor_copy(exo[:, 2:3], i1c[:]),
                    lambda e: e.tensor_copy(exo[:, 3:4], i2c[:]),
                ], deps=[v_top] + prev_sts)
                st1 = p.dma("sync", gi_o[t * 128:(t + 1) * 128, :], exo[:], deps=[v_fin])
                prev_sts = [st1]
                prev_fin = v_fin
        if layer == 0 and phases >= 3:
            p.dma("sync", tot_o[:], tot_sb[:], deps=[prev_tot])

    if layer == 2 and phases >= 2:
        q_all = p.sb("q_all", (128, NT * 512), BF16)
        k_all = p.sb("k_all", (128, NT * 512), BF16)
        ps_q = p.ps("ps_q", (128, 512), F32)
        ps_k = p.ps("ps_k", (128, 512), F32)
        ps_sc = [p.ps(f"ps_sc{i}", (128, 256), F32) for i in range(2)]
        sc_sb = p.sb("sc_sb", (128, 256), F32)
        prev_qe = None
        prev_ke = None
        qk_done = []
        for t in range(NT):
            def qk_fn(t, w_s, b_s, pstile):
                def f(e):
                    for dc in range(DC):
                        e.matmul(pstile[:], xnf[:, dc * T + t * 128: dc * T + (t + 1) * 128],
                                 w_s[:, dc * 512:(dc + 1) * 512], start=(dc == 0), stop=False)
                    return e.matmul(pstile[:], onesrb[:], b_s[:], start=False, stop=True)
                return f
            q_mm = p.on("tensor", qk_fn(t, wq_s, bq_s, ps_q),
                        deps=[ev[t * DC + DC - 1], ld_wq, ld_bq, ld_or] + ([prev_qe] if prev_qe else []))
            s_qe = p.on("scalar", (lambda t: lambda e: e.copy(
                q_all[:, t * 512:(t + 1) * 512], ps_q[:]))(t), deps=[q_mm])
            k_mm = p.on("tensor", qk_fn(t, wk_s, bk_s, ps_k),
                        deps=[ld_wk, ld_bk] + ([prev_ke] if prev_ke else []))
            s_ke = p.on("scalar", (lambda t: lambda e: e.copy(
                k_all[:, t * 512:(t + 1) * 512], ps_k[:]))(t), deps=[k_mm])
            prev_qe, prev_ke = s_qe, s_ke
            qk_done.append(s_qe)
            qk_done.append(s_ke)
        prev_ev = {}
        for h in range(8):
            bank = h % 2
            base = 64 * (h % 2)
            def sch_fn(h, bank, base):
                def f(e):
                    inst = None
                    for t in range(NT):
                        inst = e.matmul(ps_sc[bank][base:base + 64, 0:64],
                                        q_all[:, t * 512 + h * 64: t * 512 + (h + 1) * 64],
                                        k_all[:, t * 512 + h * 64: t * 512 + (h + 1) * 64],
                                        start=(t == 0), stop=(t == NT - 1),
                                        tile_position=(0, base))
                    return inst
                return f
            deps = [qk_done[-1], qk_done[-2]]
            if h - 2 in prev_ev:
                deps.append(prev_ev[h - 2])
            sc_mm = p.on("tensor", sch_fn(h, bank, base), deps=deps)
            prev_ev[h] = p.on("vector",
                              (lambda h, bank, base: lambda e: e.tensor_copy(
                                  sc_sb[base:base + 64, (h // 2) * 64:(h // 2) * 64 + 64],
                                  ps_sc[bank][base:base + 64, 0:64]))(h, bank, base),
                              deps=[sc_mm])
        p.dma("sync", scp_o[:], sc_sb[:], deps=[prev_ev[6], prev_ev[7]])

    p.finalize_wait()
    return p


def _load_xnf(p):
    xnf_i = p.inp("xnf", (DC, 128, T), BF16)
    xnf = p.sb("xnf_s", (128, DC * T), BF16)
    ld = p.dma("sync", xnf[:].rearrange("p (c t) -> p c t", c=DC),
               xnf_i.rearrange("c p t -> p c t"))
    return xnf, ld


def _emit_out_finish(p, out_sb, g_o, dep_map, act=True):
    for t in range(NT):
        sl = out_sb[:, t * D:(t + 1) * D]
        s_act = p.on("scalar", (lambda sl: lambda e: e.activation(sl, sl, AF.Silu))(sl),
                     deps=[dep_map[t]])
        p.dma("sync", p.ap["g"][t * 128:(t + 1) * 128, :], sl, deps=[s_act])


def _emit_dense_moe(p, xnf, ld_xnf, w_sb, w_deps, nf, act_fn, has_b2, n_e=E, split_ld=False, stage='full'):
    FC = 512
    NFC = nf // FC
    FP = FC // 128
    w1_i = p.ap["w1"]
    b1_i = p.ap["b1"]
    w2_i = p.ap["w2"]
    b2_i = p.ap["b2"] if has_b2 else None
    out_sb = p.sb("out_sb", (128, NT * D), F32)
    h_sb = p.sb("h_sb", (128, FP * T), BF16)
    w1c = [p.sb(f"w1c{i}", (128, DC * FC), BF16) for i in range(2)]
    w2c = [p.sb(f"w2c{i}", (128, FP * D), BF16) for i in range(2)]
    b1c = [p.sb(f"b1c{i}", (1, FC), BF16) for i in range(2)]
    b2c = p.sb("b2c", (1, D), BF16) if has_b2 else None
    onesr5 = p.sb("onesr5", (1, 512), BF16)
    tmp_bf = p.sb("tmp_bf", (128, 512), BF16)
    ps_h = [p.ps(f"ps_h{i}", (128, 512), F32) for i in range(2)]
    ps_o = [p.ps(f"ps_o{i}", (128, 512), F32) for i in range(2)]
    cc = p.ap["c_ones_row512"]
    ld_or5 = p.dma("sync", onesr5[:], cc[:])
    z = p.on("vector", lambda e: e.memset(out_sb[:], 0.0))
    ld_b2 = None

    lastuse_w = [None, None]
    lastuse_b2 = None
    last_hps_evac = [None, None]
    last_ops_evac = [None, None]
    out_dep = {t: z for t in range(NT)}
    prev_loads = {}
    chunk_idx = 0
    for e in range(n_e):
        if has_b2:
            deps = [lastuse_b2] if lastuse_b2 else []
            ld_b2 = p.dma("gpsimd", b2c[:], b2_i[e:e + 1, :], deps=deps)
        for fc in range(NFC):
            bi = chunk_idx % 2
            f0 = fc * FC
            dl = [lastuse_w[bi]] if lastuse_w[bi] else []
            if split_ld:
                for dc in range(DC):
                    ld1 = p.dma("gpsimd", w1c[bi][:, dc * FC:(dc + 1) * FC],
                                w1_i[e, dc * 128:(dc + 1) * 128, f0:f0 + FC], deps=dl)
                for q in range(FP):
                    ld2 = p.dma("gpsimd", w2c[bi][:, q * D:(q + 1) * D],
                                w2_i[e, f0 + q * 128: f0 + (q + 1) * 128, :], deps=dl)
            else:
                ld1 = p.dma("gpsimd", w1c[bi][:].rearrange("p (c f) -> p c f", c=DC),
                            w1_i[e, :, f0:f0 + FC].rearrange("(c p) f -> p c f", p=128),
                            deps=dl)
                ld2 = p.dma("gpsimd", w2c[bi][:].rearrange("p (q d) -> p q d", q=FP),
                            w2_i[e, f0:f0 + FC, :].rearrange("(q p) d -> p q d", p=128),
                            deps=dl)
            ld3 = p.dma("gpsimd", b1c[bi][:], b1_i[e:e + 1, f0:f0 + FC], deps=dl)
            h_evac = []
            for fp in range(FP):
                for tc in range(4):
                    pb = (fp * 4 + tc) % 2
                    def hmm(bi, fp, tc, pb):
                        def f(e_):
                            for dc in range(DC):
                                e_.matmul(ps_h[pb][:],
                                          w1c[bi][:, dc * FC + fp * 128: dc * FC + (fp + 1) * 128],
                                          xnf[:, dc * T + tc * 512: dc * T + (tc + 1) * 512],
                                          start=(dc == 0), stop=False)
                            return e_.matmul(ps_h[pb][:], b1c[bi][0:1, fp * 128:(fp + 1) * 128],
                                             onesr5[:], start=False, stop=True)
                        return f
                    deps = [ld1, ld3, ld_xnf, ld_or5]
                    if last_hps_evac[pb] is not None:
                        deps.append(last_hps_evac[pb])
                    if fp == 0 and tc == 0 and prev_loads.get("w2_last_mm") is not None:
                        deps.append(prev_loads["w2_last_mm"])
                    mm = p.on("tensor", hmm(bi, fp, tc, pb), deps=deps)
                    ev = p.on("scalar",
                              (lambda fp, tc, pb: lambda e_: e_.activation(
                                  h_sb[:, fp * T + tc * 512: fp * T + (tc + 1) * 512],
                                  ps_h[pb][:], act_fn))(fp, tc, pb),
                              deps=[mm])
                    last_hps_evac[pb] = ev
                    h_evac.append(ev)
            if stage == 'h':
                prev_loads["w2_last_mm"] = h_evac[-1]
                lastuse_w[bi] = h_evac[-1]
                chunk_idx += 1
                continue
            addb2 = has_b2 and (fc == NFC - 1)
            import os as _os
            if _os.environ.get("F0B2ALL"):
                addb2 = has_b2
            for t in range(NT):
                for half in range(2):
                    pb = half
                    def omm(bi, t, half, pb, addb2):
                        def f(e_):
                            for fp in range(FP):
                                e_.matmul(ps_o[pb][:],
                                          h_sb[:, fp * T + t * 128: fp * T + (t + 1) * 128],
                                          w2c[bi][:, fp * D + half * 512: fp * D + half * 512 + 512],
                                          start=(fp == 0), stop=False)
                            if addb2:
                                return e_.matmul(ps_o[pb][:], p._onesr128[:],
                                                 b2c[0:1, half * 512: half * 512 + 512],
                                                 start=False, stop=True)
                            return p.end_group(e_, ps_o[pb][:], 128, 512)
                        return f
                    deps = [h_evac[-1], ld2] + ([ld_b2] if addb2 else []) + p._ge_dep
                    if last_ops_evac[pb] is not None:
                        deps.append(last_ops_evac[pb])
                    mm = p.on("tensor", omm(bi, t, half, pb, addb2), deps=deps)
                    lastuse_w[bi] = mm
                    if has_b2:
                        lastuse_b2 = mm
                    prev_loads["w2_last_mm"] = mm
                    wcol = w_sb[:, t * E + e: t * E + e + 1]
                    if stage == 'w2noevac':
                        sc = p.on("vector",
                                  (lambda pb: lambda e_: e_.tensor_copy(tmp_bf[:], ps_o[pb][:]))(pb),
                                  deps=[mm] + ([last_ops_evac[pb]] if last_ops_evac[pb] else []))
                    else:
                        sc = p.on("scalar",
                                  (lambda pb, wcol: lambda e_: e_.activation(
                                      tmp_bf[:], ps_o[pb][:], AF.Copy, scale=wcol))(pb, wcol),
                                  deps=[mm] + w_deps + ([last_ops_evac[pb]] if last_ops_evac[pb] else []))
                    ad = p.on("vector",
                              (lambda t, half: lambda e_: e_.tensor_add(
                                  out_sb[:, t * D + half * 512: t * D + half * 512 + 512],
                                  tmp_bf[:],
                                  out_sb[:, t * D + half * 512: t * D + half * 512 + 512]))(t, half),
                              deps=[sc, out_dep[t]])
                    last_ops_evac[pb] = ad
                    out_dep[t] = ad
            chunk_idx += 1
    return out_sb, out_dep


def build_ffn0(layer=0, n_e=E, skip_w=False, split_ld=False):
    p = Prog("F0")
    cc = const_inputs(p, ["ones_row512", "ones_row128", "ltri_s", "ones_col"])
    xnf, ld_xnf = _load_xnf(p)
    p.inp("w1", (E, D, 4096), F32)
    p.inp("b1", (E, 4096), F32)
    p.inp("w2", (E, 4096, D), F32)
    p.inp("b2", (E, D), F32)
    oh_i = p.inp("oh", (T, 16), F32)
    tv_i = p.inp("tv", (T, 2), F32)
    totst_i = p.inp("totst", (16, NCORES), F32)
    maskb_i = p.inp("maskb", (16, NCORES), F32)
    g_o = p.outp("g", (T, D), F32)
    onesr128 = p.sb("onesr128b", (1, 128), BF16)
    p._onesr128 = onesr128
    ld_o128 = p.dma("sync", onesr128[:], cc["ones_row128"][:])
    p.setup_group_end(512)

    CAP = 2560
    if skip_w:
        w_sb = p.sb("w_sb", (128, NT * E), F32)
        sk = True
    sk = skip_w
    if skip_w:
        pass
    w_sb = w_sb if skip_w else p.sb("w_sb", (128, NT * E), F32)
    ltri = p.sb("ltri", (128, 128), BF16)
    rank_dep = None
    if not skip_w:
        totst = p.sb("totst_s", (16, NCORES), F32)
        maskb = p.sb("maskb_s", (16, NCORES), F32)
        offc = p.sb("offc", (16, 1), F32)
        offrow = p.sb("offrow", (1, 16), F32)
        runrow = p.sb("runrow", (1, 16), F32)
        runlo = p.sb("runlo", (1, 16), BF16)
        runhi = p.sb("runhi", (1, 16), BF16)
        runtmp = p.sb("runtmp", (1, 16), F32)
        oh_t = p.sb("oh_t", (128, 16), F32)
        oh_bf = p.sb("oh_bf", (128, 16), BF16)
        tv_t = p.sb("tv_t", (128, 2), F32)
        keep = p.sb("keep", (128, 16), F32)
        m1 = p.sb("m1", (128, 8), F32)
        m2 = p.sb("m2", (128, 8), F32)
        off_dram = p.dram("off_scratch", (16, 1), F32)
        ps_rank = p.ps("ps_rank", (128, 16), F32)
        ps_t16 = p.ps("ps_t16", (1, 16), F32)
        ld_lt = p.dma("sync", ltri[:], cc["ltri_s"][:])
        ld_ts = p.dma("sync", totst[:], totst_i[:])
        ld_mb = p.dma("sync", maskb[:], maskb_i[:])
        v_off = p.chain("vector", [
            lambda e: e.tensor_mul(totst[:], totst[:], maskb[:]),
            lambda e: e.reduce_sum(offc[:], totst[:], axis=AX.X),
        ], deps=[ld_ts, ld_mb])
        d1 = p.dma("sync", off_dram[:, :], offc[:, 0:1], deps=[v_off])
        d2 = p.dma("sync", offrow[:], off_dram.rearrange("a b -> b a"), deps=[d1])
        v_run = p.chain("vector", [
            lambda e: e.tensor_copy(runrow[:], offrow[:]),
            lambda e: e.tensor_copy(runhi[:], runrow[:]),
            lambda e: e.tensor_sub(runtmp[:], runrow[:], runhi[:]),
            lambda e: e.tensor_copy(runlo[:], runtmp[:]),
        ], deps=[d2])
        rank_dep = None
        for t in range(NT):
            ld_oh = p.dma("sync", oh_t[:], oh_i[t * 128:(t + 1) * 128, :],
                          deps=[rank_dep] if rank_dep else [])
            ld_tv = p.dma("sync", tv_t[:], tv_i[t * 128:(t + 1) * 128, :],
                          deps=[rank_dep] if rank_dep else [])
            v_cast = p.on("vector", lambda e: e.tensor_copy(oh_bf[:], oh_t[:]), deps=[ld_oh])
            def rmm(e):
                e.matmul(ps_rank[:], ltri[:], oh_bf[:], start=True, stop=False)
                e.matmul(ps_rank[:], p._onesr128[:], runhi[:], start=False, stop=False)
                return e.matmul(ps_rank[:], p._onesr128[:], runlo[:], start=False, stop=True)
            mm_r = p.on("tensor", rmm, deps=[v_cast, v_run, ld_lt, ld_o128] +
                        ([rank_dep] if rank_dep else []))
            onescb2 = getattr(p, "_onescb2", None)
            if onescb2 is None:
                onescb2 = p.sb("onescb2", (128, 1), BF16)
                p._onescb2 = onescb2
                ld_ocb = p.dma("sync", onescb2[:], cc["ones_col"][:])
                p._ld_ocb = ld_ocb
            mm_t = p.on("tensor", (lambda: lambda e: e.matmul(ps_t16[:], p._onescb2[:], oh_bf[:],
                                                              start=True, stop=True))(),
                        deps=[v_cast, p._ld_ocb] + ([rank_dep] if rank_dep else []))
            v_runup = p.chain("vector", [
                lambda e: e.tensor_add(runrow[:], ps_t16[:], runrow[:]),
                lambda e: e.tensor_copy(runhi[:], runrow[:]),
                lambda e: e.tensor_sub(runtmp[:], runrow[:], runhi[:]),
                lambda e: e.tensor_copy(runlo[:], runtmp[:]),
            ], deps=[mm_t, mm_r])
            wv = p.chain("vector", [
                lambda e: e.tensor_single_scalar(keep[:], ps_rank[:], CAP - 0.5, ALU.is_le),
                lambda e: e.tensor_mul(m1[:], keep[:, 0:8], oh_t[:, 0:8]),
                lambda e: e.tensor_mul(m2[:], keep[:, 8:16], oh_t[:, 8:16]),
                lambda e: e.tensor_scalar_mul(m1[:], m1[:], tv_t[:, 0:1]),
                (lambda t: lambda e: e.scalar_tensor_tensor(
                    w_sb[:, t * E:(t + 1) * E], m2[:], tv_t[:, 1:2], m1[:],
                    ALU.mult, ALU.add))(t),
            ], deps=[mm_r, ld_tv, v_runup])
            rank_dep = wv

    w_deps = [rank_dep]
    if skip_w:
        w_deps = [p.on("vector", lambda e: e.memset(w_sb[:], 0.25))]
    import os as _os
    out_sb, out_dep = _emit_dense_moe(p, xnf, ld_xnf, w_sb, w_deps, 4096, AF.Gelu, True,
                                      n_e=n_e, split_ld=split_ld,
                                      stage=_os.environ.get("F0STAGE", "full"))
    _emit_out_finish(p, out_sb, g_o, out_dep)
    p.finalize_wait()
    return p

def build_ffn3():
    p = Prog("F3")
    cc = const_inputs(p, ["ones_row512", "ones_row128", "ones_col"])
    xnf, ld_xnf = _load_xnf(p)
    p.inp("w1", (E, D, 4096), F32)
    p.inp("b1", (E, 4096), F32)
    p.inp("w2", (E, 4096, D), F32)
    p.inp("b2", (E, D), F32)
    g1a_i = p.inp("g1a", (128, 128), F32)
    g2a_i = p.inp("g2a", (128, 128), F32)
    i1a_i = p.inp("i1a", (128, 128), F32)
    i2a_i = p.inp("i2a", (128, 128), F32)
    gio_i = p.inp("gio", (T, 4), F32)
    g_o = p.outp("g", (T, D), F32)
    import os as _os
    _dbg = bool(_os.environ.get("F3DBG"))
    if _dbg:
        th_o = p.outp("th_dbg", (128, E), F32)
        w_o = p.outp("w_dbg", (128, NT * E), F32)
    onesr128 = p.sb("onesr128b", (1, 128), BF16)
    p._onesr128 = onesr128
    ld_o128 = p.dma("sync", onesr128[:], cc["ones_row128"][:])
    p.setup_group_end(512)
    onescb = p.sb("onescb", (128, 1), BF16)
    ld_ocb = p.dma("sync", onescb[:], cc["ones_col"][:])

    CAP3 = 2458
    NIT = 34
    g1a = p.sb("g1a_s", (128, 128), F32)
    g2a = p.sb("g2a_s", (128, 128), F32)
    i1a = p.sb("i1a_s", (128, 128), F32)
    i2a = p.sb("i2a_s", (128, 128), F32)
    lds = [p.dma("sync", s_[:], a_[:]) for s_, a_ in
           ((g1a, g1a_i), (g2a, g2a_i), (i1a, i1a_i), (i2a, i2a_i))]
    eq1 = p.sb("eq1", (128, 128), F32)
    eq2 = p.sb("eq2", (128, 128), F32)
    mt1 = p.sb("mt1", (128, 128), F32)
    mt2 = p.sb("mt2", (128, 128), F32)
    rsum = p.sb("rsum", (128, 1), F32)
    rsum_b = p.sb("rsum_b", (128, 1), BF16)
    cnt_sb = p.sb("cnt_sb", (1, 1), F32)
    cond1 = p.sb("cond1", (1, 1), F32)
    cond1b = p.sb("cond1b", (1, 1), BF16)
    lo = p.sb("lo", (128, 1), F32)
    hi = p.sb("hi", (128, 1), F32)
    midc = p.sb("midc", (128, 1), F32)
    cond = p.sb("cond", (128, 1), F32)
    notc = p.sb("notc", (128, 1), F32)
    d1c = p.sb("d1c", (128, 1), F32)
    d2c = p.sb("d2c", (128, 1), F32)
    theta = p.sb("theta", (128, E), F32)
    ps_cnt = p.ps("ps_cnt", (1, 1), F32)
    ps_bc = p.ps("ps_bc", (128, 1), F32)
    th_dep = None
    for e_i in range(E):
        eqd = p.chain("vector", [
            (lambda e_i: lambda e: e.tensor_single_scalar(eq1[:], i1a[:], float(e_i), ALU.is_equal))(e_i),
            (lambda e_i: lambda e: e.tensor_single_scalar(eq2[:], i2a[:], float(e_i), ALU.is_equal))(e_i),
            lambda e: e.memset(lo[:], -1.0),
            lambda e: e.memset(hi[:], 2.0),
        ], deps=lds + ([th_dep] if th_dep else []))
        dep = eqd
        for it in range(NIT):
            vch = p.chain("vector", [
                lambda e: e.tensor_add(midc[:], lo[:], hi[:]),
                lambda e: e.tensor_scalar_mul(midc[:], midc[:], 0.5),
                lambda e: e.tensor_single_scalar(mt1[:], g1a[:], midc[:], ALU.is_gt),
                lambda e: e.tensor_mul(mt1[:], mt1[:], eq1[:]),
                lambda e: e.tensor_single_scalar(mt2[:], g2a[:], midc[:], ALU.is_gt),
                lambda e: e.tensor_mul(mt2[:], mt2[:], eq2[:]),
                lambda e: e.tensor_add(mt1[:], mt1[:], mt2[:]),
                lambda e: e.reduce_sum(rsum[:], mt1[:], axis=AX.X),
                lambda e: e.tensor_copy(rsum_b[:], rsum[:]),
            ], deps=[dep])
            mm_c = p.on("tensor", lambda e: e.matmul(ps_cnt[:], onescb[:], rsum_b[:],
                                                     start=True, stop=True),
                        deps=[vch, ld_ocb])
            sc_e = p.on("scalar", lambda e: e.copy(cnt_sb[:], ps_cnt[:]), deps=[mm_c])
            vc = p.chain("vector", [
                lambda e: e.tensor_single_scalar(cond1[:], cnt_sb[:], CAP3 + 0.5, ALU.is_lt),
                lambda e: e.tensor_copy(cond1b[:], cond1[:]),
            ], deps=[sc_e])
            mm_b = p.on("tensor", lambda e: e.matmul(ps_bc[:], onesr128[:], cond1b[:],
                                                     start=True, stop=True),
                        deps=[vc, ld_o128])
            dep = p.chain("vector", [
                lambda e: e.tensor_copy(cond[:], ps_bc[:]),
                lambda e: e.tensor_scalar(notc[:], cond[:], -1.0, 1.0, ALU.mult, ALU.add),
                lambda e: e.tensor_sub(d1c[:], midc[:], hi[:]),
                lambda e: e.tensor_mul(d1c[:], d1c[:], cond[:]),
                lambda e: e.tensor_add(hi[:], hi[:], d1c[:]),
                lambda e: e.tensor_sub(d2c[:], midc[:], lo[:]),
                lambda e: e.tensor_mul(d2c[:], d2c[:], notc[:]),
                lambda e: e.tensor_add(lo[:], lo[:], d2c[:]),
            ], deps=[mm_b])
        th_dep = p.on("vector",
                      (lambda e_i: lambda e: e.tensor_copy(theta[:, e_i:e_i + 1], hi[:]))(e_i),
                      deps=[dep])

    w_sb = p.sb("w_sb", (128, NT * E), F32)
    gio_t = p.sb("gio_t", (128, 4), F32)
    c1 = p.sb("c1", (128, 1), F32)
    c2 = p.sb("c2", (128, 1), F32)
    w_dep = th_dep
    for t in range(NT):
        ldg = p.dma("sync", gio_t[:], gio_i[t * 128:(t + 1) * 128, :], deps=[w_dep])
        fns = []
        for e_i in range(E):
            def mk(e_i, t):
                def f1(e):
                    return e.tensor_single_scalar(c1[:], gio_t[:, 0:1], theta[:, e_i:e_i + 1], ALU.is_gt)
                def f2(e):
                    return e.tensor_single_scalar(c2[:], gio_t[:, 2:3], float(e_i), ALU.is_equal)
                def f3(e):
                    return e.tensor_mul(c1[:], c1[:], c2[:])
                def f4(e):
                    return e.tensor_mul(c1[:], c1[:], gio_t[:, 0:1])
                def f5(e):
                    return e.tensor_single_scalar(c2[:], gio_t[:, 1:2], theta[:, e_i:e_i + 1], ALU.is_gt)
                def f6(e):
                    return e.tensor_mul(c1[:], c1[:], 1.0) if False else None
                return [f1, f2, f3, f4]
            fns += mk(e_i, t)
            def mk2(e_i, t):
                def f1(e):
                    return e.tensor_single_scalar(c2[:], gio_t[:, 1:2], theta[:, e_i:e_i + 1], ALU.is_gt)
                def f2(e):
                    return e.tensor_single_scalar(cond[:], gio_t[:, 3:4], float(e_i), ALU.is_equal)
                def f3(e):
                    return e.tensor_mul(c2[:], c2[:], cond[:])
                def f4(e):
                    return e.tensor_mul(c2[:], c2[:], gio_t[:, 1:2])
                def f5(e):
                    return e.scalar_tensor_tensor(w_sb[:, t * E + e_i: t * E + e_i + 1],
                                                  c2[:], 1.0, c1[:], ALU.mult, ALU.add)
                return [f1, f2, f3, f4, f5]
            fns += mk2(e_i, t)
        w_dep = p.chain("vector", fns, deps=[ldg])
    if _dbg:
        p.dma("sync", th_o[:], theta[:], deps=[w_dep])
        p.dma("sync", w_o[:], w_sb[:], deps=[w_dep])
    out_sb, out_dep = _emit_dense_moe(p, xnf, ld_xnf, w_sb, [w_dep], 4096, AF.Silu, True)
    _emit_out_finish(p, out_sb, g_o, out_dep)
    p.finalize_wait()
    return p

def build_ffn1():
    p = Prog("F1")
    cc = const_inputs(p, ["ones_row512", "ones_row128", "ones_col_f32"])
    xnf, ld_xnf = _load_xnf(p)
    wg_i = p.inp("wg", (E, D, 512), F32)
    wu_i = p.inp("wu", (E, D, 512), F32)
    wd_i = p.inp("wd", (E, 512, D), F32)
    sg_i = p.inp("sg", (D, D), F32)
    su_i = p.inp("su", (D, D), F32)
    sd_i = p.inp("sd", (D, D), F32)
    rw_i = p.inp("rw", (T, E), F32)
    g_o = p.outp("g", (T, D), F32)
    onesr128 = p.sb("onesr128b", (1, 128), BF16)
    p._onesr128 = onesr128
    ld_o128 = p.dma("sync", onesr128[:], cc["ones_row128"][:])
    p.setup_group_end(512)
    FC = 512
    FP = 4
    out_sb = p.sb("out_sb", (128, NT * D), F32)
    h_sb = p.sb("h_sb", (128, FP * T), BF16)
    sil_sb = p.sb("sil_sb", (128, 512), BF16)
    wgc = [p.sb(f"wgc{i}", (128, DC * FC), BF16) for i in range(2)]
    wuc = [p.sb(f"wuc{i}", (128, DC * FC), BF16) for i in range(2)]
    wdc = [p.sb(f"wdc{i}", (128, FP * D), BF16) for i in range(2)]
    rw_sb = p.sb("rw_sb", (128, NT * E), F32)
    ones_col = p.sb("ones_colf", (128, 1), F32)
    tmp_bf = p.sb("tmp_bf", (128, 512), BF16)
    ps_a = [p.ps(f"ps_a{i}", (128, 512), F32) for i in range(2)]
    ps_o = [p.ps(f"ps_o{i}", (128, 512), F32) for i in range(2)]
    ld_rw = p.dma("sync", rw_sb[:].rearrange("p (t e) -> p t e", t=NT),
                  rw_i.rearrange("(t p) e -> p t e", p=128))
    ld_oc = p.dma("sync", ones_col[:], cc["ones_col_f32"][:])
    z = p.on("vector", lambda e: e.memset(out_sb[:], 0.0))

    lastuse = [None, None]
    last_aps = [None, None]
    last_ops = [None, None]
    last_sil = None
    last_h = None
    out_dep = {t: z for t in range(NT)}
    passes = [(e_i, fc, False) for e_i in range(E) for fc in range(1)] + \
             [(0, fc, True) for fc in range(2)]
    ci = 0
    for (e_i, fc2, shared) in passes:
        bi = ci % 2
        dl = [lastuse[bi]] if lastuse[bi] else []
        if not shared:
            ldg = p.dma("gpsimd", wgc[bi][:].rearrange("p (c f) -> p c f", c=DC),
                        wg_i[e_i].rearrange("(c p) f -> p c f", p=128), deps=dl)
            ldu = p.dma("gpsimd", wuc[bi][:].rearrange("p (c f) -> p c f", c=DC),
                        wu_i[e_i].rearrange("(c p) f -> p c f", p=128), deps=dl)
            ldd = p.dma("gpsimd", wdc[bi][:].rearrange("p (q d) -> p q d", q=FP),
                        wd_i[e_i].rearrange("(q p) d -> p q d", p=128), deps=dl)
        else:
            f0 = fc2 * FC
            ldg = p.dma("gpsimd", wgc[bi][:].rearrange("p (c f) -> p c f", c=DC),
                        sg_i[:, f0:f0 + FC].rearrange("(c p) f -> p c f", p=128), deps=dl)
            ldu = p.dma("gpsimd", wuc[bi][:].rearrange("p (c f) -> p c f", c=DC),
                        su_i[:, f0:f0 + FC].rearrange("(c p) f -> p c f", p=128), deps=dl)
            ldd = p.dma("gpsimd", wdc[bi][:].rearrange("p (q d) -> p q d", q=FP),
                        sd_i[f0:f0 + FC, :].rearrange("(q p) d -> p q d", p=128), deps=dl)
        h_done = []
        for fp in range(FP):
            for tc in range(4):
                def gmm(bi, fp, tc, w_s, pb):
                    def f(e_):
                        for dc in range(DC):
                            e_.matmul(ps_a[pb][:],
                                      w_s[bi][:, dc * FC + fp * 128: dc * FC + (fp + 1) * 128],
                                      xnf[:, dc * T + tc * 512: dc * T + (tc + 1) * 512],
                                      start=(dc == 0), stop=False)
                        return p.end_group(e_, ps_a[pb][:], 128, 512)
                    return f
                mm_g = p.on("tensor", gmm(bi, fp, tc, wgc, 0),
                            deps=[ldg, ld_xnf] + p._ge_dep + ([last_aps[0]] if last_aps[0] else []))
                sil = p.on("scalar",
                           lambda e_: e_.activation(sil_sb[:], ps_a[0][:], AF.Silu),
                           deps=[mm_g] + ([last_sil] if last_sil else []))
                last_aps[0] = sil
                mm_u = p.on("tensor", gmm(bi, fp, tc, wuc, 1),
                            deps=[ldu] + ([last_aps[1]] if last_aps[1] else []))
                hv = p.on("vector",
                          (lambda fp, tc: lambda e_: e_.scalar_tensor_tensor(
                              h_sb[:, fp * T + tc * 512: fp * T + (tc + 1) * 512],
                              ps_a[1][:], 1.0, sil_sb[:], ALU.mult, ALU.mult))(fp, tc),
                          deps=[mm_u, sil, prev_loads_hwar(ci, h_done)])
                last_aps[1] = hv
                last_sil = hv
                h_done.append(hv)
        last_h = h_done[-1]
        for t in range(NT):
            for half in range(2):
                pb = half
                def omm(bi, t, half, pb):
                    def f(e_):
                        for fp in range(FP):
                            e_.matmul(ps_o[pb][:],
                                      h_sb[:, fp * T + t * 128: fp * T + (t + 1) * 128],
                                      wdc[bi][:, fp * D + half * 512: fp * D + half * 512 + 512],
                                      start=(fp == 0), stop=False)
                        return p.end_group(e_, ps_o[pb][:], 128, 512)
                    return f
                mm = p.on("tensor", omm(bi, t, half, pb),
                          deps=[h_done[-1], ldd] + p._ge_dep +
                               ([last_ops[pb]] if last_ops[pb] else []))
                lastuse[bi] = mm
                wcol = ones_col[:] if shared else rw_sb[:, t * E + e_i: t * E + e_i + 1]
                sc = p.on("scalar",
                          (lambda pb, wcol: lambda e_: e_.activation(
                              tmp_bf[:], ps_o[pb][:], AF.Copy, scale=wcol))(pb, wcol),
                          deps=[mm, ld_rw, ld_oc] + ([last_ops[pb]] if last_ops[pb] else []))
                ad = p.on("vector",
                          (lambda t, half: lambda e_: e_.tensor_add(
                              out_sb[:, t * D + half * 512: t * D + half * 512 + 512],
                              tmp_bf[:],
                              out_sb[:, t * D + half * 512: t * D + half * 512 + 512]))(t, half),
                          deps=[sc, out_dep[t]])
                last_ops[pb] = ad
                out_dep[t] = ad
        ci += 1
    _emit_out_finish(p, out_sb, g_o, out_dep)
    p.finalize_wait()
    return p


def prev_loads_hwar(ci, h_done):
    return None

def build_ffn2():
    p = Prog("F2")
    cc = const_inputs(p, ["ident", "ones_row128"])
    xnf, ld_xnf = _load_xnf(p)
    w1t_i = p.inp("w1t", (D, 4096), F32)
    w2t_i = p.inp("w2t", (D, 4096), F32)
    h1t_i = p.inp("h1t", (4096, 4096), F32)
    h2t_i = p.inp("h2t", (4096, 4096), F32)
    w3t_i = p.inp("w3t", (4096, D), F32)
    s1_i = p.inp("s1", (128, 32), F32)
    s2_i = p.inp("s2", (128, 32), F32)
    h1s_i = p.inp("h1s", (128, 32), F32)
    h2s_i = p.inp("h2s", (128, 32), F32)
    w3s_i = p.inp("w3s", (128, 8), F32)
    g_o = p.outp("g", (T, D), F32)
    import os as _os
    _dbg = bool(_os.environ.get("F2DBG"))
    if _dbg:
        t1_o = p.outp("t1_dbg", (128, 32 * 1024), BF16)
        t2_o = p.outp("t2_dbg", (128, 32 * 1024), BF16)
    onesr128 = p.sb("onesr128b", (1, 128), BF16)
    p._onesr128 = onesr128
    ld_o128 = p.dma("sync", onesr128[:], cc["ones_row128"][:])
    p.setup_group_end(512)
    id_s = p.sb("id_s", (128, 128), BF16)
    ld_id = p.dma("sync", id_s[:], cc["ident"][:])
    SW = 1024
    NS = T // SW
    t1 = p.sb("t1", (128, 32 * SW), BF16)
    t2 = p.sb("t2", (128, 32 * SW), BF16)
    sil_sb = p.sb("sil_sb", (128, 512), BF16)
    gfm = p.sb("gfm", (128, DC * SW), BF16)
    gtile = p.sb("gtile", (128, D), F32)
    wa = [p.sb(f"wa{i}", (128, 8 * 128), BF16) for i in range(2)]
    wb = [p.sb(f"wb{i}", (128, 8 * 128), BF16) for i in range(2)]
    wh = [p.sb(f"wh{i}", (128, 32 * 128), BF16) for i in range(2)]
    scl = {}
    ld_scl = {}
    for nm, ap_ in (("s1", s1_i), ("s2", s2_i), ("h1s", h1s_i), ("h2s", h2s_i), ("w3s", w3s_i)):
        scl[nm] = p.sb(nm + "_s", list(ap_.tensor.shape) if False else (128, ap_.shape[1]), F32)
        ld_scl[nm] = p.dma("sync", scl[nm][:], ap_[:])
    ps_a = [p.ps(f"ps_a{i}", (128, 512), F32) for i in range(2)]
    ps_c = [p.ps(f"ps_c{i}", (128, 512), F32) for i in range(2)]
    ps_tr = [p.ps(f"ps_tr{i}", (128, 128), BF16) for i in range(2)]

    lw = {"a": [None, None], "b": [None, None], "h": [None, None]}
    last_ps = [None, None]
    last_psc = [None, None]
    last_sil = None
    last_t1w = None
    last_t1r = None
    last_t2w = None
    last_t2r = None
    last_trev = [None, None]
    ci = 0
    for s_ in range(NS):
        tok0 = s_ * SW
        aw = []
        for ob in range(32):
            bi = ci % 2
            ld1 = p.dma("gpsimd", wa[bi][:].rearrange("p (c f) -> p c f", c=DC),
                        w1t_i[:, ob * 128:(ob + 1) * 128].rearrange("(c p) f -> p c f", p=128),
                        deps=[lw["a"][bi]] if lw["a"][bi] else [])
            ld2 = p.dma("gpsimd", wb[bi][:].rearrange("p (c f) -> p c f", c=DC),
                        w2t_i[:, ob * 128:(ob + 1) * 128].rearrange("(c p) f -> p c f", p=128),
                        deps=[lw["b"][bi]] if lw["b"][bi] else [])
            for tc in range(SW // 512):
                def amm(bi, ob, tc, w_s, pb, ps, tok0):
                    def f(e_):
                        for dc in range(DC):
                            e_.matmul(ps[pb][:],
                                      w_s[bi][:, dc * 128: dc * 128 + 128],
                                      xnf[:, dc * T + tok0 + tc * 512: dc * T + tok0 + tc * 512 + 512],
                                      start=(dc == 0), stop=False)
                        return p.end_group(e_, ps[pb][:], 128, 512)
                    return f
                mm1 = p.on("tensor", amm(bi, ob, tc, wa, 0, ps_a, tok0),
                           deps=[ld1, ld_xnf] + p._ge_dep + ([last_ps[0]] if last_ps[0] else []))
                sil = p.on("scalar",
                           (lambda ob: lambda e_: e_.activation(
                               sil_sb[:], ps_a[0][:], AF.Silu,
                               scale=scl["s1"][:, ob:ob + 1]))(ob),
                           deps=[mm1, ld_scl["s1"]] + ([last_sil] if last_sil else []))
                last_ps[0] = sil
                mm2 = p.on("tensor", amm(bi, ob, tc, wb, 1, ps_a, tok0),
                           deps=[ld2] + ([last_ps[1]] if last_ps[1] else []))
                hv = p.on("vector",
                          (lambda ob, tc: lambda e_: e_.scalar_tensor_tensor(
                              t1[:, ob * SW + tc * 512: ob * SW + tc * 512 + 512],
                              ps_a[1][:], scl["s2"][:, ob:ob + 1], sil_sb[:],
                              ALU.mult, ALU.mult))(ob, tc),
                          deps=[mm2, sil, ld_scl["s2"]] + ([last_t1r] if last_t1r else []))
                last_ps[1] = hv
                last_sil = hv
                aw.append(hv)
            lw["a"][bi] = mm1
            lw["b"][bi] = mm2
            ci += 1
        last_t1w = aw[-1]
        if _dbg and s_ == 0:
            last_t1r = p.dma("sync", t1_o[:], t1[:], deps=[last_t1w])
        for stage, (win, wsn, src, dstw, nob, actf) in enumerate((
                ("h1t", "h1s", "t1", "t2", 32, AF.Gelu),
                ("h2t", "h2s", "t2", "t1", 32, AF.Gelu),
                ("w3t", "w3s", "t1", "gfm", 8, AF.Silu))):
            w_in = p.ap[win]
            src_sb = {"t1": t1, "t2": t2}[src]
            dst_sb = {"t2": t2, "t1": t1, "gfm": gfm}[dstw]
            dw = []
            for ob in range(nob):
                bi = ci % 2
                ldh = p.dma("gpsimd", wh[bi][:].rearrange("p (c f) -> p c f", c=32),
                            w_in[:, ob * 128:(ob + 1) * 128].rearrange("(c p) f -> p c f", p=128),
                            deps=[lw["h"][bi]] if lw["h"][bi] else [])
                for tc in range(SW // 512):
                    def bmm(bi, ob, tc, src_sb):
                        def f(e_):
                            for ic in range(32):
                                e_.matmul(ps_c[tc][:],
                                          wh[bi][:, ic * 128: ic * 128 + 128],
                                          src_sb[:, ic * SW + tc * 512: ic * SW + tc * 512 + 512],
                                          start=(ic == 0), stop=False)
                            return p.end_group(e_, ps_c[tc][:], 128, 512)
                        return f
                    srcdep = last_t1w if src == "t1" else last_t2w
                    deps = [ldh, srcdep] + p._ge_dep
                    if last_psc[tc] is not None:
                        deps.append(last_psc[tc])
                    wardep = None
                    if ob == 0 and tc == 0:
                        wardep = last_t2r if dstw == "t2" else (last_t1r if dstw == "t1" else None)
                    mm = p.on("tensor", bmm(bi, ob, tc, src_sb), deps=deps + ([wardep] if wardep else []))
                    lw["h"][bi] = mm
                    if src == "t1":
                        last_t1r = mm
                    else:
                        last_t2r = mm
                    colw = SW if dstw != "gfm" else SW
                    ev = p.on("scalar",
                              (lambda ob, tc, dst_sb, actf, wsn: lambda e_: e_.activation(
                                  dst_sb[:, ob * SW + tc * 512: ob * SW + tc * 512 + 512],
                                  ps_c[tc][:], actf,
                                  scale=scl[wsn][:, ob:ob + 1]))(ob, tc, dst_sb, actf, wsn),
                              deps=[mm, ld_scl[wsn]])
                    last_psc[tc] = ev
                    dw.append(ev)
                ci += 1
            if dstw == "t2":
                last_t2w = dw[-1]
                if _dbg and s_ == 0:
                    last_t2r = p.dma("sync", t2_o[:], t2[:], deps=[last_t2w])
            elif dstw == "t1":
                last_t1w = dw[-1]
        gfm_done = dw[-1]
        for tt in range(SW // 128):
            evs = []
            for dmb in range(DC):
                j = dmb % 2
                tr = p.on("tensor",
                          (lambda dmb, tt, j: lambda e_: e_.transpose(
                              ps_tr[j][:], gfm[:, dmb * SW + tt * 128: dmb * SW + (tt + 1) * 128],
                              id_s[:]))(dmb, tt, j),
                          deps=[gfm_done, ld_id] + ([last_trev[j]] if last_trev[j] else []))
                ev = p.on("scalar",
                          (lambda dmb, j: lambda e_: e_.copy(
                              gtile[:, dmb * 128:(dmb + 1) * 128], ps_tr[j][:]))(dmb, j),
                          deps=[tr])
                last_trev[j] = ev
                evs.append(ev)
            st = p.dma("sync", g_o[tok0 + tt * 128: tok0 + (tt + 1) * 128, :], gtile[:],
                       deps=evs)
            last_trev[0] = st
            last_trev[1] = st
    p.finalize_wait()
    return p


def build_attn(layer):
    p = Prog(f"A{layer}")
    L3 = layer == 3
    NKVB = 65 if L3 else 64
    xnb_i = p.inp("xnb", (DC, 128, S), BF16)
    cl = ["ones_row128", "ones_row512", "masks3" if L3 else "masks", "selT128", "sel128x2", "pickA", "pickB"]
    if layer == 0:
        cl += ["rope_cos2", "rope_sin2", "ropeRT2"]
    if L3:
        cl += ["mask3_last"]
    cc = const_inputs(p, cl)
    wq_i = p.inp("wq", (D, 256), F32)
    wk_i = p.inp("wk", (D, 128), F32)
    if layer == 1:
        wk2_i = p.inp("wk2", (D, 128), F32)
        wv_i = p.inp("wv", (D, 256), F32)
    else:
        wv_i = p.inp("wv", (D, 64), F32)
    if layer == 0:
        bq_i = p.inp("bq", (1, 256), F32)
        bk_i = p.inp("bk", (1, 128), F32)
        bv_i = p.inp("bv", (1, 64), F32)
    if L3:
        nullk_i = p.inp("nullk", (1, 64), F32)
        nullkd_i = p.inp("nullkd", (128, 1), F32)
    ao_o = p.outp("ao", (256, S), BF16)
    import os as _os
    _adbg = bool(_os.environ.get("ADBG"))
    if _adbg:
        oh_o2 = p.outp("oh_dbg", (65, 512), F32)
        ps_o2 = p.outp("ps_dbg", (128, 512), F32)
        pp_o2 = p.outp("pp_dbg", (128, 512), BF16)
        q0_o = p.outp("q0_dbg", (128, S), BF16)
        k0_o = p.outp("k0_dbg", (128, NKVB * 128), BF16)
        v0_o = p.outp("v0_dbg", (128, NKVB * 65), BF16)
        nb_o = p.outp("nb_dbg", (128, 2), F32)

    onesr128 = p.sb("onesr128b", (1, 128), BF16)
    p._onesr128 = onesr128
    ld_o128 = p.dma("sync", onesr128[:], cc["ones_row128"][:])
    p.setup_group_end(512)
    onesr512 = p.sb("onesr512b", (1, 512), BF16)
    ld_o512 = p.dma("sync", onesr512[:], cc["ones_row512"][:])
    SCALE = 0.125
    NC_ = S // 512
    NPAIR_K = 2 if layer == 1 else 1
    NVH = 4 if layer == 1 else 1

    xt = [p.sb("xt0", (128, DC * 512), BF16)] * 2
    xtb = [p.sb("xtb0", (128, DC * 128), BF16)] * 2
    q_s = [p.sb(f"q_s{i}", (128, S), BF16) for i in range(2)]
    k_s = [p.sb(f"k_s{i}", (128, NKVB * 128), BF16) for i in range(NPAIR_K)]
    v_s = [p.sb(f"v_s{i}", (128, NKVB * 65), BF16) for i in range(NVH)]
    wq_s = p.sb("wq_s", (128, DC * 256), BF16)
    wk_s = [p.sb(f"wk_sb{i}", (128, DC * 128), BF16) for i in range(NPAIR_K)]
    wv_cols = 256 if layer == 1 else 64
    wv_s = p.sb("wv_s", (128, DC * wv_cols), BF16)
    out_h = [p.sb(f"out_h{i}", (65, S), F32) for i in range(2)]
    l0_s = p.sb("l0_s", (1, 512), F32)
    linv = p.sb("linv", (1, 512), BF16)
    linv_f = p.sb("linv_f", (1, 512), F32)
    p_sb = [p.sb(f"p_sb{i}", (128, 512), BF16) for i in range(4)]
    mask_s = p.sb("mask_s", (128, 4 * 512), BF16)
    if L3:
        m3l_s = p.sb("m3l_s", (128, 512), BF16)
    selT = p.sb("selT", (2, 128), BF16)
    selC = p.sb("selC", (128, 2), BF16)
    pick = [p.sb(f"pick{i}", (2, 128), BF16) for i in range(2)]
    sq_f = p.sb("sq_f", (128, 512), BF16)
    prodc = p.sb("prodc", (2, 1), F32)
    kn2 = [p.sb(f"kn2_{i}", (2, 1), F32) for i in range(NPAIR_K)]
    qn2 = [p.sb(f"qn2_{i}", (2, 1), F32) for i in range(2)]
    tmp21 = p.sb("tmp21", (2, 1), F32)
    negb = p.sb("negb", (2, 1), BF16)
    negb_col = [p.sb(f"negbc{i}", (128, 1), F32) for i in range(4)]
    vtmp = p.sb("vtmp", (128, wv_cols), F32)
    ao_sb = p.sb("ao_sb", (64, 512), BF16)
    if layer == 0:
        cos_s = p.sb("cos_s", (128, S), BF16)
        sin_s = p.sb("sin_s", (128, S), BF16)
        rot_s = p.sb("rot_s", (128, 128), BF16)
        qraw = p.sb("qraw", (128, 512), BF16)
        rtmp = p.sb("rtmp", (128, 512), F32)
        b_sb = {}
    if L3:
        kv_dram = p.dram("kv_scratch", (S, 64), F32)
        nulld = p.sb("nulld", (128, 1), BF16)

    ps_q = [p.ps(f"ps_q{i}", (128, 512), F32) for i in range(2)]
    ps_s = [p.ps(f"ps_s{i}", (128, 512), F32) for i in range(2)]
    ps_pv = [p.ps(f"ps_pv{i}", (128, 512), F32) for i in range(2)]
    ps_n = p.ps("ps_n", (2, 512), F32)
    ps_bc = p.ps("ps_bc", (128, 512), F32)
    ps_rot = ps_bc if layer == 0 else None

    ld_msk = p.dma("sync", mask_s[:].rearrange("p (a b) -> p a b", a=4),
                   cc["masks3" if L3 else "masks"].rearrange("a p b -> p a b"))
    ld_sel = p.dma("sync", selT[:], cc["selT128"][:])
    ld_selc = p.dma("sync", selC[:], cc["sel128x2"][:])
    ld_pk = [p.dma("sync", pick[0][:], cc["pickA"][:]),
             p.dma("sync", pick[1][:], cc["pickB"][:])]
    lds = [ld_o128, ld_o512, ld_msk, ld_sel, ld_selc] + ld_pk
    ld_wq = p.dma("gpsimd", wq_s[:].rearrange("p (c f) -> p c f", c=DC),
                  wq_i.rearrange("(c p) f -> p c f", p=128))
    ld_wk = [p.dma("gpsimd", wk_s[0][:].rearrange("p (c f) -> p c f", c=DC),
                   wk_i.rearrange("(c p) f -> p c f", p=128))]
    ld_wv = p.dma("gpsimd", wv_s[:].rearrange("p (c f) -> p c f", c=DC),
                  wv_i.rearrange("(c p) f -> p c f", p=128))
    lds += [ld_wq, ld_wk[0], ld_wv]
    if layer == 1:
        ld_wk.append(p.dma("gpsimd", wk_s[1][:].rearrange("p (c f) -> p c f", c=DC),
                           wk2_i.rearrange("(c p) f -> p c f", p=128)))
        lds.append(ld_wk[1])
    if layer == 0:
        for nm, ap_, w in (("bq", bq_i, 256), ("bk", bk_i, 128), ("bv", bv_i, 64)):
            b_sb[nm] = p.sb(nm + "_sb", (1, w), BF16)
            lds.append(p.dma("gpsimd", b_sb[nm][:], ap_[:]))
        lds.append(p.dma("sync", cos_s[:], cc["rope_cos2"][:]))
        lds.append(p.dma("sync", sin_s[:], cc["rope_sin2"][:]))
        lds.append(p.dma("sync", rot_s[:], cc["ropeRT2"][:]))
    if L3:
        ld_null = p.dma("gpsimd", nulld[:], nullkd_i[:])
        lds.append(ld_null)

    vz = [p.on("vector", (lambda i: lambda e: e.memset(v_s[i][:], 1.0))(i))
          for i in range(NVH)]
    kz = [p.on("vector", (lambda i: lambda e: e.memset(k_s[i][:], 0.0))(i))
          for i in range(NPAIR_K)]

    def rope_emit(dst_sb, dst_off, src_bf, rotdep, evdep):
        steps = []
        def f1(e):
            return e.tensor_mul(rtmp[:, :], ps_rot[:, :], sin_s[:, dst_off:dst_off + 512])
        def f2(e):
            return e.scalar_tensor_tensor(
                dst_sb[:, dst_off:dst_off + 512], src_bf[:, :], 1.0,
                cos_s[:, dst_off:dst_off + 512], ALU.mult, ALU.mult)
        def f3(e):
            return e.tensor_add(dst_sb[:, dst_off:dst_off + 512],
                                dst_sb[:, dst_off:dst_off + 512], rtmp[:, :])
        return p.chain("vector", [f1, f2, f3], deps=[rotdep, evdep])

    last_evq = None
    last_evk = None
    prev_use = None
    for tc in range(NC_):
        bi = 0
        ldx = p.dma("sync", xt[bi][:].rearrange("p (c f) -> p c f", c=DC),
                    xnb_i[:, :, tc * 512:(tc + 1) * 512].rearrange("c p f -> p c f"),
                    deps=[prev_use] if prev_use else [])
        uses = []
        for pr in range(2):
            def qmm(bi, pr):
                def f(e_):
                    for dc in range(DC):
                        e_.matmul(ps_q[0][:],
                                  wq_s[:, dc * 256 + pr * 128: dc * 256 + pr * 128 + 128],
                                  xt[bi][:, dc * 512:(dc + 1) * 512],
                                  start=(dc == 0), stop=False)
                    if layer == 0:
                        return e_.matmul(ps_q[0][:],
                                         b_sb["bq"][0:1, pr * 128: pr * 128 + 128],
                                         onesr512[:], start=False, stop=True)
                    return p.end_group(e_, ps_q[0][:], 128, 512)
                return f
            mmq = p.on("tensor", qmm(bi, pr),
                       deps=[ldx, ld_wq] + p._ge_dep + ([last_evq] if last_evq else lds))
            if layer == 0:
                ev0 = p.on("scalar", lambda e: e.copy(qraw[:], ps_q[0][:]),
                           deps=[mmq] + ([last_evq] if last_evq else []))
                mmr = p.on("tensor", lambda e: e.matmul(ps_rot[:], rot_s[:], qraw[:],
                                                        start=True, stop=True), deps=[ev0])
                evq = rope_emit(q_s[pr], tc * 512, qraw, mmr, ev0)
            else:
                evq = p.on("scalar",
                           (lambda pr, tc: lambda e: e.copy(
                               q_s[pr][:, tc * 512:(tc + 1) * 512], ps_q[0][:]))(pr, tc),
                           deps=[mmq])
            last_evq = evq
            uses.append(evq)
        for ki in range(NPAIR_K):
            kcol0 = 1 if L3 else 0
            def kmm(bi, ki):
                def f(e_):
                    for dc in range(DC):
                        e_.matmul(ps_q[1][:], wk_s[ki][:, dc * 128: dc * 128 + 128],
                                  xt[bi][:, dc * 512:(dc + 1) * 512],
                                  start=(dc == 0), stop=False)
                    if layer == 0:
                        return e_.matmul(ps_q[1][:], b_sb["bk"][:], onesr512[:],
                                         start=False, stop=True)
                    return p.end_group(e_, ps_q[1][:], 128, 512)
                return f
            mmk = p.on("tensor", kmm(bi, ki),
                       deps=[ldx, ld_wk[ki]] + p._ge_dep + kz +
                            ([last_evk] if last_evk else []))
            if layer == 0:
                ev0 = p.on("scalar", lambda e: e.copy(qraw[:], ps_q[1][:]),
                           deps=[mmk, last_evq])
                mmr = p.on("tensor", lambda e: e.matmul(ps_rot[:], rot_s[:], qraw[:],
                                                        start=True, stop=True), deps=[ev0])
                evk = rope_emit(k_s[0], tc * 512, qraw, mmr, ev0)
            else:
                evk = p.on("scalar",
                           (lambda ki, tc, kcol0: lambda e: e.copy(
                               k_s[ki][:, kcol0 + tc * 512: kcol0 + tc * 512 + 512],
                               ps_q[1][:]))(ki, tc, kcol0),
                           deps=[mmk])
            last_evk = evk
            uses.append(evk)
        prev_use = uses[-1]
    if L3:
        last_evk = p.chain("vector", [
            lambda e: e.tensor_copy(k_s[0][:, 0:1], nulld[:]),
        ], deps=[ld_null, last_evk])

    last_vp = None
    vparts = []
    for blk in range(64):
        bi = 0
        ldxb = p.dma("sync", xtb[bi][:].rearrange("p (c f) -> p c f", c=DC),
                     xnb_i[:, :, blk * 128:(blk + 1) * 128].rearrange("c p f -> p c f"),
                     deps=[last_vp] if last_vp else [])
        def vmm(bi):
            def f(e_):
                for dc in range(DC):
                    e_.matmul(ps_q[1][:, 0:wv_cols],
                              xtb[bi][:, dc * 128:(dc + 1) * 128],
                              wv_s[:, dc * wv_cols:(dc + 1) * wv_cols],
                              start=(dc == 0), stop=False)
                if layer == 0:
                    return e_.matmul(ps_q[1][:, 0:wv_cols], onesr128[:], b_sb["bv"][:],
                                     start=False, stop=True)
                return p.end_group(e_, ps_q[1][:, 0:wv_cols], 128, wv_cols)
            return f
        mmv = p.on("tensor", vmm(bi),
                   deps=[ldxb, ld_wv] + p._ge_dep + ([last_vp] if last_vp else [last_evk]))
        if L3:
            evv = p.on("vector", lambda e: e.tensor_copy(vtmp[:], ps_q[1][:, 0:64]),
                       deps=[mmv] + ([last_vp] if last_vp else []))
            std = p.dma("sync", kv_dram[blk * 128:(blk + 1) * 128, :], vtmp[:, 0:64],
                        deps=[evv])
            last_vp = std
        else:
            evs = []
            for h in range(NVH):
                evv = p.on("scalar",
                           (lambda h, blk: lambda e: e.copy(
                               v_s[h][:, blk * 65: blk * 65 + 64],
                               ps_q[1][:, h * 64:(h + 1) * 64]))(h, blk),
                           deps=[mmv] + vz + ([last_vp] if last_vp else []))
                evs.append(evv)
            last_vp = evs[-1]
        vparts.append(last_vp)
    if L3:
        d0 = p.dma("gpsimd", v_s[0][0:1, 0:64], nullk_i[:], deps=[last_vp] + vz)
        d1 = p.dma("gpsimd", v_s[0][1:128, 0:64], kv_dram[0:127, :], deps=[d0])
        last = d1
        for blk in range(1, 64):
            last = p.dma("gpsimd", v_s[0][:, blk * 65: blk * 65 + 64],
                         kv_dram[blk * 128 - 1: blk * 128 + 127, :], deps=[last])
        last = p.dma("gpsimd", v_s[0][0:1, 64 * 65: 64 * 65 + 64],
                     kv_dram[S - 1: S, :], deps=[last])
        last_vp = last
        vparts.append(last_vp)

    def norms_emit(src_sb, ncols, dst, deps):
        z = p.on("vector", (lambda dst: lambda e: e.memset(dst[:], 0.0))(dst), deps=deps)
        last = z
        lastn = None
        for ch in range(ncols // 512):
            sq = p.on("scalar",
                      (lambda ch: lambda e: e.activation(
                          sq_f[:], src_sb[:, ch * 512:(ch + 1) * 512], AF.Square))(ch),
                      deps=[last] if last else [])
            mm = p.on("tensor", lambda e: e.matmul(ps_n[:], selC[:], sq_f[:],
                                                   start=True, stop=True),
                      deps=[sq, ld_selc] + ([lastn] if lastn else []))
            last = p.chain("vector", [
                lambda e: e.tensor_reduce(tmp21[:], ps_n[:], AX.X, ALU.max),
                (lambda dst: lambda e: e.tensor_max(dst[:], dst[:], tmp21[:]))(dst),
            ], deps=[mm])
            lastn = last
        return last
    nq = []
    nk = []
    prevn = None
    for pr in range(2):
        nq.append(norms_emit(q_s[pr], S, qn2[pr],
                             [last_evq, last_evk] + ([prevn] if prevn else [])))
        prevn = nq[-1]
    for ki in range(NPAIR_K):
        nk.append(norms_emit(k_s[ki], NKVB * 128, kn2[ki], [prevn]))
        prevn = nk[-1]

    bias_done = []
    for pr in range(2):
        ki = pr if layer == 1 else 0
        vch = p.chain("vector", [
            (lambda pr, ki: lambda e: e.tensor_mul(prodc[:], qn2[pr][:], kn2[ki][:]))(pr, ki),
        ], deps=[nq[pr], nk[ki]] + (bias_done[-1:] if bias_done else []))
        sch = p.on("scalar", lambda e: e.activation(prodc[:], prodc[:], AF.Sqrt), deps=[vch])
        vch2 = p.chain("vector", [
            lambda e: e.tensor_scalar_mul(prodc[:], prodc[:], -1.02 * SCALE),
            lambda e: e.tensor_copy(negb[:], prodc[:]),
        ], deps=[sch])
        for half in range(2):
            mmb = p.on("tensor",
                       (lambda half: lambda e: e.matmul(ps_bc[:, 0:1], pick[half][:],
                                                        negb[:], start=True, stop=True))(half),
                       deps=[vch2] + ld_pk + (bias_done[-1:] if bias_done else []))
            evb = p.on("vector",
                       (lambda pr, half: lambda e: e.tensor_copy(
                           negb_col[2 * pr + half][:], ps_bc[:, 0:1]))(pr, half),
                       deps=[mmb])
            bias_done.append(evb)

    if _adbg:
        p.dma("sync", q0_o[:], q_s[0][:], deps=[last_evq, last_evk])
        p.dma("sync", k0_o[:], k_s[0][:], deps=[last_evq, last_evk])
        p.dma("sync", v0_o[:], v_s[0][:], deps=[last_vp] + vparts[:1])
        nbpk = p.sb("nbpk", (128, 2), F32)
        nbc = p.chain("vector", [
            lambda e: e.memset(nbpk[:], 0.0),
            lambda e: e.tensor_copy(nbpk[0:2, 0:1], negb[:]),
            lambda e: e.tensor_copy(nbpk[:, 1:2], negb_col[0][:]),
        ], deps=bias_done)
        p.dma("sync", nb_o[:], nbpk[:], deps=[nbc])
    sbank = {(h, 0): ps_s[h] for h in range(2)}
    sbank.update({(h, 1): ps_q[h] for h in range(2)})
    last_exp = {}
    last_pv_p = {}
    last_pmaskv = None
    last_fin = None
    it = 0
    for pr in range(2):
        hA, hB = 2 * pr, 2 * pr + 1
        ki = pr if layer == 1 else 0
        pv_evac = {}
        for half in range(2):
            pv_evac[half] = p.on("vector",
                                 (lambda half: lambda e: e.memset(out_h[half][:], 0.0))(half),
                                 deps=[last_fin] if last_fin else [])
        for g in range((NKVB + 7) // 8):
            blo = g * 8
            bhi = min(NKVB, blo + 8)
            for c in range(2 * g, NC_):
                blks = [b for b in range(blo, bhi) if b <= 4 * c + 3]
                if not blks:
                    continue
                first = True
                for b in blks:
                    par = it % 2
                    it += 1
                    def smm(pr, ki, b, c, half, par):
                        def f(e_):
                            return e_.matmul(
                                sbank[(half, par)][:],
                                k_s[ki][half * 64:(half + 1) * 64, b * 128:(b + 1) * 128],
                                q_s[pr][half * 64:(half + 1) * 64, c * 512:(c + 1) * 512],
                                start=True, stop=True,
                                tile_position=(half * 64, 0))
                        return f
                    sdeps = [last_evq, last_evk, last_vp] + bias_done
                    mms = []
                    for half in range(2):
                        dd = sdeps + ([last_exp[(half, par)]] if (half, par) in last_exp else [])
                        mms.append(p.on("tensor", smm(pr, ki, b, c, half, par), deps=dd))
                    exps = []
                    for half in range(2):
                        ex = p.on("scalar",
                                  (lambda half, pr, par: lambda e: e.activation(
                                      p_sb[2 * half + par][:], sbank[(half, par)][:], AF.Exp,
                                      bias=negb_col[2 * pr + half][:], scale=SCALE))(half, pr, par),
                                  deps=[mms[half]] +
                                       ([last_pv_p[(half, par)]] if (half, par) in last_pv_p else []))
                        exps.append(ex)
                        last_exp[(half, par)] = ex
                    pdeps = []
                    r = b - 4 * c
                    for half in range(2):
                        if L3 and b == 64:
                            mfn = (lambda half, par: lambda e: e.tensor_mul(
                                p_sb[2 * half + par][:], p_sb[2 * half + par][:], m3l_s[:]))(half, par)
                        elif 0 <= r <= 3:
                            mfn = (lambda half, r, par: lambda e: e.tensor_mul(
                                p_sb[2 * half + par][:], p_sb[2 * half + par][:],
                                mask_s[:, r * 512:(r + 1) * 512]))(half, r, par)
                        else:
                            mfn = None
                        if mfn is not None:
                            pm = p.on("vector", mfn, deps=[exps[half], ld_msk] +
                                      ([last_pmaskv] if last_pmaskv else []))
                            last_pmaskv = pm
                            pdeps.append(pm)
                        else:
                            pdeps.append(exps[half])
                    if _adbg and pr == 0 and g == int(_os.environ.get("DG", "0")) \
                            and c == int(_os.environ.get("DCk", "0")) and b == int(_os.environ.get("DB", "0")):
                        snp = p.sb("snp", (128, 512), F32)
                        sv = p.on("vector", lambda e: e.tensor_copy(snp[:], ps_s[0][:]),
                                  deps=[mms[0], pdeps[0]])
                        p.dma("sync", ps_o2[:], snp[:], deps=[sv])
                        dpp = p.dma("sync", pp_o2[:], p_sb[0][:], deps=[pdeps[0]])
                        pdeps[0] = dpp
                    for half in range(2):
                        def pvmm(pr, half, b, first, lastb, par):
                            def f(e_):
                                e_.matmul(ps_pv[half][0:65, :],
                                          v_s[0 if NVH == 1 else (2 * pr + half)][:, b * 65:(b + 1) * 65],
                                          p_sb[2 * half + par][:],
                                          start=first, stop=False)
                                if lastb:
                                    return p.end_group(e_, ps_pv[half][0:65, :], 65, 512)
                                return None
                            return f
                        lastb = (b == blks[-1])
                        dd = [pdeps[half], last_vp] + p._ge_dep
                        if first and (half, "acc") in last_pv_p:
                            dd.append(last_pv_p[(half, "acc")])
                        pv = p.on("tensor", pvmm(pr, half, b, first, lastb, par), deps=dd)
                        last_pv_p[(half, par)] = pv
                        if lastb:
                            ad = p.on("vector",
                                      (lambda half, c: lambda e: e.tensor_add(
                                          out_h[half][:, c * 512:(c + 1) * 512],
                                          ps_pv[half][0:65, :],
                                          out_h[half][:, c * 512:(c + 1) * 512]))(half, c),
                                      deps=[pv, pv_evac[half]])
                            pv_evac[half] = ad
                            last_pv_p[(half, "acc")] = ad
                            if _adbg and pr == 0 and half == 0 and g == int(_os.environ.get("DG", "0")) \
                                    and c == int(_os.environ.get("DCk", "0")):
                                dx = p.dma("sync", oh_o2[:], out_h[0][0:65, c * 512:(c + 1) * 512],
                                           deps=[ad])
                                pv_evac[half] = dx
                    first = False
        for half in range(2):
            h_ = 2 * pr + half
            for c in range(NC_):
                dmal = p.dma("sync", l0_s[:], out_h[half][64:65, c * 512:(c + 1) * 512],
                             deps=[pv_evac[half]] + ([last_fin] if last_fin else []))
                rv = p.chain("vector", [
                    lambda e: e.reciprocal(linv_f[:], l0_s[:]),
                    lambda e: e.tensor_copy(linv[:], linv_f[:]),
                ], deps=[dmal] + ([last_fin] if last_fin else []))
                mmb = p.on("tensor",
                           lambda e: e.matmul(ps_bc[0:64, :], onesr128[0:1, 0:64],
                                              linv[0:1, :], start=True, stop=True),
                           deps=[rv, ld_o128] + ([last_fin] if last_fin else []))
                mv = p.on("vector",
                          (lambda half, c: lambda e: e.tensor_mul(
                              ao_sb[:], out_h[half][0:64, c * 512:(c + 1) * 512],
                              ps_bc[0:64, :]))(half, c),
                          deps=[mmb] + ([last_fin] if last_fin else []))
                st = p.dma("sync", ao_o[h_ * 64:(h_ + 1) * 64, c * 512:(c + 1) * 512],
                           ao_sb[:], deps=[mv])
                last_fin = st
    p.finalize_wait()
    return p


def build_attn2():
    p = Prog("A2")
    cc = const_inputs(p, ["ones_row128", "ones_row512"])
    xnf, ld_xnf = _load_xnf(p)
    xs_i = p.inp("xs", (T, D), F32)
    scp_i = p.inp("scp", (4, 128, 256), F32)
    wv_i = p.inp("wv", (D, 512), F32)
    bv_i = p.inp("bv", (1, 512), F32)
    wfc_i = p.inp("wfc", (512, D), F32)
    bfc_i = p.inp("bfc", (1, D), F32)
    gm_i = p.inp("gm", (1, D), F32)
    bt_i = p.inp("bt", (1, D), F32)
    s_o = p.outp("s", (T, D), F32)
    onesr128 = p.sb("onesr128b", (1, 128), BF16)
    p._onesr128 = onesr128
    ld_o128 = p.dma("sync", onesr128[:], cc["ones_row128"][:])
    onesr512 = p.sb("onesr512b", (1, 512), BF16)
    ld_o512 = p.dma("sync", onesr512[:], cc["ones_row512"][:])
    p.setup_group_end(512)

    scp_s = p.sb("scp_s", (128, 4 * 256), F32)
    sc = p.sb("sc", (128, 256), F32)
    attn_bf = p.sb("attn_bf", (128, 256), BF16)
    mx = p.sb("mx", (128, 1), F32)
    nmx = p.sb("nmx", (128, 1), F32)
    smc = p.sb("smc", (128, 1), F32)
    rsc = p.sb("rsc", (128, 1), F32)
    att_f = p.sb("att_f", (128, 64), F32)
    wv_s = p.sb("wv_s", (128, DC * 512), BF16)
    bv_s = p.sb("bv_s", (1, 512), BF16)
    wfc_s = p.sb("wfc_s", (128, 4 * D), BF16)
    bfc_s = p.sb("bfc_s", (1, D), BF16)
    gm_s = p.sb("gm_s", (1, D), BF16)
    bt_s = p.sb("bt_s", (1, D), BF16)
    gmb = p.sb("gmb", (128, D), F32)
    btb = p.sb("btb", (128, D), F32)
    vf = p.sb("vf", (128, 4 * T), BF16)
    of = p.sb("of", (128, 4 * T), BF16)
    xt_ = p.sb("xt_", (128, D), F32)
    xsq_ = p.sb("xsq_", (128, D), F32)
    xn_tm = p.sb("xn_tm", (128, D), F32)
    z_sb = p.sb("z_sb", (128, D), F32)
    s_sb = p.sb("s_sb", (128, D), F32)
    cA = p.sb("cA", (128, 1), F32)
    cB = p.sb("cB", (128, 1), F32)
    cC = p.sb("cC", (128, 1), F32)
    cD = p.sb("cD", (128, 1), F32)
    cE = p.sb("cE", (128, 1), F32)
    cF = p.sb("cF", (128, 1), F32)
    ps_a = [p.ps(f"ps_a{i}", (128, 512), F32) for i in range(2)]
    ps_b = p.ps("ps_b", (128, 512), F32)

    ld_sc = p.dma("sync", scp_s[:].rearrange("p (a b) -> p a b", a=4),
                  scp_i.rearrange("a p b -> p a b"))
    ld_wv = p.dma("gpsimd", *fm3(wv_s, wv_i, 512))
    ld_bv = p.dma("gpsimd", bv_s[:], bv_i[:])
    ld_wfc = p.dma("gpsimd", wfc_s[:].rearrange("p (c d) -> p c d", c=4),
                   wfc_i.rearrange("(c p) d -> p c d", p=128))
    ld_bfc = p.dma("gpsimd", bfc_s[:], bfc_i[:])
    ld_gm = p.dma("gpsimd", gm_s[:], gm_i[:])
    ld_bt = p.dma("gpsimd", bt_s[:], bt_i[:])

    gb = []
    for half in range(2):
        for (src_r, dstb) in ((gm_s, gmb), (bt_s, btb)):
            mm = p.on("tensor",
                      (lambda src_r, half: lambda e: e.matmul(
                          ps_b[:], onesr128[:], src_r[0:1, half * 512: half * 512 + 512],
                          start=True, stop=True))(src_r, half),
                      deps=[ld_gm, ld_bt, ld_o128] + gb[-1:])
            ev = p.on("vector",
                      (lambda dstb, half: lambda e: e.tensor_copy(
                          dstb[:, half * 512: half * 512 + 512], ps_b[:]))(dstb, half),
                      deps=[mm])
            gb.append(ev)
    gb_done = gb[-1]

    vsum = p.chain("vector", [
        lambda e: e.tensor_add(sc[:], scp_s[:, 0:256], scp_s[:, 256:512]),
        lambda e: e.tensor_add(sc[:], sc[:], scp_s[:, 512:768]),
        lambda e: e.tensor_add(sc[:], sc[:], scp_s[:, 768:1024]),
    ], deps=[ld_sc])
    at_done = []
    for qd in range(4):
        sl = slice(qd * 64, qd * 64 + 64)
        vm = p.chain("vector", [
            (lambda sl: lambda e: e.reduce_max(mx[:], sc[:, sl], axis=AX.X))(sl),
            lambda e: e.tensor_scalar_mul(nmx[:], mx[:], -0.125),
        ], deps=[vsum] + at_done[-1:])
        se = p.on("scalar",
                  (lambda sl: lambda e: e.activation(att_f[:], sc[:, sl], AF.Exp,
                                                     bias=nmx[:], scale=0.125,
                                                     accum_out=smc[:]))(sl),
                  deps=[vm])
        vn = p.chain("vector", [
            lambda e: e.reciprocal(rsc[:], smc[:]),
            (lambda sl: lambda e: e.tensor_scalar_mul(att_f[:], att_f[:], rsc[:]))(sl),
            (lambda sl: lambda e: e.tensor_copy(attn_bf[:, sl], att_f[:]))(sl),
        ], deps=[se])
        at_done.append(vn)
    at_done = at_done[-1]

    vev = None
    for vc in range(4):
        for tc in range(4):
            def vmm(vc, tc):
                def f(e_):
                    for dc in range(DC):
                        e_.matmul(ps_a[0][:],
                                  wv_s[:, dc * 512 + vc * 128: dc * 512 + (vc + 1) * 128],
                                  xnf[:, dc * T + tc * 512: dc * T + (tc + 1) * 512],
                                  start=(dc == 0), stop=False)
                    return e_.matmul(ps_a[0][:], bv_s[0:1, vc * 128:(vc + 1) * 128],
                                     onesr512[:], start=False, stop=True)
                return f
            mm = p.on("tensor", vmm(vc, tc), deps=[ld_wv, ld_bv, ld_xnf, ld_o512] +
                      ([vev] if vev else []))
            vev = p.on("scalar",
                       (lambda vc, tc: lambda e: e.copy(
                           vf[:, vc * T + tc * 512: vc * T + (tc + 1) * 512],
                           ps_a[0][:]))(vc, tc),
                       deps=[mm])

    oev = None
    for ec in range(4):
        for tc in range(4):
            def omm(ec, tc, half):
                def f(e_):
                    return e_.matmul(ps_a[1][half * 64:half * 64 + 64, :],
                                     attn_bf[half * 64:half * 64 + 64,
                                             ec * 64:(ec + 1) * 64],
                                     vf[half * 64:half * 64 + 64,
                                        ec * T + tc * 512: ec * T + (tc + 1) * 512],
                                     start=True, stop=True,
                                     tile_position=(half * 64, half * 64))
                return f
            mms = []
            for half in range(2):
                mms.append(p.on("tensor", omm(ec, tc, half),
                                deps=[at_done, vev] + ([oev] if oev else [])))
            oev = p.on("scalar",
                       (lambda ec, tc: lambda e: e.copy(
                           of[:, ec * T + tc * 512: ec * T + (tc + 1) * 512],
                           ps_a[1][:]))(ec, tc),
                       deps=mms)

    prev = None
    for t in range(NT):
        ldx = p.dma("sync", xt_[:], xs_i[t * 128:(t + 1) * 128, :],
                    deps=[prev] if prev else [])
        s_sq = p.on("scalar", lambda e: e.activation(xsq_[:], xt_[:], AF.Square),
                    deps=[ldx] + ([prev] if prev else []))
        v_st = p.chain("vector", [
            lambda e: e.reduce_sum(cA[:], xt_[:], axis=AX.X),
            lambda e: e.reduce_sum(cB[:], xsq_[:], axis=AX.X),
            lambda e: e.tensor_scalar_mul(cC[:], cA[:], 1.0 / D),
            lambda e: e.tensor_scalar_mul(cD[:], cB[:], 1.0 / D),
            lambda e: e.tensor_mul(cB[:], cC[:], cC[:]),
            lambda e: e.tensor_sub(cD[:], cD[:], cB[:]),
            lambda e: e.tensor_scalar_add(cD[:], cD[:], LN_EPS),
        ], deps=[ldx, s_sq])
        s_sq2 = p.on("scalar", lambda e: e.activation(cF[:], cD[:], AF.Sqrt), deps=[v_st])
        v_n = p.chain("vector", [
            lambda e: e.reciprocal(cE[:], cF[:]),
            lambda e: e.tensor_scalar(xn_tm[:], xt_[:], cC[:], cE[:], ALU.subtract, ALU.mult),
        ], deps=[s_sq2])
        zdeps = []
        for half in range(2):
            def pmm(t, half):
                def f(e_):
                    for ec in range(4):
                        e_.matmul(ps_a[half][:],
                                  of[:, ec * T + t * 128: ec * T + (t + 1) * 128],
                                  wfc_s[:, ec * D + half * 512: ec * D + half * 512 + 512],
                                  start=(ec == 0), stop=False)
                    return e_.matmul(ps_a[half][:], onesr128[:],
                                     bfc_s[0:1, half * 512: half * 512 + 512],
                                     start=False, stop=True)
                return f
            mm = p.on("tensor", pmm(t, half),
                      deps=[oev, ld_wfc, ld_bfc] + ([prev] if prev else []))
            ad = p.on("vector",
                      (lambda t, half: lambda e: e.tensor_add(
                          z_sb[:, half * 512: half * 512 + 512], ps_a[half][:],
                          xn_tm[:, half * 512: half * 512 + 512]))(t, half),
                      deps=[mm, v_n])
            zdeps.append(ad)
        s_sq3 = p.on("scalar", lambda e: e.activation(xsq_[:], z_sb[:], AF.Square),
                     deps=zdeps)
        v_st2 = p.chain("vector", [
            lambda e: e.reduce_sum(cA[:], z_sb[:], axis=AX.X),
            lambda e: e.reduce_sum(cB[:], xsq_[:], axis=AX.X),
            lambda e: e.tensor_scalar_mul(cC[:], cA[:], 1.0 / D),
            lambda e: e.tensor_scalar_mul(cD[:], cB[:], 1.0 / D),
            lambda e: e.tensor_mul(cB[:], cC[:], cC[:]),
            lambda e: e.tensor_sub(cD[:], cD[:], cB[:]),
            lambda e: e.tensor_scalar_add(cD[:], cD[:], 1e-6),
        ], deps=zdeps + [s_sq3])
        s_sq4 = p.on("scalar", lambda e: e.activation(cF[:], cD[:], AF.Sqrt), deps=[v_st2])
        v_fin = p.chain("vector", [
            lambda e: e.reciprocal(cE[:], cF[:]),
            lambda e: e.tensor_scalar(s_sb[:], z_sb[:], cC[:], cE[:], ALU.subtract, ALU.mult),
            lambda e: e.tensor_mul(s_sb[:], s_sb[:], gmb[:]),
            lambda e: e.tensor_add(s_sb[:], s_sb[:], btb[:]),
        ], deps=[s_sq4, gb_done])
        prev = p.dma("sync", s_o[t * 128:(t + 1) * 128, :], s_sb[:], deps=[v_fin])
    p.finalize_wait()
    return p



def build_resid(layer):
    p = Prog(f"R{layer}")
    xs_i = p.inp("xs", (T, D), F32)
    g_i = p.inp("g", (T, D), F32)
    xo_o = p.outp("xo", (T, D), F32)
    if layer == 2:
        s_i = p.inp("s", (T, D), F32)
    else:
        cc = const_inputs(p, ["ones_row128"])
        aof_i = p.inp("aof", (DC, 128, T), BF16)
        wo_i = p.inp("wo", (D, D), F32)
        onesr128 = p.sb("onesr128b", (1, 128), BF16)
        p._onesr128 = onesr128
        ld_o128 = p.dma("sync", onesr128[:], cc["ones_row128"][:])
        p.setup_group_end(512)
        if layer == 0:
            bo_i = p.inp("bo", (1, D), F32)
            bo_s = p.sb("bo_s", (1, D), BF16)
            ld_bo = p.dma("gpsimd", bo_s[:], bo_i[:])
        aof = p.sb("aof_s", (128, DC * T), BF16)
        wo_s = p.sb("wo_s", (128, DC * D), BF16)
        ld_ao = p.dma("sync", aof[:].rearrange("p (c t) -> p c t", c=DC),
                      aof_i.rearrange("c p t -> p c t"))
        ld_wo = p.dma("gpsimd", *fm3(wo_s, wo_i, D))
        ps_s = [p.ps(f"ps_s{i}", (128, 512), F32) for i in range(2)]
    xt_ = p.sb("xt_", (128, D), F32)
    gt_ = p.sb("gt_", (128, D), F32)
    st_ = p.sb("st_", (128, D), F32)
    xo_sb = p.sb("xo_sb", (128, D), F32)
    prev = None
    for t in range(NT):
        dl = [prev] if prev else []
        ldx = p.dma("sync", xt_[:], xs_i[t * 128:(t + 1) * 128, :], deps=dl)
        ldg = p.dma("sync", gt_[:], g_i[t * 128:(t + 1) * 128, :], deps=dl)
        if layer == 2:
            lds_ = p.dma("sync", st_[:], s_i[t * 128:(t + 1) * 128, :], deps=dl)
            v_fin = p.chain("vector", [
                lambda e: e.tensor_mul(st_[:], st_[:], gt_[:]),
                lambda e: e.tensor_add(xo_sb[:], xt_[:], st_[:]),
            ], deps=[ldx, ldg, lds_])
        else:
            vparts = []
            for half in range(2):
                def pmm(t, half):
                    def f(e_):
                        for fc in range(DC):
                            e_.matmul(ps_s[half][:],
                                      aof[:, fc * T + t * 128: fc * T + (t + 1) * 128],
                                      wo_s[:, fc * D + half * 512: fc * D + half * 512 + 512],
                                      start=(fc == 0), stop=False)
                        if layer == 0:
                            return e_.matmul(ps_s[half][:], onesr128[:],
                                             bo_s[0:1, half * 512: half * 512 + 512],
                                             start=False, stop=True)
                        return p.end_group(e_, ps_s[half][:], 128, 512)
                    return f
                deps = [ld_ao, ld_wo] + p._ge_dep + dl
                if layer == 0:
                    deps.append(ld_bo)
                mm = p.on("tensor", pmm(t, half), deps=deps)
                vch = p.chain("vector", [
                    (lambda t, half: lambda e: e.tensor_mul(
                        st_[:, half * 512: half * 512 + 512], ps_s[half][:],
                        gt_[:, half * 512: half * 512 + 512]))(t, half),
                    (lambda t, half: lambda e: e.tensor_add(
                        xo_sb[:, half * 512: half * 512 + 512],
                        xt_[:, half * 512: half * 512 + 512],
                        st_[:, half * 512: half * 512 + 512]))(t, half),
                ], deps=[mm, ldx, ldg])
                vparts.append(vch)
            v_fin = vparts[-1]
        prev = p.dma("sync", xo_o[t * 128:(t + 1) * 128, :], xo_sb[:], deps=[v_fin] +
                     (vparts if layer != 2 else []))
    p.finalize_wait()
    return p


_NC_CACHE = {}

def _get_nc(name, builder):
    if name not in _NC_CACHE:
        _NC_CACHE[name] = builder().build()
    return _NC_CACHE[name]


def _run(nc, in_maps):
    res = run_bass_kernel_spmd(nc, in_maps, core_ids=list(range(NCORES)))
    return res.results


def kernel(**inp):
    f32 = np.float32
    TT = NCORES * T
    x = np.ascontiguousarray(inp["L0_attn_x"].astype(f32, copy=False).reshape(TT, D))
    cN = lambda *names: const_in_map(list(names))
    CONST_S = cN("ident", "iota8", "iota8p", "ones_col", "ones_row128")
    zeros8 = np.zeros((1, E), f32)
    sL = {0: dict(wr=np.ascontiguousarray(inp["L0_ffn_Wr"]), gbias=zeros8),
          1: dict(wr=np.ascontiguousarray(inp["L1_ffn_Wr"]),
                  gbias=np.ascontiguousarray(inp["L1_ffn_expert_bias"].reshape(1, E))),
          2: dict(wq=np.ascontiguousarray(inp["L2_attn_w_qkv"][:, 0:512]),
                  wk=np.ascontiguousarray(inp["L2_attn_w_qkv"][:, 512:1024]),
                  bq=np.ascontiguousarray(inp["L2_attn_b_qkv"][0:512].reshape(1, 512)),
                  bk=np.ascontiguousarray(inp["L2_attn_b_qkv"][512:1024].reshape(1, 512))),
          3: dict(wr=np.ascontiguousarray(inp["L3_ffn_Wr"]),
                  gbias=np.ascontiguousarray(inp["L3_ffn_br"].reshape(1, E)))}

    for layer in range(4):
        nc = _get_nc(f"S{layer}", lambda: build_stats(layer))
        ims = [{"xs": x[c * T:(c + 1) * T], **CONST_S, **sL[layer]} for c in range(NCORES)]
        rs = _run(nc, ims)
        xnf = [np.asarray(r["xnf"]) for r in rs]
        xnb = [np.concatenate([xnf[b * 4 + j] for j in range(4)], axis=2) for b in range(2)]

        if layer == 0:
            ncf = _get_nc("F0", lambda: build_ffn0())
            tots = np.stack([np.asarray(rs[c]["tot"])[0] for c in range(NCORES)])
            totst = np.ascontiguousarray(tots.T)
            imf = []
            for c in range(NCORES):
                mb = np.zeros((16, NCORES), f32)
                mb[:, :c] = 1.0
                imf.append({"xnf": xnf[c], "w1": inp["L0_ffn_W1"], "b1": inp["L0_ffn_b1"],
                            "w2": inp["L0_ffn_W2"], "b2": inp["L0_ffn_b2"],
                            "oh": np.asarray(rs[c]["oh"]), "tv": np.asarray(rs[c]["tv"]),
                            "totst": totst, "maskb": mb,
                            **cN("ones_row512", "ones_row128", "ltri_s", "ones_col")})
            g = [np.asarray(r["g"]) for r in _run(ncf, imf)]
        elif layer == 1:
            ncf = _get_nc("F1", lambda: build_ffn1())
            imf = [{"xnf": xnf[c], "wg": inp["L1_ffn_Wg"], "wu": inp["L1_ffn_Wu"],
                    "wd": inp["L1_ffn_Wd"], "sg": inp["L1_ffn_Sg"], "su": inp["L1_ffn_Su"],
                    "sd": inp["L1_ffn_Sd"], "rw": np.asarray(rs[c]["rw"]),
                    **cN("ones_row512", "ones_row128", "ones_col_f32")}
                   for c in range(NCORES)]
            g = [np.asarray(r["g"]) for r in _run(ncf, imf)]
        elif layer == 2:
            ncf = _get_nc("F2", lambda: build_ffn2())
            sc_ = lambda v: np.ascontiguousarray(v.reshape(-1, 128).T)
            w2f = dict(w1t=np.ascontiguousarray(inp["L2_ffn_w1_w"].T),
                       w2t=np.ascontiguousarray(inp["L2_ffn_w2_w"].T),
                       h1t=np.ascontiguousarray(inp["L2_ffn_h1_w"].T),
                       h2t=np.ascontiguousarray(inp["L2_ffn_h2_w"].T),
                       w3t=np.ascontiguousarray(inp["L2_ffn_w3_w"].T),
                       s1=sc_(inp["L2_ffn_w1_s"]), s2=sc_(inp["L2_ffn_w2_s"]),
                       h1s=sc_(inp["L2_ffn_h1_s"]), h2s=sc_(inp["L2_ffn_h2_s"]),
                       w3s=sc_(inp["L2_ffn_w3_s"]))
            imf = [{"xnf": xnf[c], **w2f, **cN("ident", "ones_row128")}
                   for c in range(NCORES)]
            g = [np.asarray(r["g"]) for r in _run(ncf, imf)]
        else:
            ncf = _get_nc("F3", lambda: build_ffn3())
            gi = np.concatenate([np.asarray(rs[c]["gi"]) for c in range(NCORES)])
            arr = lambda v: np.ascontiguousarray(v.reshape(128, 128)).astype(f32)
            gparts = dict(g1a=arr(gi[:, 0]), g2a=arr(gi[:, 1]),
                          i1a=arr(gi[:, 2]), i2a=arr(gi[:, 3]))
            imf = [{"xnf": xnf[c], "w1": inp["L3_ffn_W1"], "b1": inp["L3_ffn_b1"],
                    "w2": inp["L3_ffn_W2"], "b2": inp["L3_ffn_b2"],
                    **gparts, "gio": np.ascontiguousarray(gi[c * T:(c + 1) * T]),
                    **cN("ones_row512", "ones_row128", "ones_col")}
                   for c in range(NCORES)]
            g = [np.asarray(r["g"]) for r in _run(ncf, imf)]

        if layer != 2:
            nca = _get_nc(f"A{layer}", lambda: build_attn(layer))
            ima = []
            for c in range(NCORES):
                b, g_ = c // 4, c % 4
                m = {"xnb": xnb[b]}
                if layer == 0:
                    m.update(wq=np.ascontiguousarray(inp["L0_attn_Wq"][:, g_ * 256:(g_ + 1) * 256]),
                             wk=np.ascontiguousarray(np.concatenate(
                                 [inp["L0_attn_Wk"][:, g_ * 64:(g_ + 1) * 64]] * 2, 1)),
                             wv=np.ascontiguousarray(inp["L0_attn_Wv"][:, g_ * 64:(g_ + 1) * 64]),
                             bq=np.ascontiguousarray(inp["L0_attn_bq"][g_ * 256:(g_ + 1) * 256].reshape(1, 256)),
                             bk=np.ascontiguousarray(np.concatenate(
                                 [inp["L0_attn_bk"][g_ * 64:(g_ + 1) * 64]] * 2).reshape(1, 128)),
                             bv=np.ascontiguousarray(inp["L0_attn_bv"][g_ * 64:(g_ + 1) * 64].reshape(1, 64)))
                    m.update(cN("ones_row128", "ones_row512", "masks", "selT128",
                                "sel128x2", "pickA", "pickB", "rope_cos2", "rope_sin2", "ropeRT2"))
                elif layer == 1:
                    tr = lambda W: np.ascontiguousarray(np.transpose(W, (2, 0, 1)).reshape(D, -1))
                    m.update(wq=tr(inp["L1_attn_W_Q"][4 * g_:4 * g_ + 4]),
                             wk=tr(inp["L1_attn_W_K"][4 * g_:4 * g_ + 2]),
                             wk2=tr(inp["L1_attn_W_K"][4 * g_ + 2:4 * g_ + 4]),
                             wv=tr(inp["L1_attn_W_V"][4 * g_:4 * g_ + 4]))
                    m.update(cN("ones_row128", "ones_row512", "masks", "selT128",
                                "sel128x2", "pickA", "pickB"))
                else:
                    nk = inp["L3_attn_null_kv"].reshape(1, 64)
                    m.update(wq=np.ascontiguousarray(inp["L3_attn_Wq"][:, g_ * 256:(g_ + 1) * 256]),
                             wk=np.ascontiguousarray(np.concatenate([inp["L3_attn_Wkv"]] * 2, 1)),
                             wv=np.ascontiguousarray(inp["L3_attn_Wkv"]),
                             nullk=np.ascontiguousarray(nk),
                             nullkd=np.ascontiguousarray(np.concatenate([nk.T] * 2, 0)))
                    m.update(cN("ones_row128", "ones_row512", "masks3", "selT128",
                                "sel128x2", "pickA", "pickB", "mask3_last"))
                ima.append(m)
            ra = _run(nca, ima)
            ao = [np.asarray(r["ao"]) for r in ra]
            AO = [np.concatenate([ao[b * 4 + j] for j in range(4)], axis=0) for b in range(2)]
            ncr = _get_nc(f"R{layer}", lambda: build_resid(layer))
            wo = {0: inp["L0_attn_Wo"], 1: np.ascontiguousarray(inp["L1_attn_W_O"].T),
                  3: inp["L3_attn_Wout"]}[layer]
            imr = []
            for c in range(NCORES):
                b = c // 4
                aof = np.ascontiguousarray(
                    AO[b][:, (c % 4) * T:((c % 4) + 1) * T]).reshape(DC, 128, T)
                m = {"xs": x[c * T:(c + 1) * T], "g": g[c], "aof": aof, "wo": wo,
                     **cN("ones_row128")}
                if layer == 0:
                    m["bo"] = np.ascontiguousarray(inp["L0_attn_bo"].reshape(1, D))
                imr.append(m)
            xo = [np.asarray(r["xo"]) for r in _run(ncr, imr)]
        else:
            nca = _get_nc("A2f", lambda: build_attn2())
            scps = [np.asarray(rs[c]["scp"]) for c in range(NCORES)]
            scpg = [np.stack([scps[b * 4 + j] for j in range(4)]) for b in range(2)]
            ima = [{"xnf": xnf[c], "xs": x[c * T:(c + 1) * T], "scp": scpg[c // 4],
                    "wv": np.ascontiguousarray(inp["L2_attn_w_qkv"][:, 1024:1536]),
                    "bv": np.ascontiguousarray(inp["L2_attn_b_qkv"][1024:1536].reshape(1, 512)),
                    "wfc": inp["L2_attn_w_fc"],
                    "bfc": np.ascontiguousarray(inp["L2_attn_b_fc"].reshape(1, D)),
                    "gm": np.ascontiguousarray(inp["L2_attn_gamma"].reshape(1, D)),
                    "bt": np.ascontiguousarray(inp["L2_attn_beta"].reshape(1, D)),
                    **cN("ones_row128", "ones_row512")}
                   for c in range(NCORES)]
            s_ = [np.asarray(r["s"]) for r in _run(nca, ima)]
            ncr = _get_nc("R2", lambda: build_resid(2))
            imr = [{"xs": x[c * T:(c + 1) * T], "g": g[c], "s": s_[c]}
                   for c in range(NCORES)]
            xo = [np.asarray(r["xo"]) for r in _run(ncr, imr)]
        x = np.ascontiguousarray(np.concatenate(xo, axis=0))

    return x.reshape(2, S, D)
